# version 11
# baseline (speedup 1.0000x reference)
import numpy as np

import concourse.bass as bass
import concourse.mybir as mybir
from concourse.aot_env import allow_ici_all_to_all
from concourse.expressions import RuntimeValue
from concourse.bass_utils import run_bass_kernel_spmd

F32 = mybir.dt.float32
BF16 = mybir.dt.bfloat16
AF = mybir.ActivationFunctionType
ALU = mybir.AluOpType

ENGINES = ("PE", "ACT", "DVE", "POOL", "SP")

DM = 2048
DIN = 4096
TOK = 2048
SEQ = 4096
SC = 512
NKT = DM // 128
NEG = -1.0e9
RG = [[0, 1], [2, 3], [4, 5], [6, 7]]


class Op:
    __slots__ = ("stream", "emit", "deps", "kind", "seq", "tag_idx", "needs_tag",
                 "snapshot", "label", "dma_idx")

    def __init__(self, stream, emit, deps, kind, seq, label=""):
        self.stream, self.emit, self.deps, self.kind, self.seq = stream, emit, deps, kind, seq
        self.tag_idx = None
        self.needs_tag = kind in ("dma", "cc")
        self.snapshot = None
        self.label = label


class _Buf:
    __slots__ = ("writer", "readers")

    def __init__(self):
        self.writer = None
        self.readers = []


class Prog:
    def __init__(self):
        self.ops = []
        self.streams = {e: [] for e in ENGINES}
        self.bufs = {}
        self.last_dma = {}
        self.last_cc = None
        self.dma_tmp_counts = {}

    def _buf(self, key):
        b = self.bufs.get(key)
        if b is None:
            b = _Buf()
            self.bufs[key] = b
        return b

    def op(self, stream, emit, reads=(), writes=(), kind="compute", label=""):
        deps = set()
        for k in reads:
            b = self._buf(k)
            if b.writer is not None:
                deps.add(b.writer)
        for k in writes:
            b = self._buf(k)
            if b.writer is not None:
                deps.add(b.writer)
            deps.update(b.readers)
        o = Op(stream, emit, deps, kind, len(self.ops), label)
        deps.discard(o)
        self.ops.append(o)
        self.streams[stream].append(o)
        if kind == "dma":
            o.dma_idx = self.dma_tmp_counts.get(stream, 0)
            self.dma_tmp_counts[stream] = o.dma_idx + 1
            self.last_dma[(stream, o.dma_idx % Prog.NSUB)] = o
        elif kind == "cc":
            self.last_cc = o
        for k in reads:
            self._buf(k).readers.append(o)
        for k in writes:
            b = self._buf(k)
            b.writer = o
            b.readers = []
        return o

    def barrier(self):
        lasts = [self.streams[e][-1] for e in ENGINES if self.streams[e]]
        lasts += list(self.last_dma.values())
        if self.last_cc is not None:
            lasts.append(self.last_cc)
        for e in ENGINES:
            o = self.op(e, lambda eng: eng.nop(), kind="compute", label="barrier")
            o.deps.update(d for d in lasts
                          if d.stream != e or d.kind != "compute")

    NSUB = 8

    def plan(self):
        for o in self.ops:
            for d in o.deps:
                if d.stream != o.stream or d.kind != "compute":
                    d.needs_tag = True
        counters = {}
        dma_counts = {}

        def sem_of(o):
            if o.kind == "dma":
                return ("dma", o.stream, o.dma_idx % Prog.NSUB)
            if o.kind == "cc":
                return ("cc",)
            return ("eng", o.stream)

        for o in self.ops:
            if not o.needs_tag:
                continue
            if o.kind == "dma":
                o.tag_idx = 16 * (o.dma_idx // Prog.NSUB + 1)
                continue
            s = sem_of(o)
            counters[s] = counters.get(s, 0) + 1
            o.tag_idx = counters[s]

        clocks = {e: {} for e in ENGINES}
        plans = {e: [] for e in ENGINES}
        for o in self.ops:
            ck = clocks[o.stream]
            wmap = {}
            if o.kind == "dma" and o.dma_idx >= Prog.NSUB:
                s = ("dma", o.stream, o.dma_idx % Prog.NSUB)
                v = 16 * (o.dma_idx // Prog.NSUB)
                if ck.get(s, 0) < v:
                    wmap[s] = v
                    ck[s] = v
            for d in sorted(o.deps, key=lambda x: x.seq):
                if d.stream == o.stream and d.kind == "compute":
                    continue
                s = sem_of(d)
                if ck.get(s, 0) < d.tag_idx:
                    wmap[s] = max(wmap.get(s, 0), d.tag_idx)
                    ck[s] = d.tag_idx
                    if d.snapshot:
                        for ks, v in d.snapshot.items():
                            if ck.get(ks, 0) < v:
                                ck[ks] = v
            plans[o.stream].append((sorted(wmap.items(), key=lambda kv: str(kv[0])), o))
            if o.needs_tag:
                snap = dict(ck)
                snap[sem_of(o)] = o.tag_idx
                o.snapshot = snap
        return plans


class Ctx:
    pass


def _sb(ctx, name, shape, dtype, at=None):
    per_part = int(np.prod(shape[1:])) * mybir.dt.size(dtype)
    per_part = (per_part + 31) & ~31
    if at is None:
        off = ctx.sb_off
        ctx.sb_off = off + per_part
    else:
        off = at
        ctx.sb_off = max(ctx.sb_off, off + per_part)
    assert ctx.sb_off <= ctx.sb_limit, (name, ctx.sb_off, ctx.sb_limit)
    return ctx.nc.alloc_sbuf_tensor_at(name, shape, dtype, offset=off)


def tap(ctx, name, ap, reads):
    if not ctx.dbg:
        return
    shp = list(ap.shape)
    t = ctx.nc.dram_tensor("tap_" + name, shp, ap.dtype, kind="ExternalOutput")
    dma(ctx, "ACT", t.ap(), ap, reads=reads, writes=(("tap", name),), label="tap")


def mm_op(ctx, psum_ap, lhsT_ap, rhs_ap, reads, pwrite, start, stop, label=""):
    ctx.P.op("PE",
             lambda t, o=psum_ap, l=lhsT_ap, r=rhs_ap, s1=start, s2=stop:
             t.matmul(o, l, r, start=s1, stop=s2),
             reads=reads, writes=(pwrite,), label=label)


def dma(ctx, stream, out_ap, in_ap, reads, writes, label=""):
    ctx.P.op(stream,
             lambda e, o=out_ap, i=in_ap: e.dma_start(out=o, in_=i),
             reads=reads, writes=writes, kind="dma", label=label)


def rep_ap(t, col_off, step, count, rep):
    base = t.ap()
    return bass.AP(base.tensor, col_off, [base.ap[0], [step, count], [0, rep]])



def _strips(W):
    R, K = W.shape
    m, kt = R // 128, K // 128
    w = W.reshape(m, 128, kt, 128)
    return np.ascontiguousarray(np.transpose(w, (0, 3, 2, 1)))


def _pad_rows(W, mult=128):
    r = W.shape[0]
    pr = (r + mult - 1) // mult * mult
    if pr == r:
        return np.ascontiguousarray(W)
    pad = np.zeros((pr - r,) + W.shape[1:], W.dtype)
    return np.concatenate([W, pad], axis=0)


def _pad_vec(v, n=128):
    out = np.zeros(n, np.float32)
    out[:len(v)] = v
    return out


def host_prepare(inputs):
    u = np.asarray(inputs["L0_attn_a_u"], dtype=np.float32)

    def mixer_pack(W_in, conv_w, conv_b, dt_bias, A_log, Dv, norm_w, W_out,
                   H, P, NST, q, ipb=None, opb=None):
        H2 = H // 2
        hsel = slice(q * H2, (q + 1) * H2)
        csel = slice(q * 2048, (q + 1) * 2048)
        z = W_in[0:DIN][csel]
        x = W_in[DIN:2 * DIN][csel]
        B = W_in[2 * DIN:2 * DIN + NST]
        C = W_in[2 * DIN + NST:2 * DIN + 2 * NST]
        dt = W_in[2 * DIN + 2 * NST:][hsel]
        Wrows = np.concatenate([z, x, B, C, _pad_rows(dt)], axis=0)
        wi = _strips(np.ascontiguousarray(Wrows))
        conv_rows = np.concatenate(
            [conv_w[0:DIN][csel], conv_w[DIN:DIN + NST], conv_w[DIN + NST:DIN + 2 * NST]],
            axis=0)
        cb_rows = np.concatenate(
            [conv_b[0:DIN][csel], conv_b[DIN:DIN + NST], conv_b[DIN + NST:DIN + 2 * NST]],
            axis=0)
        nct = conv_rows.shape[0] // 128
        cv = np.zeros((nct, 128, 4, 128), np.float32)
        for ct in range(nct):
            for w in range(4):
                np.fill_diagonal(cv[ct, :, w, :], conv_rows[ct * 128:(ct + 1) * 128, w])
        d = dict(
            wi=wi.astype(np.float32),
            cv=cv,
            cb=np.ascontiguousarray(cb_rows.reshape(nct, 128).T).astype(np.float32),
            dtb=_pad_vec(dt_bias[hsel]).reshape(128, 1),
            alog=_pad_vec(A_log[hsel]).reshape(128, 1),
            dex=np.ascontiguousarray(
                np.repeat(Dv[hsel], P).reshape(16, 128).T).astype(np.float32),
            nw=np.ascontiguousarray(norm_w.reshape(32, 128).T).astype(np.float32),
            wo=_strips(np.ascontiguousarray(W_out)).astype(np.float32),
        )
        if ipb is not None:
            brows = np.concatenate(
                [ipb[0:DIN][csel], ipb[DIN:2 * DIN][csel],
                 ipb[2 * DIN:2 * DIN + NST], ipb[2 * DIN + NST:2 * DIN + 2 * NST],
                 _pad_vec(ipb[2 * DIN + 2 * NST:][hsel])], axis=0)
            nmt = wi.shape[0]
            d["ipb"] = np.ascontiguousarray(brows.reshape(nmt, 128).T).astype(np.float32)
        if opb is not None:
            d["opb"] = np.ascontiguousarray(opb.reshape(16, 128).T).astype(np.float32)
        return d

    I = 8192
    perm = np.arange(I).reshape(I // 4, 4).T.reshape(-1)
    wg = _strips(np.ascontiguousarray(np.asarray(inputs["L0_ffn_w_gate"], np.float32).T[perm]))
    wu = _strips(np.ascontiguousarray(np.asarray(inputs["L0_ffn_w_up"], np.float32).T[perm]))
    wd = _strips(np.ascontiguousarray(np.asarray(inputs["L0_ffn_w_down"], np.float32)[perm].T))

    ident = np.eye(128, dtype=np.float32)
    ones1f = np.ones((1, 128), np.float32)
    onescol = np.ones((128, 1), np.float32)
    e127 = np.zeros((128, 1), np.float32)
    e127[127, 0] = 1.0
    cconst = np.zeros((128, 4), np.float32)
    cconst[:, 0] = 1e-5
    cconst[:, 1] = -1.0
    maskT = np.zeros((128, 256), np.float32)
    maskT[:, 0:128] = np.where(
        np.arange(128)[:, None] <= np.arange(128)[None, :], 0.0, NEG)

    in_maps = []
    for c in range(8):
        b, q = c // 2, c % 2
        m = {}
        m["u"] = np.ascontiguousarray(u[b, q * TOK:(q + 1) * TOK, :].T)
        da = mixer_pack(np.asarray(inputs["L0_attn_a_W_in"], np.float32),
                        np.asarray(inputs["L0_attn_a_conv_w"], np.float32),
                        np.asarray(inputs["L0_attn_a_conv_b"], np.float32),
                        np.asarray(inputs["L0_attn_a_dt_bias"], np.float32),
                        np.asarray(inputs["L0_attn_a_A_log"], np.float32),
                        np.asarray(inputs["L0_attn_a_D"], np.float32),
                        np.asarray(inputs["L0_attn_a_norm_w"], np.float32),
                        np.asarray(inputs["L0_attn_a_W_out"], np.float32),
                        64, 64, 64, q)
        for k, v in da.items():
            m["a_" + k] = v
        db = mixer_pack(np.asarray(inputs["L0_attn_b_in_proj_w"], np.float32),
                        np.asarray(inputs["L0_attn_b_conv_w"], np.float32)[:, 0, :],
                        np.asarray(inputs["L0_attn_b_conv_b"], np.float32),
                        np.asarray(inputs["L0_attn_b_dt_bias"], np.float32),
                        np.asarray(inputs["L0_attn_b_A_log"], np.float32),
                        np.asarray(inputs["L0_attn_b_D"], np.float32),
                        np.asarray(inputs["L0_attn_b_norm_w"], np.float32),
                        np.asarray(inputs["L0_attn_b_out_proj_w"], np.float32),
                        32, 128, 128, q,
                        ipb=np.asarray(inputs["L0_attn_b_in_proj_b"], np.float32),
                        opb=np.asarray(inputs["L0_attn_b_out_proj_b"], np.float32))
        for k, v in db.items():
            m["b_" + k] = v
        m["wg"], m["wu"], m["wd"] = wg, wu, wd
        m["ident"], m["ones1f"], m["onescol"] = ident, ones1f, onescol
        m["e127"], m["maskT"] = e127, maskT
        m["cconst"] = cconst
        in_maps.append(m)
    return in_maps



class MixCfg:
    def __init__(self, pre, H, P, NST, has_bias):
        self.pre, self.H, self.P, self.NST = pre, H, P, NST
        self.H2 = H // 2
        self.NCT = (2048 + 2 * NST) // 128
        self.NMT = 16 + self.NCT + 1
        self.has_bias = has_bias


CFG_A = MixCfg("a_", 64, 64, 64, False)
CFG_B = MixCfg("b_", 32, 128, 128, True)


def build_program(phases=("mixa", "ffn1", "mixb", "ffn2"), n_mix_sc=8, n_tok_sc=4,
                  dbg=False):
    nc = bass.Bass("TRN2", target_bir_lowering=False, debug=False)
    ctx = Ctx()
    ctx.nc, ctx.P = nc, Prog()
    ctx.phases, ctx.n_mix_sc, ctx.n_tok_sc = phases, n_mix_sc, n_tok_sc
    ctx.dbg = dbg
    ctx.tapn = 0

    D = {}
    ctx.D = D
    D["u"] = nc.dram_tensor("u", [DM, TOK], F32, kind="ExternalInput")
    for pre, cfg in (("a_", CFG_A), ("b_", CFG_B)):
        if ("mixa" if pre == "a_" else "mixb") not in phases:
            continue
        D[pre + "wi"] = nc.dram_tensor(pre + "wi", [cfg.NMT, 128, NKT, 128], F32, kind="ExternalInput")
        D[pre + "cv"] = nc.dram_tensor(pre + "cv", [cfg.NCT, 128, 4, 128], F32, kind="ExternalInput")
        D[pre + "cb"] = nc.dram_tensor(pre + "cb", [128, cfg.NCT], F32, kind="ExternalInput")
        D[pre + "dtb"] = nc.dram_tensor(pre + "dtb", [128, 1], F32, kind="ExternalInput")
        D[pre + "alog"] = nc.dram_tensor(pre + "alog", [128, 1], F32, kind="ExternalInput")
        D[pre + "dex"] = nc.dram_tensor(pre + "dex", [128, 16], F32, kind="ExternalInput")
        D[pre + "nw"] = nc.dram_tensor(pre + "nw", [128, 32], F32, kind="ExternalInput")
        D[pre + "wo"] = nc.dram_tensor(pre + "wo", [16, 128, 32, 128], F32, kind="ExternalInput")
    if "mixb" in phases:
        D["b_ipb"] = nc.dram_tensor("b_ipb", [128, CFG_B.NMT], F32, kind="ExternalInput")
        D["b_opb"] = nc.dram_tensor("b_opb", [128, 16], F32, kind="ExternalInput")
    if "ffn1" in phases or "ffn2" in phases:
        D["wg"] = nc.dram_tensor("wg", [64, 128, NKT, 128], F32, kind="ExternalInput")
        D["wu"] = nc.dram_tensor("wu", [64, 128, NKT, 128], F32, kind="ExternalInput")
        D["wd"] = nc.dram_tensor("wd", [16, 128, 64, 128], F32, kind="ExternalInput")
    for nm, shp in (("ident", [128, 128]), ("ones1f", [1, 128]), ("onescol", [128, 1]),
                    ("e127", [128, 1]), ("maskT", [128, 256]), ("cconst", [128, 4])):
        D[nm] = nc.dram_tensor(nm, shp, F32, kind="ExternalInput")
    D["yout"] = nc.dram_tensor("yout", [DM, TOK], F32, kind="ExternalOutput")

    S = {}
    ctx.S = S
    S["u"] = nc.dram_tensor("s_u", [DM, TOK], BF16)
    for pre, cfg in (("a_", CFG_A), ("b_", CFG_B)):
        if ("mixa" if pre == "a_" else "mixb") not in phases:
            continue
        S[pre + "wi"] = nc.dram_tensor("s_" + pre + "wi", [cfg.NMT, 128, NKT, 128], BF16)
        S[pre + "cv"] = nc.dram_tensor("s_" + pre + "cv", [cfg.NCT, 128, 4, 128], BF16)
        S[pre + "wo"] = nc.dram_tensor("s_" + pre + "wo", [16, 128, 32, 128], BF16)
    if "ffn1" in phases or "ffn2" in phases:
        S["wg"] = nc.dram_tensor("s_wg", [64, 128, NKT, 128], BF16)
        S["wu"] = nc.dram_tensor("s_wu", [64, 128, NKT, 128], BF16)
        S["wd"] = nc.dram_tensor("s_wd", [16, 128, 64, 128], BF16)
    if "mixa" in phases or "mixb" in phases:
        for j in range(4):
            S[f"xcn_in{j}"] = nc.dram_tensor(f"s_xcn_in{j}", [DM, SC], BF16)
            S[f"xcn_g{j}"] = nc.dram_tensor(f"s_xcn_g{j}", [2 * DM, SC], BF16)
            S[f"y_in{j}"] = nc.dram_tensor(f"s_y_in{j}", [2048, 1024], BF16)
        S["ygt"] = nc.dram_tensor("s_ygt", [4, 2 * 2048, 1024], BF16)
        S["st_in"] = nc.dram_tensor("s_st_in", [1, SEQ], F32)
        S["st_g"] = nc.dram_tensor("s_st_g", [2, SEQ], F32)

    ctx.sb_limit = nc.SBUF_PARTITION_SIZE_BYTES - 16 * 1024 - 640
    ctx.sb_off = 16 * 1024 + 640
    T = {}
    ctx.T = T
    T["R"] = _sb(ctx, "sb_R", [128, NKT, TOK], BF16)
    T["identf"] = _sb(ctx, "sb_identf", [128, 128], F32)
    T["identb"] = _sb(ctx, "sb_identb", [128, 128], BF16)
    T["ones1f"] = _sb(ctx, "sb_ones1f", [1, 128], F32)
    T["ones1b"] = _sb(ctx, "sb_ones1b", [1, 128], BF16)
    T["onescf"] = _sb(ctx, "sb_onescf", [128, 1], F32)
    T["onescb"] = _sb(ctx, "sb_onescb", [128, 1], BF16)
    T["cconst"] = _sb(ctx, "sb_cconst", [128, 4], F32)
    T["e127"] = _sb(ctx, "sb_e127", [128, 1], F32)
    T["maskT"] = _sb(ctx, "sb_maskT", [128, 256], F32)
    T["pcol"] = _sb(ctx, "sb_pcol", [128, 4], F32)
    T["dex"] = _sb(ctx, "sb_dex", [128, 16], F32)
    T["nw"] = _sb(ctx, "sb_nw", [128, 32], F32)
    T["cbias"] = _sb(ctx, "sb_cbias", [128, 20], F32)
    T["bias_m"] = _sb(ctx, "sb_bias_m", [128, 56], F32)
    ctx.sb_phase_base = ctx.sb_off

    PS = {}
    ctx.PS = PS
    for i in range(8):
        nc.psum_base = i
        PS[i] = nc.alloc_psum_tensor(f"pb{i}", [128, 512], F32)
    nc.psum_base = 3
    PS["t3"] = nc.alloc_psum_tensor("pbt3", [128, 512], BF16)
    nc.psum_base = 4
    PS["t4"] = nc.alloc_psum_tensor("pbt4", [128, 512], BF16)
    nc.psum_base = 8

    emit_init(ctx)
    for ph in phases:
        if ph == "mixa":
            emit_mixer(ctx, CFG_A)
        elif ph == "mixb":
            emit_mixer(ctx, CFG_B)
        elif ph in ("ffn1", "ffn2"):
            emit_ffn(ctx, ph)
        ctx.P.barrier()
    emit_out(ctx)
    ctx.P.barrier()

    plans = ctx.P.plan()

    import contextlib
    with contextlib.ExitStack() as _stk:
        s_pe = _stk.enter_context(nc.semaphore("sem_PE"))
        s_act = _stk.enter_context(nc.semaphore("sem_ACT"))
        s_dve = _stk.enter_context(nc.semaphore("sem_DVE"))
        s_pool = _stk.enter_context(nc.semaphore("sem_POOL"))
        s_sp = _stk.enter_context(nc.semaphore("sem_SP"))
        s_cc = _stk.enter_context(nc.semaphore("cc_sem"))
        dsems = {}
        for q in ("SP", "ACT", "POOL"):
            for i in range(Prog.NSUB):
                dsems[(q, i)] = _stk.enter_context(nc.semaphore(f"dma_{q}{i}"))
        block = _stk.enter_context(nc.Block())
        sems = {"PE": s_pe, "ACT": s_act, "DVE": s_dve, "POOL": s_pool, "SP": s_sp}

        def resolve(s):
            if s[0] == "dma":
                return dsems[(s[1], s[2])]
            if s[0] == "cc":
                return s_cc
            return sems[s[1]]

        def run(stream, eng):
            for waits, o in plans[stream]:
                for s, v in waits:
                    eng.wait_ge(resolve(s), v)
                inst = o.emit(eng)
                if o.needs_tag:
                    if o.kind == "dma":
                        inst.then_inc(dsems[(o.stream, o.dma_idx % Prog.NSUB)], 16)
                    elif o.kind == "cc":
                        inst.then_inc(s_cc, 1)
                    else:
                        inst.then_inc(sems[o.stream], 1)

        @block.tensor
        def _(t):
            run("PE", t)

        @block.scalar
        def _(a):
            run("ACT", a)

        @block.vector
        def _(v):
            run("DVE", v)

        @block.gpsimd
        def _(g):
            run("POOL", g)

        @block.sync
        def _(s):
            run("SP", s)

    return nc, ctx



def emit_init(ctx):
    P, D, S, T = ctx.P, ctx.D, ctx.S, ctx.T
    order = ["u"]
    for ph in ctx.phases:
        if ph == "mixa":
            order += ["a_wi", "a_cv", "a_wo"]
        elif ph == "mixb":
            order += ["b_wi", "b_cv", "b_wo"]
        elif ph == "ffn1" or (ph == "ffn2" and "ffn1" not in ctx.phases):
            order += ["wg", "wu", "wd"]
    for nm in order:
        dma(ctx, "POOL", S[nm].ap(), D[nm].ap(), reads=(),
            writes=(("dram", nm),), label=f"cast_{nm}")
    for nm, dn in (("identf", "ident"), ("ones1f", "ones1f"), ("onescf", "onescol"),
                   ("e127", "e127"), ("maskT", "maskT"), ("cconst", "cconst")):
        dma(ctx, "SP", T[nm].ap(), D[dn].ap(), (), ((nm,),), label="ldc")
    P.op("DVE", lambda v, o=T["identb"].ap(), i=T["identf"].ap(): v.tensor_copy(o, i),
         reads=(("identf",),), writes=(("identb",),))
    P.op("DVE", lambda v, o=T["ones1b"].ap(), i=T["ones1f"].ap(): v.tensor_copy(o, i),
         reads=(("ones1f",),), writes=(("ones1b",),))
    P.op("DVE", lambda v, o=T["onescb"].ap(), i=T["onescf"].ap(): v.tensor_copy(o, i),
         reads=(("onescf",),), writes=(("onescb",),))
    dma(ctx, "SP", T["R"].ap(), S["u"].ap().rearrange("(kt p) t -> p kt t", p=128),
        reads=(("dram", "u"),),
        writes=tuple(("R", kt, gg) for kt in range(NKT) for gg in range(4)), label="loadR")
    P.barrier()


def emit_out(ctx):
    P, D, T = ctx.P, ctx.D, ctx.T
    dma(ctx, "POOL", D["yout"].ap().rearrange("(kt p) t -> p kt t", p=128), T["R"].ap(),
        reads=tuple(("R", kt, gg) for kt in range(NKT) for gg in range(4)),
        writes=(("dram", "yout"),), label="store_out")



def emit_ln(ctx, cols, xcn_ap, xcn_key, psA, psB, tag, ln_t):
    P, T, PS = ctx.P, ctx.T, ctx.PS
    R = ctx.T["R"]
    gR = cols.start // SC
    sq, st, mb, rb = ln_t
    s1p = PS[psA].ap()[0:1, 0:512]
    s2p = PS[psB].ap()[0:1, 0:512]
    for kt in range(NKT):
        mm_op(ctx, s1p, T["onescb"].ap(), R.ap()[:, kt, cols],
              reads=(("onescb",), ("R", kt, gR)), pwrite=("ps", psA),
              start=(kt == 0), stop=(kt == NKT - 1), label="lns1")
    for kt in range(NKT):
        P.op("ACT", lambda a, o=sq.ap()[:, 0, :], i=R.ap()[:, kt, cols]:
             a.activation(o, i, AF.Square),
             reads=(("R", kt, gR),), writes=((tag, "lnsq", 0),), label="lnsq")
        mm_op(ctx, s2p, T["onescb"].ap(), sq.ap()[:, 0, :],
              reads=(("onescb",), (tag, "lnsq", 0)), pwrite=("ps", psB),
              start=(kt == 0), stop=(kt == NKT - 1), label="lns2")
    m_ = st.ap()[0:1, 0:512]
    t_ = st.ap()[0:1, 512:1024]
    v_ = st.ap()[0:1, 1024:1536]
    P.op("DVE", lambda v, o=m_, i=s1p: v.tensor_scalar_mul(o, i, 1.0 / DM),
         reads=(("ps", psA),), writes=((tag, "m"),))
    P.op("DVE", lambda v, o=t_, i=s2p: v.tensor_scalar_mul(o, i, 1.0 / DM),
         reads=(("ps", psB),), writes=((tag, "t"),))
    P.op("DVE", lambda v, o=v_, a=m_: v.scalar_tensor_tensor(o, a, -1.0, a, ALU.mult, ALU.mult),
         reads=((tag, "m"),), writes=((tag, "v"),))
    P.op("DVE", lambda v, o=v_, a=v_, b=t_: v.tensor_tensor(o, a, b, ALU.add),
         reads=((tag, "v"), (tag, "t")), writes=((tag, "v"),))
    P.op("ACT", lambda a, o=v_, i=v_, e=T["cconst"].ap()[0:1, 0:1]:
         a.activation(o, i, AF.Sqrt, bias=e),
         reads=((tag, "v"), ("cconst",)), writes=((tag, "v"),))
    P.op("DVE", lambda v, o=v_, i=v_: v.reciprocal(o, i),
         reads=((tag, "v"),), writes=((tag, "v"),))
    mm_op(ctx, PS[psA].ap()[:, 0:512], T["ones1f"].ap(), m_,
          reads=(("ones1f",), (tag, "m")), pwrite=("ps", psA), start=True, stop=True)
    mm_op(ctx, PS[psB].ap()[:, 0:512], T["ones1f"].ap(), v_,
          reads=(("ones1f",), (tag, "v")), pwrite=("ps", psB), start=True, stop=True)
    P.op("ACT", lambda a, o=mb.ap(), i=PS[psA].ap()[:, 0:512]: a.copy(o, i),
         reads=(("ps", psA),), writes=((tag, "mb"),))
    P.op("ACT", lambda a, o=rb.ap(), i=PS[psB].ap()[:, 0:512]: a.copy(o, i),
         reads=(("ps", psB),), writes=((tag, "rb"),))
    for kt in range(NKT):
        P.op("POOL", lambda v, o=xcn_ap[:, kt, :], a=R.ap()[:, kt, cols], b=mb.ap():
             v.tensor_tensor(o, a, b, ALU.subtract),
             reads=(("R", kt, gR), (tag, "mb")), writes=((xcn_key, kt),))
        P.op("POOL", lambda v, o=xcn_ap[:, kt, :], a=xcn_ap[:, kt, :], b=rb.ap():
             v.tensor_tensor(o, a, b, ALU.mult),
             reads=((xcn_key, kt), (tag, "rb")), writes=((xcn_key, kt),))


def _ln_alloc(ctx, tag):
    sq = _sb(ctx, f"lnsq_{tag}", [128, 1, 512], BF16)
    ctx.lnst_off = ctx.sb_off
    st = _sb(ctx, f"lnst_{tag}", [1, 3 * 512], F32)
    mb = _sb(ctx, f"lnmb_{tag}", [128, 512], BF16)
    rb = _sb(ctx, f"lnrb_{tag}", [128, 512], BF16)
    return sq, st, mb, rb



def emit_ffn(ctx, tag):
    P, S, T, PS = ctx.P, ctx.S, ctx.T, ctx.PS
    ctx.sb_off = ctx.sb_phase_base
    ln_t = _ln_alloc(ctx, tag)
    xcn = _sb(ctx, f"xcn_{tag}", [128, NKT, 512], BF16)
    h = _sb(ctx, f"h_{tag}", [128, 64, 512], BF16)
    gs = _sb(ctx, f"gs_{tag}", [128, 1, 512], BF16)
    wgb = _sb(ctx, f"wgb_{tag}", [128, 2, NKT, 128], BF16)
    wub = _sb(ctx, f"wub_{tag}", [128, 2, NKT, 128], BF16)
    spbase = ctx.sb_off
    aq = _sb(ctx, f"aq_{tag}", [128, 4, 512], BF16)
    bq = [_sb(ctx, f"b{i}_{tag}", [128, 512], BF16) for i in range(4)]
    gq = _sb(ctx, f"gq_{tag}", [128, 512], BF16)
    sp_end = ctx.sb_off
    wdb = _sb(ctx, f"wdb_{tag}", [128, 2, 32, 128], BF16, at=spbase)
    ctx.sb_off = max(ctx.sb_off, sp_end)
    SPK = tuple((tag, "sp", i) for i in range(6))

    emit_ln(ctx, slice(0, SC), xcn.ap(), (tag, "xcn"), 6, 7, tag, ln_t)

    def sparsify(g0):
            hk = lambda j: ((tag, "h", j * 16 + g0),)
            hk = lambda j: ((tag, "h", j * 16 + g0),)
            for j in range(4):
                P.op("ACT", lambda a, o=aq.ap()[:, j, :],
                     i=h.ap()[:, j * 16 + g0, :]:
                     a.activation(o, i, AF.Abs),
                     reads=hk(j), writes=(SPK[j],), label="spabs")
            aj = [aq.ap()[:, j, :] for j in range(4)]
            bb = [t.ap() for t in bq]
            gg = gq.ap()
            AQ = tuple(SPK[j] for j in range(4))
            BK = lambda i: (tag, "b", i)
            GK = ((tag, "gq"),)

            def tt(dst, dk, x, y, op, reads, label="sp"):
                P.op("DVE", lambda v, o=dst, a_=x, c=y, op_=op: v.tensor_tensor(o, a_, c, op_),
                     reads=reads, writes=(dk,), label=label)

            def cpy(dst, dk, src, reads):
                P.op("DVE", lambda v, o=dst, i=src: v.tensor_copy(o, i),
                     reads=reads, writes=(dk,), label="spc")

            tt(bb[0], BK(0), aj[0], aj[1], ALU.is_ge, AQ)
            cpy(bb[1], BK(1), bb[0], (BK(0),))
            tt(gg, GK[0], aj[0], aj[2], ALU.is_ge, AQ)
            tt(bb[0], BK(0), bb[0], gg, ALU.add, (BK(0), GK[0]))
            cpy(bb[2], BK(2), gg, (GK[0],))
            tt(gg, GK[0], aj[0], aj[3], ALU.is_ge, AQ)
            tt(bb[0], BK(0), bb[0], gg, ALU.add, (BK(0), GK[0]))
            cpy(bb[3], BK(3), gg, (GK[0],))
            tt(gg, GK[0], aj[1], aj[2], ALU.is_ge, AQ)
            tt(bb[1], BK(1), bb[1], gg, ALU.subtract, (BK(1), GK[0]))
            tt(bb[2], BK(2), bb[2], gg, ALU.add, (BK(2), GK[0]))
            tt(gg, GK[0], aj[1], aj[3], ALU.is_ge, AQ)
            tt(bb[1], BK(1), bb[1], gg, ALU.subtract, (BK(1), GK[0]))
            tt(bb[3], BK(3), bb[3], gg, ALU.add, (BK(3), GK[0]))
            tt(gg, GK[0], aj[2], aj[3], ALU.is_ge, AQ)
            tt(bb[2], BK(2), bb[2], gg, ALU.subtract, (BK(2), GK[0]))
            tt(bb[3], BK(3), bb[3], gg, ALU.add, (BK(3), GK[0]))
            thr = [(ALU.is_ge, 1.5), (ALU.is_le, -0.5), (ALU.is_le, 0.5), (ALU.is_le, 1.5)]
            for i in range(4):
                op_, cc_ = thr[i]
                P.op("DVE", lambda v, o=bb[i], i_=bb[i], c=cc_, op2=op_:
                     v.tensor_single_scalar(o, i_, c, op2),
                     reads=(BK(i),), writes=(BK(i),), label="spthr")
            for j in range(4):
                P.op("DVE", lambda v, o=h.ap()[:, j * 16 + g0, :],
                     a_=h.ap()[:, j * 16 + g0, :], c=bb[j]:
                     v.tensor_tensor(o, a_, c, ALU.mult),
                     reads=hk(j) + (BK(j),), writes=hk(j), label="spapply")

    for g in range(ctx.n_tok_sc):
        cols = slice(g * SC, (g + 1) * SC)
        for mi in range(64):
            gt, j = mi // 4, mi % 4
            m = j * 16 + gt
            w = mi % 2
            dma(ctx, "SP", wgb.ap()[:, w, :, :], S["wg"].ap()[m],
                reads=(("dram", "wg"),), writes=((tag, "wgb", w),), label="ldwg")
            dma(ctx, "SP", wub.ap()[:, w, :, :], S["wu"].ap()[m],
                reads=(("dram", "wu"),), writes=((tag, "wub", w),), label="ldwu")
            pg = PS[w].ap()[:, 0:512]
            pu = PS[2 + w].ap()[:, 0:512]
            for k in range(NKT):
                mm_op(ctx, pg, wgb.ap()[:, w, k, :], xcn.ap()[:, k, :],
                      reads=((tag, "wgb", w), ((tag, "xcn"), k)), pwrite=("ps", w),
                      start=(k == 0), stop=(k == NKT - 1), label="mmg")
            for k in range(NKT):
                mm_op(ctx, pu, wub.ap()[:, w, k, :], xcn.ap()[:, k, :],
                      reads=((tag, "wub", w), ((tag, "xcn"), k)), pwrite=("ps", 2 + w),
                      start=(k == 0), stop=(k == NKT - 1), label="mmu")
            P.op("ACT", lambda a, o=gs.ap()[:, 0, :], i=pg: a.activation(o, i, AF.Silu),
                 reads=(("ps", w),), writes=((tag, "gs", 0),), label="silug")
            P.op("DVE", lambda v, o=h.ap()[:, m, :], a=gs.ap()[:, 0, :], b=pu:
                 v.tensor_tensor(o, a, b, ALU.mult),
                 reads=((tag, "gs", 0), ("ps", 2 + w)), writes=((tag, "h", m),), label="hmul")
            if j == 3:
                sparsify(gt)
        if g + 1 < ctx.n_tok_sc:
            emit_ln(ctx, slice((g + 1) * SC, (g + 2) * SC), xcn.ap(), (tag, "xcn"),
                    6, 7, tag, ln_t)
        allsp = tuple(SPK) + tuple((tag, "b", i) for i in range(4)) + ((tag, "gq"),)
        for m in range(16):
            pd = PS[4 + (m % 2)].ap()[:, 0:512]
            for half in range(2):
                w = (2 * m + half) % 2
                dma(ctx, "SP", wdb.ap()[:, w, :, :], S["wd"].ap()[m][:, half * 32:(half + 1) * 32, :],
                    reads=(("dram", "wd"),) + (allsp if m == 0 else ()),
                    writes=((tag, "wdb", w),) + (allsp if m == 0 else ()),
                    label="ldwd")
                for kk in range(32):
                    k = half * 32 + kk
                    mm_op(ctx, pd, wdb.ap()[:, w, kk, :], h.ap()[:, k, :],
                          reads=((tag, "wdb", w), (tag, "h", k)), pwrite=("ps", 4 + (m % 2)),
                          start=(k == 0), stop=(k == 63), label="mmd")
            P.op("DVE", lambda v, o=T["R"].ap()[:, m, cols], a_=pd,
                 b_=T["R"].ap()[:, m, cols]: v.tensor_tensor(o, a_, b_, ALU.add),
                 reads=(("ps", 4 + (m % 2)), ("R", m, g)), writes=(("R", m, g),), label="resid")



def emit_mixer(ctx, cfg):
    P, D, S, T, PS, nc = ctx.P, ctx.D, ctx.S, ctx.T, ctx.PS, ctx.nc
    pre = cfg.pre
    tag = "mx" + pre[0]
    H2, PP, NST, NCT, NMT = cfg.H2, cfg.P, cfg.NST, cfg.NCT, cfg.NMT
    HPD = 2048
    ctx.sb_off = ctx.sb_phase_base
    ln_t = _ln_alloc(ctx, tag)

    pcol = T["pcol"]
    dma(ctx, "SP", pcol.ap()[:, 0:1], D[pre + "dtb"].ap(), (), ((tag, "dtb"),))
    dma(ctx, "SP", pcol.ap()[:, 1:2], D[pre + "alog"].ap(), (), ((tag, "alog"),))
    P.op("ACT", lambda a, o=pcol.ap()[:, 2:3], i=pcol.ap()[:, 1:2]:
         a.activation(o, i, AF.Exp), reads=((tag, "alog"),), writes=((tag, "A"),))
    P.op("DVE", lambda v, o=pcol.ap()[:, 2:3], i=pcol.ap()[:, 2:3]:
         v.tensor_scalar_mul(o, i, -1.0), reads=((tag, "A"),), writes=((tag, "A"),))
    dma(ctx, "SP", T["dex"].ap(), D[pre + "dex"].ap(), (), ((tag, "dex"),))
    dma(ctx, "SP", T["nw"].ap(), D[pre + "nw"].ap(), (), ((tag, "nw"),))
    dma(ctx, "SP", T["cbias"].ap()[:, 0:NCT],
        D[pre + "cb"].ap(), (), ((tag, "cb"),))
    if cfg.has_bias:
        dma(ctx, "SP", T["bias_m"].ap()[:, 0:NMT],
            D["b_ipb"].ap(), (), ((tag, "ipb"),))
        dma(ctx, "SP", T["bias_m"].ap()[:, 36:52],
            D["b_opb"].ap(), (), ((tag, "opb"),))

    xg_base = ctx.sb_off
    xcn = _sb(ctx, f"xcn_{tag}", [128, NKT, 512], BF16)
    for g in range(ctx.n_tok_sc):
        cols = slice(g * SC, (g + 1) * SC)
        emit_ln(ctx, cols, xcn.ap(), (tag, "xcn"), 5, 6, tag, ln_t)
        if ctx.dbg and g == 0:
            tap(ctx, "R0" + tag, T["R"].ap()[:, :, 0:512], tuple(("R", kt, 0) for kt in range(NKT)))
            tap(ctx, "xcn1" + tag, xcn.ap(), tuple(((tag, "xcn"), kt) for kt in range(NKT)))
        dma(ctx, "SP",
            S[f"xcn_in{g}"].ap().rearrange("(kt p) t -> p kt t", p=128),
            xcn.ap(), reads=tuple(((tag, "xcn"), kt) for kt in range(NKT)),
            writes=(("dram", "xcn_in", g),), label="st_xcn")
        P.op("POOL", lambda g_, j=g: g_.collective_compute(
            "AllGather", ALU.bypass, replica_groups=RG,
            ins=[S[f"xcn_in{j}"].ap()], outs=[S[f"xcn_g{j}"].ap()]),
            reads=(("dram", "xcn_in", g),), writes=(("dram", "xcn_g", g),), kind="cc",
            label="ag_xcn")

    ctx.sb_off = xg_base
    xg = _sb(ctx, f"xg_{tag}", [128, NKT, 512], BF16)
    wib = _sb(ctx, f"wib_{tag}", [128, 2, NKT, 128], BF16)
    cvb = _sb(ctx, f"cvb_{tag}", [128, 2, 4, 128], BF16)
    sz = _sb(ctx, f"sz_{tag}", [128, 16, 512], BF16)
    xbcp = _sb(ctx, f"xbcp_{tag}", [128, NCT, 516], BF16)
    xbc = _sb(ctx, f"xbc_{tag}", [128, NCT, 256], BF16)
    xdt = _sb(ctx, f"xdt_{tag}", [128, 2, HPD], BF16)
    ysb = _sb(ctx, f"ysb_{tag}", [128, 16, 256], BF16)
    Sst = _sb(ctx, f"Sst_{tag}", [128, HPD], F32)
    Sbf = _sb(ctx, f"Sbf_{tag}", [128, HPD], BF16)
    dtfm = _sb(ctx, f"dtfm_{tag}", [128, 512], F32)
    abfm = _sb(ctx, f"abfm_{tag}", [128, 512], F32)
    acfm = _sb(ctx, f"acfm_{tag}", [128, 256], F32)
    dsfm = _sb(ctx, f"dsfm_{tag}", [128, 256], F32)
    dttm = _sb(ctx, f"dttm_{tag}", [128, 64], F32)
    natm = _sb(ctx, f"natm_{tag}", [128, 64], F32)
    dstm = _sb(ctx, f"dstm_{tag}", [128, 64], F32)
    erow = _sb(ctx, f"erow_{tag}", [1, 32], F32)
    ebc = _sb(ctx, f"ebc_{tag}", [128, 32], F32)
    gsb = _sb(ctx, f"gsb_{tag}", [128, 384], BF16)
    btok = _sb(ctx, f"btok_{tag}", [128, 2, 128], BF16)
    seg = _sb(ctx, f"seg_{tag}", [128, 2, 384], F32)
    Lb = _sb(ctx, f"Lb_{tag}", [128, 2, 384], BF16)
    glb = _sb(ctx, f"glb_{tag}", [128, 2, 384], BF16)
    cea = _sb(ctx, f"cea_{tag}", [128, 2, 256], BF16)
    expb = _sb(ctx, f"expb_{tag}", [128, 2, 256], BF16)
    cfm0 = _sb(ctx, f"cfm0_{tag}", [128, 256], BF16)
    bacb = ctx.nc.alloc_sbuf_tensor_at(f"bacb_{tag}", [128, 2, 256], F32, offset=ctx.lnst_off)
    strow = ctx.nc.alloc_sbuf_tensor_at(f"strow_{tag}", [1, 512], F32,
                                         offset=ctx.lnst_off + 4096)
    mix_end = ctx.sb_off

    P.op("DVE", lambda v, o=Sst.ap(): v.memzero(o), reads=(), writes=((tag, "S"),))
    P.op("DVE", lambda v, o=Sbf.ap(): v.memzero(o), reads=(), writes=((tag, "Sb"),))
    for ct in range(NCT):
        P.op("DVE", lambda v, o=xbcp.ap()[:, ct, 0:4]: v.memzero(o),
             reads=(), writes=((tag, "xbcp", ct),))

    ap0 = dttm.ap().ap[0]

    def drep(t, col, per_head, nh):
        return bass.AP(t.ap().tensor, col, [ap0, [1, nh], [0, per_head]])

    def ld_xg(g):
        rank, j = g // 4, g % 4
        dma(ctx, "SP", xg.ap(),
            S[f"xcn_g{j}"].ap().rearrange("(r kt p) t -> r p kt t", r=2, p=128)[rank],
            reads=(("dram", "xcn_g", j),), writes=((tag, "xg"),), label="ld_xg")
    ld_xg(0)
    for g in range(ctx.n_mix_sc):
        if g > 0:
            for ct in range(NCT):
                P.op("DVE", lambda v, o=xbcp.ap()[:, ct, 0:3], i=xbcp.ap()[:, ct, 512:515]:
                     v.tensor_copy(o, i),
                     reads=((tag, "xbcp", ct),), writes=((tag, "xbcp", ct),), label="carry")
        for mt in range(NMT):
            w = mt % 2
            dma(ctx, "SP", wib.ap()[:, w, :, :], S[pre + "wi"].ap()[mt],
                reads=(("dram", pre + "wi"),), writes=((tag, "wib", w),), label="ld_wi")
            ps = PS[w].ap()[:, 0:512]
            for k in range(NKT):
                mm_op(ctx, ps, wib.ap()[:, w, k, :], xg.ap()[:, k, :],
                      reads=((tag, "wib", w), (tag, "xg")), pwrite=("ps", w),
                      start=(k == 0), stop=(k == NKT - 1), label="mmip")
            brd = ((tag, "ipb"),) if cfg.has_bias else ()
            bap = T["bias_m"].ap()[:, mt:mt + 1] if cfg.has_bias else None
            if mt < 16:
                def em_z(a, o=sz.ap()[:, mt, :], i=ps, ba=bap):
                    return (a.activation(o, i, AF.Silu) if ba is None
                            else a.activation(o, i, AF.Silu, bias=ba))
                P.op("ACT", em_z, reads=(("ps", w),) + brd, writes=((tag, "sz", mt),),
                     label="evz")
            elif mt < 16 + NCT:
                ct = mt - 16
                def em_x(a, o=xbcp.ap()[:, ct, 3:515], i=ps, ba=bap):
                    return (a.copy(o, i) if ba is None
                            else a.activation(o, i, AF.Identity, bias=ba))
                P.op("ACT", em_x, reads=(("ps", w),) + brd, writes=((tag, "xbcp", ct),),
                     label="evx")
            else:
                if cfg.has_bias:
                    P.op("ACT", lambda a, o=dtfm.ap()[0:H2, :], i=ps[0:H2, :], ba=bap:
                         a.activation(o, i, AF.Identity, bias=ba[0:H2, :]),
                         reads=(("ps", w),) + brd, writes=((tag, "dtfm"),))
                    src, rsrc = dtfm.ap()[0:H2, :], ((tag, "dtfm"),)
                else:
                    src, rsrc = ps[0:H2, :], (("ps", w),)
                dtb = pcol.ap()[0:H2, 0:1]
                t1 = abfm.ap()[0:H2, :]
                P.op("ACT", lambda a, o=t1, i=src, bb=dtb: a.activation(o, i, AF.Abs, bias=bb),
                     reads=rsrc + ((tag, "dtb"),), writes=((tag, "ab"),))
                P.op("ACT", lambda a, o=t1, i=t1, n1=T["cconst"].ap()[0:H2, 1:2]:
                     a.activation(o, i, AF.Exp, scale=n1),
                     reads=((tag, "ab"), ("cconst",)), writes=((tag, "ab"),))
                P.op("ACT", lambda a, o=t1, i=t1: a.activation(o, i, AF.Ln, bias=1.0),
                     reads=((tag, "ab"),), writes=((tag, "ab"),))
                P.op("ACT", lambda a, o=dtfm.ap()[0:H2, :], i=src, bb=dtb:
                     a.activation(o, i, AF.Relu, bias=bb),
                     reads=rsrc + ((tag, "dtb"),), writes=((tag, "dtfm"),))
                P.op("DVE", lambda v, o=dtfm.ap()[0:H2, :], a_=dtfm.ap()[0:H2, :], b_=t1:
                     v.tensor_tensor(o, a_, b_, ALU.add),
                     reads=((tag, "dtfm"), (tag, "ab")), writes=((tag, "dtfm"),))
        P.op("DVE", lambda v, o=abfm.ap()[0:H2, :], i=dtfm.ap()[0:H2, :],
             s=pcol.ap()[0:H2, 2:3]: v.tensor_scalar_mul(o, i, s),
             reads=((tag, "dtfm"), (tag, "A")), writes=((tag, "ab"),), label="abar")
        if g + 1 < ctx.n_mix_sc:
            ld_xg(g + 1)
        if ctx.dbg and g == 0:
            tap(ctx, "xg", xg.ap(), ((tag, "xg"),))
            tap(ctx, "sz", sz.ap(), tuple((tag, "sz", k) for k in range(16)))
            tap(ctx, "xbcp", xbcp.ap(), tuple((tag, "xbcp", c) for c in range(NCT)))
            tap(ctx, "dtfm", dtfm.ap(), ((tag, "dtfm"),))
            tap(ctx, "abfm", abfm.ap(), ((tag, "ab"),))

        for cc in range(2):
            ccols = slice(cc * 256, (cc + 1) * 256)
            for ct in range(NCT):
                w = ct % 2
                if cc == 0 and True:
                    pass
                dma(ctx, "ACT", cvb.ap()[:, w, :, :], S[pre + "cv"].ap()[ct],
                    reads=(("dram", pre + "cv"),), writes=((tag, "cvb", w),), label="ld_cv")
                ps = PS[w].ap()[:, 0:256]
                for wp in range(4):
                    mm_op(ctx, ps, cvb.ap()[:, w, wp, :],
                          xbcp.ap()[:, ct, cc * 256 + wp:cc * 256 + wp + 256],
                          reads=((tag, "cvb", w), (tag, "xbcp", ct)), pwrite=("ps", w),
                          start=(wp == 0), stop=(wp == 3), label="mmcv")
                P.op("ACT", lambda a, o=xbc.ap()[:, ct, :], i=ps,
                     bb=T["cbias"].ap()[:, ct:ct + 1]: a.activation(o, i, AF.Silu, bias=bb),
                     reads=(("ps", w), (tag, "cb")), writes=((tag, "xbc", ct),), label="evcv")
            if ctx.dbg and g == 0 and cc == 0:
                tap(ctx, "xbc", xbc.ap(), tuple((tag, "xbc", c) for c in range(NCT)))
            _o1 = T["onescf"].ap()[0:H2, 0:1]
            ones_rep = bass.AP(_o1.tensor, _o1.offset, [_o1.ap[0], [0, 256]])
            P.op("DVE", lambda v, o=acfm.ap()[0:H2, :], d0=ones_rep,
                 d1=abfm.ap()[0:H2, ccols]:
                 v.tensor_tensor_scan(o, d0, d1, 0.0, ALU.mult, ALU.add),
                 reads=((tag, "ab"), ("onescf",)), writes=((tag, "ac"),), label="scan")
            P.op("ACT", lambda a, o=dsfm.ap()[0:H2, :], i=acfm.ap()[0:H2, :],
                 bb=acfm.ap()[0:H2, 255:256], n1=T["cconst"].ap()[0:H2, 1:2]:
                 a.activation(o, i, AF.Exp, scale=n1, bias=bb),
                 reads=((tag, "ac"), ("cconst",)), writes=((tag, "ds"),), label="dsfm")
            for tb in range(2):
                c0 = cc * 256 + tb * 128
                for j, (srct, srck) in enumerate(
                        ((dtfm, (tag, "dtfm")), (acfm, (tag, "ac")), (dsfm, (tag, "ds")))):
                    off = 0 if srct is dtfm else None
                    sap = (srct.ap()[0:H2, c0:c0 + 128] if srct is dtfm
                           else srct.ap()[0:H2, tb * 128:tb * 128 + 128])
                    slot = (tb * 3 + j) * 64
                    P.op("PE", lambda t, o=PS[2].ap()[0:128, slot:slot + H2], i=sap,
                         id_=T["identf"].ap()[0:H2, 0:H2]: t.transpose(o, i, id_),
                         reads=(srck, ("identf",)), writes=(("ps", 2),), label="tpdt")
            for tb in range(2):
                for j, (dstt, dstk, neg) in enumerate(
                        ((dttm, (tag, "dttm"), False), (natm, (tag, "natm"), True),
                         (dstm, (tag, "dstm"), False))):
                    slot = (tb * 3 + j) * 64
                    if neg:
                        P.op("DVE", lambda v, o=dstt.ap()[:, tb * 32:tb * 32 + H2],
                             i=PS[2].ap()[:, slot:slot + H2]: v.tensor_scalar_mul(o, i, -1.0),
                             reads=(("ps", 2),), writes=(dstk,), label="evtp")
                    else:
                        P.op("DVE", lambda v, o=dstt.ap()[:, tb * 32:tb * 32 + H2],
                             i=PS[2].ap()[:, slot:slot + H2]: v.tensor_copy(o, i),
                             reads=(("ps", 2),), writes=(dstk,), label="evtp")
            mm_op(ctx, PS[2].ap()[0:1, 384:384 + H2], T["e127"].ap(),
                  natm.ap()[:, 32:32 + H2], reads=(("e127",), (tag, "natm")),
                  pwrite=("ps", 2), start=True, stop=True, label="e127mm")
            P.op("ACT", lambda a, o=erow.ap()[0:1, 0:H2], i=PS[2].ap()[0:1, 384:384 + H2],
                 n1=T["cconst"].ap()[0:1, 1:2]:
                 a.activation(o, i, AF.Exp, scale=n1),
                 reads=(("ps", 2), ("cconst",)), writes=((tag, "erow"),))
            mm_op(ctx, PS[2].ap()[:, 448:448 + H2], T["ones1f"].ap(), erow.ap()[0:1, 0:H2],
                  reads=(("ones1f",), (tag, "erow")), pwrite=("ps", 2),
                  start=True, stop=True, label="ebcmm")
            P.op("DVE", lambda v, o=ebc.ap()[:, 0:H2], i=PS[2].ap()[:, 448:448 + H2]:
                 v.tensor_copy(o, i), reads=(("ps", 2),), writes=((tag, "ebc"),))
            if ctx.dbg and g == 0 and cc == 0:
                tap(ctx, "acfm", acfm.ap(), ((tag, "ac"),))
                tap(ctx, "dsfm", dsfm.ap(), ((tag, "ds"),))
                tap(ctx, "dttm", dttm.ap(), ((tag, "dttm"),))
                tap(ctx, "natm", natm.ap(), ((tag, "natm"),))
                tap(ctx, "dstm", dstm.ap(), ((tag, "dstm"),))
                tap(ctx, "ebc", ebc.ap(), ((tag, "ebc"),))
            bfm_base = 0 if NST == 128 else 0
            B_ct = 16
            for tb in range(2):
                tp = PS["t3" if tb == 0 else "t4"].ap()
                P.op("PE", lambda t, o=tp[:, 0:NST],
                     i=xbc.ap()[0:NST, B_ct, tb * 128:tb * 128 + 128],
                     id_=T["identb"].ap()[0:NST, 0:NST]: t.transpose(o, i, id_),
                     reads=((tag, "xbc", B_ct), ("identb",)), writes=(("ps", "t3" if tb == 0 else "t4"),),
                     label="tpB")
                P.op("DVE", lambda v, o=btok.ap()[:, tb, 0:NST], i=tp[:, 0:NST]:
                     v.tensor_copy(o, i),
                     reads=(("ps", "t3" if tb == 0 else "t4"),), writes=((tag, "btok"),),
                     label="evB")
            nrep = 2 if PP == 64 else 1
            for ct in range(16):
                for tb in range(2):
                    pk = "t3" if (ct * 2 + tb) % 2 == 0 else "t4"
                    tp = PS[pk].ap()
                    P.op("PE", lambda t, o=tp[:, 0:128],
                         i=xbc.ap()[:, ct, tb * 128:tb * 128 + 128],
                         id_=T["identb"].ap(): t.transpose(o, i, id_),
                         reads=((tag, "xbc", ct), ("identb",)), writes=(("ps", pk),),
                         label="tpx")
                    dcol = tb * 32 + (ct * nrep if PP == 64 else ct)
                    dr = drep(dttm, dcol, PP, nrep) if PP == 64 else \
                        bass.AP(dttm.ap().tensor, tb * 32 + ct, [ap0, [1, 1], [0, 128]])
                    P.op("DVE", lambda v, o=xdt.ap()[:, tb, ct * 128:(ct + 1) * 128],
                         a_=tp[:, 0:128], b_=dr: v.tensor_tensor(o, a_, b_, ALU.mult),
                         reads=(("ps", pk), (tag, "dttm")), writes=((tag, "xdt", tb),),
                         label="evx2")
            cfm_lo = 0
            C_ct = 16 if NST == 64 else 17
            Bs = (xbc.ap()[0:64, B_ct, :] if NST == 64 else xbc.ap()[:, B_ct, :])
            if NST == 64:
                P.op("DVE", lambda v, o=cfm0.ap()[0:64, :], i=xbc.ap()[64:128, C_ct, :]:
                     v.tensor_copy(o, i),
                     reads=((tag, "xbc", C_ct),), writes=((tag, "cfm0"),), label="cp_c")
                Cs = cfm0.ap()[0:64, :]
                CKEY = (tag, "cfm0")
            else:
                Cs = xbc.ap()[:, C_ct, :]
                CKEY = (tag, "xbc", C_ct)
            mm_op(ctx, PS[5].ap()[:, 0:256], Bs[:, 0:128], Cs,
                  reads=((tag, "xbc", B_ct), CKEY), pwrite=("ps", 5),
                  start=True, stop=True, label="mmG0")
            mm_op(ctx, PS[5].ap()[:, 256:384], Bs[:, 128:256], Cs[:, 128:256],
                  reads=((tag, "xbc", B_ct), CKEY), pwrite=("ps", 5),
                  start=True, stop=True, label="mmG1")
            P.op("ACT", lambda a, o=gsb.ap(), i=PS[5].ap()[:, 0:384]: a.copy(o, i),
                 reads=(("ps", 5),), writes=((tag, "gsb"),), label="evG")
            if ctx.dbg and g == 0 and cc == 0:
                tap(ctx, "xdt", xdt.ap(), ((tag, "xdt", 0), (tag, "xdt", 1)))
                tap(ctx, "gsb", gsb.ap(), ((tag, "gsb"),))
                tap(ctx, "btok", btok.ap(), ((tag, "btok"),))
            def prep(h):
                hb = h % 2
                s0 = acfm.ap()[h:h + 1, 0:256]
                src = bass.AP(s0.tensor, s0.offset, [s0.ap[0], [0, 128], [1, 256]])
                dma(ctx, "ACT", bacb.ap()[:, hb, :], src,
                    reads=((tag, "ac"),), writes=((tag, "bAc", hb),), label="bac")
                bAc = bacb.ap()[:, hb, :]
                P.op("DVE", lambda v, o=seg.ap()[:, hb, 0:256], i=bAc,
                     s=natm.ap()[:, h:h + 1], m_=T["maskT"].ap()[:, 0:256]:
                     v.scalar_tensor_tensor(o, i, s, m_, ALU.add, ALU.add),
                     reads=(((tag, "bAc", hb)), (tag, "natm"), ("maskT",)),
                     writes=((tag, "seg", hb),), label="seg0")
                P.op("DVE", lambda v, o=seg.ap()[:, hb, 256:384], i=bAc[:, 128:256],
                     s=natm.ap()[:, 32 + h:33 + h], m_=T["maskT"].ap()[:, 0:128]:
                     v.scalar_tensor_tensor(o, i, s, m_, ALU.add, ALU.add),
                     reads=(((tag, "bAc", hb)), (tag, "natm"), ("maskT",)),
                     writes=((tag, "seg", hb),), label="seg1")
                P.op("ACT", lambda a, o=Lb.ap()[:, hb, :], i=seg.ap()[:, hb, :]:
                     a.activation(o, i, AF.Exp),
                     reads=((tag, "seg", hb),), writes=((tag, "L", hb),), label="Lexp")
                P.op("DVE", lambda v, o=glb.ap()[:, hb, :], a_=gsb.ap(), b_=Lb.ap()[:, hb, :]:
                     v.tensor_tensor(o, a_, b_, ALU.mult),
                     reads=((tag, "gsb"), (tag, "L", hb)), writes=((tag, "gl", hb),),
                     label="GL")
                erows = slice(0, NST)
                P.op("ACT", lambda a, o=expb.ap()[erows, hb, :], i=bAc[erows, :]:
                     a.activation(o, i, AF.Exp),
                     reads=(((tag, "bAc", hb)),), writes=((tag, "expb", hb),), label="expbAc")
                P.op("POOL", lambda v, o=cea.ap()[0:NST, hb, :], a_=Cs, b_=expb.ap()[erows, hb, :]:
                     v.tensor_tensor(o, a_, b_, ALU.mult),
                     reads=(CKEY, (tag, "expb", hb)), writes=((tag, "cea", hb),),
                     label="cea")
                if ctx.dbg and g == 0 and cc == 0 and h == 0:
                    tap(ctx, "seg", seg.ap(), ((tag, "seg", 0),))
                    tap(ctx, "Lb", Lb.ap(), ((tag, "L", 0),))
                    tap(ctx, "glb", glb.ap()[:, 0, :], ((tag, "gl", 0),))
                    tap(ctx, "cea", cea.ap()[:, 0, :], ((tag, "cea", 0),))

            def comp(h):
                hb = h % 2
                if PP == 64:
                    pY = 5 + ((h // 2) % 2)
                    r0 = (h % 2) * 64
                else:
                    pY = 5 + (h % 2)
                    r0 = 0
                Yp = PS[pY].ap()
                mm_op(ctx, Yp[r0:r0 + PP, 0:256], xdt.ap()[:, 0, h * PP:(h + 1) * PP],
                      glb.ap()[:, hb, 0:256],
                      reads=(((tag, "xdt", 0)), (tag, "gl", hb)), pwrite=("ps", pY),
                      start=True, stop=False, label="mmY0")
                mm_op(ctx, Yp[r0:r0 + PP, 128:256], xdt.ap()[:, 1, h * PP:(h + 1) * PP],
                      glb.ap()[:, hb, 256:384],
                      reads=(((tag, "xdt", 1)), (tag, "gl", hb)), pwrite=("ps", pY),
                      start=False, stop=False, label="mmY1")
                mm_op(ctx, Yp[r0:r0 + PP, 0:256], Sbf.ap()[0:NST, h * PP:(h + 1) * PP],
                      cea.ap()[0:NST, hb, :],
                      reads=(((tag, "Sb")), (tag, "cea", hb)), pwrite=("ps", pY),
                      start=False, stop=True, label="mmY2")
                if (PP == 64 and h % 2 == 1) or PP == 128:
                    ct = h // 2 if PP == 64 else h
                    P.op("DVE", lambda v, o=ysb.ap()[:, ct, :], i=xbc.ap()[:, ct, :],
                         s=T["dex"].ap()[:, ct:ct + 1], Y=Yp[:, 0:256]:
                         v.scalar_tensor_tensor(o, i, s, Y, ALU.mult, ALU.add),
                         reads=((tag, "xbc", ct), (tag, "dex"), ("ps", pY)),
                         writes=((tag, "y", ct),), label="evY")

            for h in range(H2 + 1):
                if h < H2:
                    prep(h)
                if h > 0:
                    comp(h - 1)
            nh_rep = H2 // 2 if PP == 64 else H2
            for tb in range(2):
                dr = bass.AP(dstm.ap().tensor, tb * 32, [ap0, [1, H2], [0, PP]])
                P.op("DVE", lambda v, o=xdt.ap()[:, tb, :], a_=xdt.ap()[:, tb, :], b_=dr:
                     v.tensor_tensor(o, a_, b_, ALU.mult),
                     reads=((tag, "xdt", tb), (tag, "dstm")), writes=((tag, "xdt", tb),),
                     label="xdd")
            for qq in range(4):
                pM = 5 + (qq % 2)
                qs = slice(qq * 512, (qq + 1) * 512)
                for tb in range(2):
                    mm_op(ctx, PS[pM].ap()[0:NST, 0:512], btok.ap()[:, tb, 0:NST],
                          xdt.ap()[:, tb, qs],
                          reads=((tag, "btok"), (tag, "xdt", tb)), pwrite=("ps", pM),
                          start=(tb == 0), stop=(tb == 1), label="mmM1")
                hlo = qq * (512 // PP)
                _e0 = ebc.ap()[0:NST, 0:1]
                er = bass.AP(_e0.tensor, _e0.offset + hlo, [_e0.ap[0], [1, 512 // PP], [0, PP]])
                P.op("DVE", lambda v, o=Sst.ap()[0:NST, qs], a_=Sst.ap()[0:NST, qs], b_=er:
                     v.tensor_tensor(o, a_, b_, ALU.mult),
                     reads=((tag, "S"), (tag, "ebc")), writes=((tag, "S"),), label="Sdec")
                P.op("DVE", lambda v, o=Sst.ap()[0:NST, qs], a_=Sst.ap()[0:NST, qs],
                     b_=PS[pM].ap()[0:NST, 0:512]: v.tensor_tensor(o, a_, b_, ALU.add),
                     reads=((tag, "S"), ("ps", pM)), writes=((tag, "S"),), label="Sadd")
            P.op("DVE", lambda v, o=Sbf.ap()[0:NST, :], i=Sst.ap()[0:NST, :]:
                 v.tensor_copy(o, i), reads=((tag, "S"),), writes=((tag, "Sb"),),
                 label="Sbf")
            if ctx.dbg and g == 0:
                tap(ctx, "S%d" % cc, Sst.ap(), ((tag, "S"),))
            for kt in range(16):
                P.op("DVE", lambda v, o=ysb.ap()[:, kt, :], a_=ysb.ap()[:, kt, :],
                     b_=sz.ap()[:, kt, ccols]: v.tensor_tensor(o, a_, b_, ALU.mult),
                     reads=((tag, "y", kt), (tag, "sz", kt)), writes=((tag, "y", kt),),
                     label="gate")
            sqb = ln_t[0]
            for kt in range(16):
                b2 = kt % 2
                P.op("ACT", lambda a, o=sqb.ap()[:, 0, b2 * 256:b2 * 256 + 256],
                     i=ysb.ap()[:, kt, :]: a.activation(o, i, AF.Square),
                     reads=((tag, "y", kt),), writes=((tag, "ysq", b2),), label="ysq")
                mm_op(ctx, PS[7].ap()[0:1, 256:512], T["onescb"].ap(),
                      sqb.ap()[:, 0, b2 * 256:b2 * 256 + 256],
                      reads=(("onescb",), (tag, "ysq", b2)), pwrite=("ps", 7),
                      start=(kt == 0), stop=(kt == 15), label="mmst")
            tcol = g * SC + cc * 256
            P.op("ACT", lambda a, o=strow.ap()[0:1, 0:256], i=PS[7].ap()[0:1, 256:512]:
                 a.copy(o, i), reads=(("ps", 7),), writes=((tag, "strow"),), label="evst")
            dma(ctx, "SP", S["st_in"].ap()[0:1, tcol:tcol + 256],
                strow.ap()[0:1, 0:256],
                reads=((tag, "strow"),), writes=(("dram", "st_in"),), label="st_stats")
            qq, lc = tcol // 1024, tcol % 1024
            dma(ctx, "SP",
                S[f"y_in{qq}"].ap().rearrange("(kt p) t -> p kt t", p=128)[:, :, lc:lc + 256],
                ysb.ap(), reads=tuple(((tag, "y", kt)) for kt in range(16)),
                writes=(("dram", "y_in", qq),), label="st_y")
            if tcol + 256 == ctx.n_mix_sc * SC:
                P.op("POOL", lambda g_: g_.collective_compute(
                    "AllGather", ALU.bypass, replica_groups=RG,
                    ins=[S["st_in"].ap()], outs=[S["st_g"].ap()]),
                    reads=(("dram", "st_in"),), writes=(("dram", "st_g"),),
                    kind="cc", label="ag_st")
            if lc + 256 == 1024:
                P.op("POOL", lambda g_, j=qq: g_.collective_compute(
                    "AllGather", ALU.bypass, replica_groups=RG,
                    ins=[S[f"y_in{j}"].ap()], outs=[S["ygt"].ap()[j]]),
                    reads=(("dram", "y_in", qq),), writes=(("dram", "ygt", qq),),
                    kind="cc", label="ag_y")


    ctx.sb_off = xg_base
    yg = _sb(ctx, f"yg_{tag}", [128, 32, 512], BF16, at=xg_base)
    wob = _sb(ctx, f"wob_{tag}", [128, 2, 32, 128], BF16)
    rr = _sb(ctx, f"rr_{tag}", [1, 512], F32)
    rbc2 = _sb(ctx, f"rbc2_{tag}", [128, 512], BF16)
    st2 = _sb(ctx, f"st2_{tag}", [1, 2, 512], F32)
    ctx.sb_off = max(ctx.sb_off, mix_end)


    def em_qsetup(s):
        if not hasattr(ctx, "qreg"):
            pid = s.partition_id()
            reg = s.to_reg(pid)
            s.reg_mod(reg, reg, 2)
            ctx.qreg = reg
            ctx.qyreg = s.alloc_register("qy")
            ctx.qtreg = s.alloc_register("qt")
            s.reg_alu(ctx.qyreg, ctx.qreg, 4194304 * 2, ALU.mult)
            s.reg_alu(ctx.qtreg, ctx.qreg, TOK, ALU.mult)
        return s.nop()
    ctx.mixregs = []
    for m0 in range(2):
        dma(ctx, "SP", wob.ap()[:, m0, :, :], S[pre + "wo"].ap()[m0],
            reads=(("dram", pre + "wo"),), writes=((tag, "wob", m0),), label="ld_wo")
    P.op("SP", em_qsetup, reads=(), writes=((tag, "qreg"),), label="qsetup")

    for g in range(ctx.n_tok_sc):
        cols = slice(g * SC, (g + 1) * SC)

        def em_gofs(s, gg=g):
            r2 = s.alloc_register(f"gofs{gg}{tag}")
            s.reg_alu(r2, ctx.qtreg, gg * SC, ALU.add)
            ctx.gofs = RuntimeValue(r2)
            r5 = s.alloc_register(f"yh0_{gg}{tag}")
            r6 = s.alloc_register(f"yh1_{gg}{tag}")
            s.reg_alu(r5, ctx.qyreg, (gg // 2) * 4194304 + (gg % 2) * 512, ALU.add)
            s.reg_alu(r6, r5, 2097152, ALU.add)
            ctx.yhreg = (r5, r6)
            ctx.mixregs.extend([r2, r5, r6])
            return s.nop()
        P.op("SP", em_gofs, reads=((tag, "qreg"),), writes=((tag, "gofs"),), label="gofs")

        def em_ldst(s):
            src = bass.AP(S["st_g"].ap().tensor, ctx.gofs, [[4096, 2], [1, 512]])
            return s.dma_start(out=st2.ap(), in_=src)
        ctx.P.op("SP", em_ldst, reads=(("dram", "st_g"), (tag, "gofs")),
                 writes=((tag, "st2"),), kind="dma", label="ld_st")
        P.op("DVE", lambda v, o=rr.ap(), a_=st2.ap()[0:1, 0, :], b_=st2.ap()[0:1, 1, :]:
             v.tensor_tensor(o, a_, b_, ALU.add),
             reads=((tag, "st2"),), writes=((tag, "rr"),))
        P.op("DVE", lambda v, o=rr.ap(), i=rr.ap(): v.tensor_scalar_mul(o, i, 1.0 / DIN),
             reads=((tag, "rr"),), writes=((tag, "rr"),))
        P.op("ACT", lambda a, o=rr.ap(), i=rr.ap(), e=T["cconst"].ap()[0:1, 0:1]:
             a.activation(o, i, AF.Sqrt, bias=e),
             reads=((tag, "rr"), ("cconst",)), writes=((tag, "rr"),))
        P.op("DVE", lambda v, o=rr.ap(), i=rr.ap(): v.reciprocal(o, i),
             reads=((tag, "rr"),), writes=((tag, "rr"),))
        mm_op(ctx, PS[2].ap()[:, 0:512], T["ones1f"].ap(), rr.ap(),
              reads=(("ones1f",), (tag, "rr")), pwrite=("ps", 2), start=True, stop=True)
        P.op("ACT", lambda a, o=rbc2.ap(), i=PS[2].ap()[:, 0:512]: a.copy(o, i),
             reads=(("ps", 2),), writes=((tag, "rbc2"),))
        yq_reads = (("dram", "ygt", g // 2), ("dram", "ygt", 2 + g // 2), (tag, "gofs"))
        for half in range(2):
            ks = range(half * 16, (half + 1) * 16)

            def em_ldyg(s, hh=half):
                src = bass.AP(S["ygt"].ap().tensor, RuntimeValue(ctx.yhreg[hh]),
                              [[1024, 128], [131072, 16], [1, 512]])
                return s.dma_start(out=yg.ap()[:, hh * 16:(hh + 1) * 16, :], in_=src)
            ctx.P.op("SP", em_ldyg, reads=yq_reads,
                     writes=tuple((tag, "yg", k) for k in ks), kind="dma", label="ld_yg")
        for kt in range(32):
            P.op("DVE", lambda v, o=yg.ap()[:, kt, :], i=yg.ap()[:, kt, :],
                 s=T["nw"].ap()[:, kt:kt + 1]: v.tensor_scalar_mul(o, i, s),
                 reads=((tag, "yg", kt), (tag, "nw")), writes=((tag, "yg", kt),),
                 label="ynw")
            P.op("DVE", lambda v, o=yg.ap()[:, kt, :], a_=yg.ap()[:, kt, :], b_=rbc2.ap():
                 v.tensor_tensor(o, a_, b_, ALU.mult),
                 reads=((tag, "yg", kt), (tag, "rbc2")), writes=((tag, "yg", kt),),
                 label="yr")
        if ctx.dbg and g == 0:
            tap(ctx, "yg", yg.ap(), tuple((tag, "yg", k) for k in range(32)))
            tap(ctx, "rbc2", rbc2.ap(), ((tag, "rbc2"),))
        for m in range(16):
            w = m % 2
            if g > 0 and m < 2:
                dma(ctx, "SP", wob.ap()[:, w, :, :], S[pre + "wo"].ap()[m],
                    reads=(("dram", pre + "wo"),), writes=((tag, "wob", w),), label="ld_wo")
            ps = PS[w].ap()[:, 0:512]
            for k in range(32):
                mm_op(ctx, ps, wob.ap()[:, w, k, :], yg.ap()[:, k, :],
                      reads=((tag, "wob", w), (tag, "yg", k)), pwrite=("ps", w),
                      start=(k == 0), stop=(k == 31), label="mmop")
            if m + 2 < 16:
                dma(ctx, "SP", wob.ap()[:, (m + 2) % 2, :, :], S[pre + "wo"].ap()[m + 2],
                    reads=(("dram", pre + "wo"),), writes=((tag, "wob", (m + 2) % 2),),
                    label="ld_wo")
            if cfg.has_bias:
                P.op("DVE", lambda v, o=T["R"].ap()[:, m, cols], i=ps,
                     s=T["bias_m"].ap()[:, 36 + m:37 + m], b_=T["R"].ap()[:, m, cols]:
                     v.scalar_tensor_tensor(o, i, s, b_, ALU.add, ALU.add),
                     reads=(("ps", w), (tag, "opb"), ("R", m, g)), writes=(("R", m, g),),
                     label="resop")
            else:
                P.op("DVE", lambda v, o=T["R"].ap()[:, m, cols], a_=ps,
                     b_=T["R"].ap()[:, m, cols]: v.tensor_tensor(o, a_, b_, ALU.add),
                     reads=(("ps", w), ("R", m, g)), writes=(("R", m, g),), label="resop")
    def em_freeregs(s):
        for r in ctx.mixregs:
            s.free_register(r)
        ctx.mixregs = []
        return s.nop()
    P.op("SP", em_freeregs, reads=(), writes=(), label="freeregs")



_CACHED = {}


def kernel(**inputs):
    in_maps = host_prepare(inputs)
    key = "full"
    if key not in _CACHED:
        _CACHED[key] = build_program()
    nc, ctx = _CACHED[key]
    res = run_bass_kernel_spmd(nc, in_maps, core_ids=list(range(8)))
    out = np.zeros((4, SEQ, DM), np.float32)
    for c in range(8):
        b, q = c // 2, c % 2
        out[b, q * TOK:(q + 1) * TOK, :] = res.results[c]["yout"].T
    return out


# version 12
# speedup vs baseline: 1.1669x; 1.1669x over previous
import numpy as np

import concourse.bass as bass
import concourse.mybir as mybir
from concourse.aot_env import allow_ici_all_to_all
from concourse.expressions import RuntimeValue
from concourse.bass_utils import run_bass_kernel_spmd

F32 = mybir.dt.float32
BF16 = mybir.dt.bfloat16
AF = mybir.ActivationFunctionType
ALU = mybir.AluOpType

ENGINES = ("PE", "ACT", "DVE", "POOL", "SP")

DM = 2048
DIN = 4096
TOK = 2048
SEQ = 4096
SC = 512
NKT = DM // 128
NEG = -1.0e9
RG = [[0, 1], [2, 3], [4, 5], [6, 7]]


class Op:
    __slots__ = ("stream", "emit", "deps", "kind", "seq", "tag_idx", "needs_tag",
                 "snapshot", "label", "dma_idx")

    def __init__(self, stream, emit, deps, kind, seq, label=""):
        self.stream, self.emit, self.deps, self.kind, self.seq = stream, emit, deps, kind, seq
        self.tag_idx = None
        self.needs_tag = kind in ("dma", "cc")
        self.snapshot = None
        self.label = label


class _Buf:
    __slots__ = ("writer", "readers")

    def __init__(self):
        self.writer = None
        self.readers = []


class Prog:
    def __init__(self):
        self.ops = []
        self.streams = {e: [] for e in ENGINES}
        self.bufs = {}
        self.last_dma = {}
        self.last_cc = None
        self.dma_tmp_counts = {}

    def _buf(self, key):
        b = self.bufs.get(key)
        if b is None:
            b = _Buf()
            self.bufs[key] = b
        return b

    def op(self, stream, emit, reads=(), writes=(), kind="compute", label=""):
        deps = set()
        for k in reads:
            b = self._buf(k)
            if b.writer is not None:
                deps.add(b.writer)
        for k in writes:
            b = self._buf(k)
            if b.writer is not None:
                deps.add(b.writer)
            deps.update(b.readers)
        o = Op(stream, emit, deps, kind, len(self.ops), label)
        deps.discard(o)
        self.ops.append(o)
        self.streams[stream].append(o)
        if kind == "dma":
            o.dma_idx = self.dma_tmp_counts.get(stream, 0)
            self.dma_tmp_counts[stream] = o.dma_idx + 1
            self.last_dma[(stream, o.dma_idx % Prog.NSUB)] = o
        elif kind == "cc":
            self.last_cc = o
        for k in reads:
            self._buf(k).readers.append(o)
        for k in writes:
            b = self._buf(k)
            b.writer = o
            b.readers = []
        return o

    def barrier(self):
        lasts = [self.streams[e][-1] for e in ENGINES if self.streams[e]]
        lasts += list(self.last_dma.values())
        if self.last_cc is not None:
            lasts.append(self.last_cc)
        for e in ENGINES:
            o = self.op(e, lambda eng: eng.nop(), kind="compute", label="barrier")
            o.deps.update(d for d in lasts
                          if d.stream != e or d.kind != "compute")

    NSUB = 8

    def plan(self):
        for o in self.ops:
            for d in o.deps:
                if d.stream != o.stream or d.kind != "compute":
                    d.needs_tag = True
        counters = {}
        dma_counts = {}

        def sem_of(o):
            if o.kind == "dma":
                return ("dma", o.stream, o.dma_idx % Prog.NSUB)
            if o.kind == "cc":
                return ("cc",)
            return ("eng", o.stream)

        for o in self.ops:
            if not o.needs_tag:
                continue
            if o.kind == "dma":
                o.tag_idx = 16 * (o.dma_idx // Prog.NSUB + 1)
                continue
            s = sem_of(o)
            counters[s] = counters.get(s, 0) + 1
            o.tag_idx = counters[s]

        clocks = {e: {} for e in ENGINES}
        plans = {e: [] for e in ENGINES}
        for o in self.ops:
            ck = clocks[o.stream]
            wmap = {}
            if o.kind == "dma" and o.dma_idx >= Prog.NSUB:
                s = ("dma", o.stream, o.dma_idx % Prog.NSUB)
                v = 16 * (o.dma_idx // Prog.NSUB)
                if ck.get(s, 0) < v:
                    wmap[s] = v
                    ck[s] = v
            for d in sorted(o.deps, key=lambda x: x.seq):
                if d.stream == o.stream and d.kind == "compute":
                    continue
                s = sem_of(d)
                if ck.get(s, 0) < d.tag_idx:
                    wmap[s] = max(wmap.get(s, 0), d.tag_idx)
                    ck[s] = d.tag_idx
                    if d.snapshot:
                        for ks, v in d.snapshot.items():
                            if ck.get(ks, 0) < v:
                                ck[ks] = v
            plans[o.stream].append((sorted(wmap.items(), key=lambda kv: str(kv[0])), o))
            if o.needs_tag:
                snap = dict(ck)
                snap[sem_of(o)] = o.tag_idx
                o.snapshot = snap
        return plans


class Ctx:
    pass


def _sb(ctx, name, shape, dtype, at=None):
    per_part = int(np.prod(shape[1:])) * mybir.dt.size(dtype)
    per_part = (per_part + 31) & ~31
    if at is None:
        off = ctx.sb_off
        ctx.sb_off = off + per_part
    else:
        off = at
        ctx.sb_off = max(ctx.sb_off, off + per_part)
    assert ctx.sb_off <= ctx.sb_limit, (name, ctx.sb_off, ctx.sb_limit)
    return ctx.nc.alloc_sbuf_tensor_at(name, shape, dtype, offset=off)


def tap(ctx, name, ap, reads):
    if not ctx.dbg:
        return
    shp = list(ap.shape)
    t = ctx.nc.dram_tensor("tap_" + name, shp, ap.dtype, kind="ExternalOutput")
    dma(ctx, "ACT", t.ap(), ap, reads=reads, writes=(("tap", name),), label="tap")


def mm_op(ctx, psum_ap, lhsT_ap, rhs_ap, reads, pwrite, start, stop, label=""):
    ctx.P.op("PE",
             lambda t, o=psum_ap, l=lhsT_ap, r=rhs_ap, s1=start, s2=stop:
             t.matmul(o, l, r, start=s1, stop=s2),
             reads=reads, writes=(pwrite,), label=label)


def dma(ctx, stream, out_ap, in_ap, reads, writes, label=""):
    ctx.P.op(stream,
             lambda e, o=out_ap, i=in_ap: e.dma_start(out=o, in_=i),
             reads=reads, writes=writes, kind="dma", label=label)


def rep_ap(t, col_off, step, count, rep):
    base = t.ap()
    return bass.AP(base.tensor, col_off, [base.ap[0], [step, count], [0, rep]])



def _strips(W):
    R, K = W.shape
    m, kt = R // 128, K // 128
    w = W.reshape(m, 128, kt, 128)
    return np.ascontiguousarray(np.transpose(w, (0, 3, 2, 1)))


def _pad_rows(W, mult=128):
    r = W.shape[0]
    pr = (r + mult - 1) // mult * mult
    if pr == r:
        return np.ascontiguousarray(W)
    pad = np.zeros((pr - r,) + W.shape[1:], W.dtype)
    return np.concatenate([W, pad], axis=0)


def _pad_vec(v, n=128):
    out = np.zeros(n, np.float32)
    out[:len(v)] = v
    return out


def host_prepare(inputs):
    u = np.asarray(inputs["L0_attn_a_u"], dtype=np.float32)

    def mixer_pack(W_in, conv_w, conv_b, dt_bias, A_log, Dv, norm_w, W_out,
                   H, P, NST, q, ipb=None, opb=None):
        H2 = H // 2
        hsel = slice(q * H2, (q + 1) * H2)
        csel = slice(q * 2048, (q + 1) * 2048)
        z = W_in[0:DIN][csel]
        x = W_in[DIN:2 * DIN][csel]
        B = W_in[2 * DIN:2 * DIN + NST]
        C = W_in[2 * DIN + NST:2 * DIN + 2 * NST]
        dt = W_in[2 * DIN + 2 * NST:][hsel]
        Wrows = np.concatenate([z, x, B, C, _pad_rows(dt)], axis=0)
        wi = _strips(np.ascontiguousarray(Wrows))
        conv_rows = np.concatenate(
            [conv_w[0:DIN][csel], conv_w[DIN:DIN + NST], conv_w[DIN + NST:DIN + 2 * NST]],
            axis=0)
        cb_rows = np.concatenate(
            [conv_b[0:DIN][csel], conv_b[DIN:DIN + NST], conv_b[DIN + NST:DIN + 2 * NST]],
            axis=0)
        nct = conv_rows.shape[0] // 128
        cv = np.zeros((nct, 128, 4, 128), np.float32)
        for ct in range(nct):
            for w in range(4):
                np.fill_diagonal(cv[ct, :, w, :], conv_rows[ct * 128:(ct + 1) * 128, w])
        d = dict(
            wi=wi.astype(np.float32),
            cv=cv,
            cb=np.ascontiguousarray(cb_rows.reshape(nct, 128).T).astype(np.float32),
            dtb=_pad_vec(dt_bias[hsel]).reshape(128, 1),
            alog=_pad_vec(A_log[hsel]).reshape(128, 1),
            dex=np.ascontiguousarray(
                np.repeat(Dv[hsel], P).reshape(16, 128).T).astype(np.float32),
            nw=np.ascontiguousarray(norm_w.reshape(32, 128).T).astype(np.float32),
            wo=_strips(np.ascontiguousarray(W_out)).astype(np.float32),
        )
        if ipb is not None:
            brows = np.concatenate(
                [ipb[0:DIN][csel], ipb[DIN:2 * DIN][csel],
                 ipb[2 * DIN:2 * DIN + NST], ipb[2 * DIN + NST:2 * DIN + 2 * NST],
                 _pad_vec(ipb[2 * DIN + 2 * NST:][hsel])], axis=0)
            nmt = wi.shape[0]
            d["ipb"] = np.ascontiguousarray(brows.reshape(nmt, 128).T).astype(np.float32)
        if opb is not None:
            d["opb"] = np.ascontiguousarray(opb.reshape(16, 128).T).astype(np.float32)
        return d

    I = 8192
    perm = np.arange(I).reshape(I // 4, 4).T.reshape(-1)
    wg = _strips(np.ascontiguousarray(np.asarray(inputs["L0_ffn_w_gate"], np.float32).T[perm]))
    wu = _strips(np.ascontiguousarray(np.asarray(inputs["L0_ffn_w_up"], np.float32).T[perm]))
    wd = _strips(np.ascontiguousarray(np.asarray(inputs["L0_ffn_w_down"], np.float32)[perm].T))

    ident = np.eye(128, dtype=np.float32)
    ones1f = np.ones((1, 128), np.float32)
    onescol = np.ones((128, 1), np.float32)
    e127 = np.zeros((128, 1), np.float32)
    e127[127, 0] = 1.0
    cconst = np.zeros((128, 4), np.float32)
    cconst[:, 0] = 1e-5
    cconst[:, 1] = -1.0
    maskT = np.zeros((128, 256), np.float32)
    maskT[:, 0:128] = np.where(
        np.arange(128)[:, None] <= np.arange(128)[None, :], 0.0, NEG)

    in_maps = []
    for c in range(8):
        b, q = c // 2, c % 2
        m = {}
        m["u"] = np.ascontiguousarray(u[b, q * TOK:(q + 1) * TOK, :].T)
        da = mixer_pack(np.asarray(inputs["L0_attn_a_W_in"], np.float32),
                        np.asarray(inputs["L0_attn_a_conv_w"], np.float32),
                        np.asarray(inputs["L0_attn_a_conv_b"], np.float32),
                        np.asarray(inputs["L0_attn_a_dt_bias"], np.float32),
                        np.asarray(inputs["L0_attn_a_A_log"], np.float32),
                        np.asarray(inputs["L0_attn_a_D"], np.float32),
                        np.asarray(inputs["L0_attn_a_norm_w"], np.float32),
                        np.asarray(inputs["L0_attn_a_W_out"], np.float32),
                        64, 64, 64, q)
        for k, v in da.items():
            m["a_" + k] = v
        db = mixer_pack(np.asarray(inputs["L0_attn_b_in_proj_w"], np.float32),
                        np.asarray(inputs["L0_attn_b_conv_w"], np.float32)[:, 0, :],
                        np.asarray(inputs["L0_attn_b_conv_b"], np.float32),
                        np.asarray(inputs["L0_attn_b_dt_bias"], np.float32),
                        np.asarray(inputs["L0_attn_b_A_log"], np.float32),
                        np.asarray(inputs["L0_attn_b_D"], np.float32),
                        np.asarray(inputs["L0_attn_b_norm_w"], np.float32),
                        np.asarray(inputs["L0_attn_b_out_proj_w"], np.float32),
                        32, 128, 128, q,
                        ipb=np.asarray(inputs["L0_attn_b_in_proj_b"], np.float32),
                        opb=np.asarray(inputs["L0_attn_b_out_proj_b"], np.float32))
        for k, v in db.items():
            m["b_" + k] = v
        m["wg"], m["wu"], m["wd"] = wg, wu, wd
        m["ident"], m["ones1f"], m["onescol"] = ident, ones1f, onescol
        m["e127"], m["maskT"] = e127, maskT
        m["cconst"] = cconst
        in_maps.append(m)
    return in_maps



class MixCfg:
    def __init__(self, pre, H, P, NST, has_bias):
        self.pre, self.H, self.P, self.NST = pre, H, P, NST
        self.H2 = H // 2
        self.NCT = (2048 + 2 * NST) // 128
        self.NMT = 16 + self.NCT + 1
        self.has_bias = has_bias


CFG_A = MixCfg("a_", 64, 64, 64, False)
CFG_B = MixCfg("b_", 32, 128, 128, True)


def build_program(phases=("mixa", "ffn1", "mixb", "ffn2"), n_mix_sc=8, n_tok_sc=4,
                  dbg=False):
    nc = bass.Bass("TRN2", target_bir_lowering=False, debug=False)
    ctx = Ctx()
    ctx.nc, ctx.P = nc, Prog()
    ctx.phases, ctx.n_mix_sc, ctx.n_tok_sc = phases, n_mix_sc, n_tok_sc
    ctx.dbg = dbg
    ctx.tapn = 0

    D = {}
    ctx.D = D
    D["u"] = nc.dram_tensor("u", [DM, TOK], F32, kind="ExternalInput")
    for pre, cfg in (("a_", CFG_A), ("b_", CFG_B)):
        if ("mixa" if pre == "a_" else "mixb") not in phases:
            continue
        D[pre + "wi"] = nc.dram_tensor(pre + "wi", [cfg.NMT, 128, NKT, 128], F32, kind="ExternalInput")
        D[pre + "cv"] = nc.dram_tensor(pre + "cv", [cfg.NCT, 128, 4, 128], F32, kind="ExternalInput")
        D[pre + "cb"] = nc.dram_tensor(pre + "cb", [128, cfg.NCT], F32, kind="ExternalInput")
        D[pre + "dtb"] = nc.dram_tensor(pre + "dtb", [128, 1], F32, kind="ExternalInput")
        D[pre + "alog"] = nc.dram_tensor(pre + "alog", [128, 1], F32, kind="ExternalInput")
        D[pre + "dex"] = nc.dram_tensor(pre + "dex", [128, 16], F32, kind="ExternalInput")
        D[pre + "nw"] = nc.dram_tensor(pre + "nw", [128, 32], F32, kind="ExternalInput")
        D[pre + "wo"] = nc.dram_tensor(pre + "wo", [16, 128, 32, 128], F32, kind="ExternalInput")
    if "mixb" in phases:
        D["b_ipb"] = nc.dram_tensor("b_ipb", [128, CFG_B.NMT], F32, kind="ExternalInput")
        D["b_opb"] = nc.dram_tensor("b_opb", [128, 16], F32, kind="ExternalInput")
    if "ffn1" in phases or "ffn2" in phases:
        D["wg"] = nc.dram_tensor("wg", [64, 128, NKT, 128], F32, kind="ExternalInput")
        D["wu"] = nc.dram_tensor("wu", [64, 128, NKT, 128], F32, kind="ExternalInput")
        D["wd"] = nc.dram_tensor("wd", [16, 128, 64, 128], F32, kind="ExternalInput")
    for nm, shp in (("ident", [128, 128]), ("ones1f", [1, 128]), ("onescol", [128, 1]),
                    ("e127", [128, 1]), ("maskT", [128, 256]), ("cconst", [128, 4])):
        D[nm] = nc.dram_tensor(nm, shp, F32, kind="ExternalInput")
    D["yout"] = nc.dram_tensor("yout", [DM, TOK], F32, kind="ExternalOutput")

    S = {}
    ctx.S = S
    S["u"] = nc.dram_tensor("s_u", [DM, TOK], BF16)
    for pre, cfg in (("a_", CFG_A), ("b_", CFG_B)):
        if ("mixa" if pre == "a_" else "mixb") not in phases:
            continue
        S[pre + "wi"] = nc.dram_tensor("s_" + pre + "wi", [cfg.NMT, 128, NKT, 128], BF16)
        S[pre + "cv"] = nc.dram_tensor("s_" + pre + "cv", [cfg.NCT, 128, 4, 128], BF16)
        S[pre + "wo"] = nc.dram_tensor("s_" + pre + "wo", [16, 128, 32, 128], BF16)
    if "ffn1" in phases or "ffn2" in phases:
        S["wg"] = nc.dram_tensor("s_wg", [64, 128, NKT, 128], BF16)
        S["wu"] = nc.dram_tensor("s_wu", [64, 128, NKT, 128], BF16)
        S["wd"] = nc.dram_tensor("s_wd", [16, 128, 64, 128], BF16)
    if "mixa" in phases or "mixb" in phases:
        for j in range(4):
            S[f"xcn_in{j}"] = nc.dram_tensor(f"s_xcn_in{j}", [DM, SC], BF16)
            S[f"xcn_g{j}"] = nc.dram_tensor(f"s_xcn_g{j}", [2 * DM, SC], BF16)
            S[f"y_in{j}"] = nc.dram_tensor(f"s_y_in{j}", [2048, 1024], BF16)
        S["ygt"] = nc.dram_tensor("s_ygt", [4, 2 * 2048, 1024], BF16)
        S["st_in"] = nc.dram_tensor("s_st_in", [1, SEQ], F32)
        S["st_g"] = nc.dram_tensor("s_st_g", [2, SEQ], F32)

    ctx.sb_limit = nc.SBUF_PARTITION_SIZE_BYTES - 16 * 1024 - 640
    ctx.sb_off = 16 * 1024 + 640
    T = {}
    ctx.T = T
    T["R"] = _sb(ctx, "sb_R", [128, NKT, TOK], BF16)
    T["identf"] = _sb(ctx, "sb_identf", [128, 128], F32)
    T["identb"] = _sb(ctx, "sb_identb", [128, 128], BF16)
    T["ones1f"] = _sb(ctx, "sb_ones1f", [1, 128], F32)
    T["ones1b"] = _sb(ctx, "sb_ones1b", [1, 128], BF16)
    T["onescf"] = _sb(ctx, "sb_onescf", [128, 1], F32)
    T["onescb"] = _sb(ctx, "sb_onescb", [128, 1], BF16)
    T["cconst"] = _sb(ctx, "sb_cconst", [128, 4], F32)
    T["e127"] = _sb(ctx, "sb_e127", [128, 1], F32)
    T["maskT"] = _sb(ctx, "sb_maskT", [128, 256], F32)
    T["pcol"] = _sb(ctx, "sb_pcol", [128, 4], F32)
    T["dex"] = _sb(ctx, "sb_dex", [128, 16], F32)
    T["nw"] = _sb(ctx, "sb_nw", [128, 32], F32)
    T["cbias"] = _sb(ctx, "sb_cbias", [128, 20], F32)
    T["bias_m"] = _sb(ctx, "sb_bias_m", [128, 56], F32)
    ctx.sb_phase_base = ctx.sb_off

    PS = {}
    ctx.PS = PS
    for i in range(8):
        nc.psum_base = i
        PS[i] = nc.alloc_psum_tensor(f"pb{i}", [128, 512], F32)
    nc.psum_base = 3
    PS["t3"] = nc.alloc_psum_tensor("pbt3", [128, 512], BF16)
    nc.psum_base = 4
    PS["t4"] = nc.alloc_psum_tensor("pbt4", [128, 512], BF16)
    nc.psum_base = 8

    emit_init(ctx)
    for ph in phases:
        if ph == "mixa":
            emit_mixer(ctx, CFG_A)
        elif ph == "mixb":
            emit_mixer(ctx, CFG_B)
        elif ph in ("ffn1", "ffn2"):
            emit_ffn(ctx, ph)
        ctx.P.barrier()
    emit_out(ctx)
    ctx.P.barrier()

    plans = ctx.P.plan()

    import contextlib
    with contextlib.ExitStack() as _stk:
        s_pe = _stk.enter_context(nc.semaphore("sem_PE"))
        s_act = _stk.enter_context(nc.semaphore("sem_ACT"))
        s_dve = _stk.enter_context(nc.semaphore("sem_DVE"))
        s_pool = _stk.enter_context(nc.semaphore("sem_POOL"))
        s_sp = _stk.enter_context(nc.semaphore("sem_SP"))
        s_cc = _stk.enter_context(nc.semaphore("cc_sem"))
        dsems = {}
        for q in ("SP", "ACT", "POOL"):
            for i in range(Prog.NSUB):
                dsems[(q, i)] = _stk.enter_context(nc.semaphore(f"dma_{q}{i}"))
        block = _stk.enter_context(nc.Block())
        sems = {"PE": s_pe, "ACT": s_act, "DVE": s_dve, "POOL": s_pool, "SP": s_sp}

        def resolve(s):
            if s[0] == "dma":
                return dsems[(s[1], s[2])]
            if s[0] == "cc":
                return s_cc
            return sems[s[1]]

        def run(stream, eng):
            for waits, o in plans[stream]:
                for s, v in waits:
                    eng.wait_ge(resolve(s), v)
                inst = o.emit(eng)
                if o.needs_tag:
                    if o.kind == "dma":
                        inst.then_inc(dsems[(o.stream, o.dma_idx % Prog.NSUB)], 16)
                    elif o.kind == "cc":
                        inst.then_inc(s_cc, 1)
                    else:
                        inst.then_inc(sems[o.stream], 1)

        @block.tensor
        def _(t):
            run("PE", t)

        @block.scalar
        def _(a):
            run("ACT", a)

        @block.vector
        def _(v):
            run("DVE", v)

        @block.gpsimd
        def _(g):
            run("POOL", g)

        @block.sync
        def _(s):
            run("SP", s)

    return nc, ctx



def emit_init(ctx):
    P, D, S, T = ctx.P, ctx.D, ctx.S, ctx.T
    order = ["u"]
    for ph in ctx.phases:
        if ph == "mixa":
            order += ["a_wi", "a_cv", "a_wo"]
        elif ph == "mixb":
            order += ["b_wi", "b_cv", "b_wo"]
        elif ph == "ffn1" or (ph == "ffn2" and "ffn1" not in ctx.phases):
            order += ["wg", "wu", "wd"]
    for nm in order:
        dma(ctx, "POOL", S[nm].ap(), D[nm].ap(), reads=(),
            writes=(("dram", nm),), label=f"cast_{nm}")
    for nm, dn in (("identf", "ident"), ("ones1f", "ones1f"), ("onescf", "onescol"),
                   ("e127", "e127"), ("maskT", "maskT"), ("cconst", "cconst")):
        dma(ctx, "SP", T[nm].ap(), D[dn].ap(), (), ((nm,),), label="ldc")
    P.op("DVE", lambda v, o=T["identb"].ap(), i=T["identf"].ap(): v.tensor_copy(o, i),
         reads=(("identf",),), writes=(("identb",),))
    P.op("DVE", lambda v, o=T["ones1b"].ap(), i=T["ones1f"].ap(): v.tensor_copy(o, i),
         reads=(("ones1f",),), writes=(("ones1b",),))
    P.op("DVE", lambda v, o=T["onescb"].ap(), i=T["onescf"].ap(): v.tensor_copy(o, i),
         reads=(("onescf",),), writes=(("onescb",),))
    dma(ctx, "SP", T["R"].ap(), S["u"].ap().rearrange("(kt p) t -> p kt t", p=128),
        reads=(("dram", "u"),),
        writes=tuple(("R", kt, gg) for kt in range(NKT) for gg in range(4)), label="loadR")
    P.barrier()


def emit_out(ctx):
    P, D, T = ctx.P, ctx.D, ctx.T
    dma(ctx, "POOL", D["yout"].ap().rearrange("(kt p) t -> p kt t", p=128), T["R"].ap(),
        reads=tuple(("R", kt, gg) for kt in range(NKT) for gg in range(4)),
        writes=(("dram", "yout"),), label="store_out")



def emit_ln(ctx, cols, xcn_ap, xcn_key, psA, psB, tag, ln_t):
    P, T, PS = ctx.P, ctx.T, ctx.PS
    R = ctx.T["R"]
    gR = cols.start // SC
    sq, st, mb, rb = ln_t
    s1p = PS[psA].ap()[0:1, 0:512]
    s2p = PS[psB].ap()[0:1, 0:512]
    for kt in range(NKT):
        mm_op(ctx, s1p, T["onescb"].ap(), R.ap()[:, kt, cols],
              reads=(("onescb",), ("R", kt, gR)), pwrite=("ps", psA),
              start=(kt == 0), stop=(kt == NKT - 1), label="lns1")
    for kt in range(NKT):
        P.op("ACT", lambda a, o=sq.ap()[:, 0, :], i=R.ap()[:, kt, cols]:
             a.activation(o, i, AF.Square),
             reads=(("R", kt, gR),), writes=((tag, "lnsq", 0),), label="lnsq")
        mm_op(ctx, s2p, T["onescb"].ap(), sq.ap()[:, 0, :],
              reads=(("onescb",), (tag, "lnsq", 0)), pwrite=("ps", psB),
              start=(kt == 0), stop=(kt == NKT - 1), label="lns2")
    m_ = st.ap()[0:1, 0:512]
    t_ = st.ap()[0:1, 512:1024]
    v_ = st.ap()[0:1, 1024:1536]
    P.op("DVE", lambda v, o=m_, i=s1p: v.tensor_scalar_mul(o, i, 1.0 / DM),
         reads=(("ps", psA),), writes=((tag, "m"),))
    P.op("DVE", lambda v, o=t_, i=s2p: v.tensor_scalar_mul(o, i, 1.0 / DM),
         reads=(("ps", psB),), writes=((tag, "t"),))
    P.op("DVE", lambda v, o=v_, a=m_: v.scalar_tensor_tensor(o, a, -1.0, a, ALU.mult, ALU.mult),
         reads=((tag, "m"),), writes=((tag, "v"),))
    P.op("DVE", lambda v, o=v_, a=v_, b=t_: v.tensor_tensor(o, a, b, ALU.add),
         reads=((tag, "v"), (tag, "t")), writes=((tag, "v"),))
    P.op("ACT", lambda a, o=v_, i=v_, e=T["cconst"].ap()[0:1, 0:1]:
         a.activation(o, i, AF.Sqrt, bias=e),
         reads=((tag, "v"), ("cconst",)), writes=((tag, "v"),))
    P.op("DVE", lambda v, o=v_, i=v_: v.reciprocal(o, i),
         reads=((tag, "v"),), writes=((tag, "v"),))
    mm_op(ctx, PS[psA].ap()[:, 0:512], T["ones1f"].ap(), m_,
          reads=(("ones1f",), (tag, "m")), pwrite=("ps", psA), start=True, stop=True)
    mm_op(ctx, PS[psB].ap()[:, 0:512], T["ones1f"].ap(), v_,
          reads=(("ones1f",), (tag, "v")), pwrite=("ps", psB), start=True, stop=True)
    P.op("ACT", lambda a, o=mb.ap(), i=PS[psA].ap()[:, 0:512]: a.copy(o, i),
         reads=(("ps", psA),), writes=((tag, "mb"),))
    P.op("ACT", lambda a, o=rb.ap(), i=PS[psB].ap()[:, 0:512]: a.copy(o, i),
         reads=(("ps", psB),), writes=((tag, "rb"),))
    for kt in range(NKT):
        P.op("POOL", lambda v, o=xcn_ap[:, kt, :], a=R.ap()[:, kt, cols], b=mb.ap():
             v.tensor_tensor(o, a, b, ALU.subtract),
             reads=(("R", kt, gR), (tag, "mb")), writes=((xcn_key, kt),))
        P.op("POOL", lambda v, o=xcn_ap[:, kt, :], a=xcn_ap[:, kt, :], b=rb.ap():
             v.tensor_tensor(o, a, b, ALU.mult),
             reads=((xcn_key, kt), (tag, "rb")), writes=((xcn_key, kt),))


def _ln_alloc(ctx, tag):
    sq = _sb(ctx, f"lnsq_{tag}", [128, 1, 512], BF16)
    ctx.lnst_off = ctx.sb_off
    st = _sb(ctx, f"lnst_{tag}", [1, 3 * 512], F32)
    mb = _sb(ctx, f"lnmb_{tag}", [128, 512], BF16)
    rb = _sb(ctx, f"lnrb_{tag}", [128, 512], BF16)
    return sq, st, mb, rb



def emit_ffn(ctx, tag):
    P, S, T, PS = ctx.P, ctx.S, ctx.T, ctx.PS
    ctx.sb_off = ctx.sb_phase_base
    ln_t = _ln_alloc(ctx, tag)
    xcn = _sb(ctx, f"xcn_{tag}", [128, NKT, 512], BF16)
    h = _sb(ctx, f"h_{tag}", [128, 64, 512], BF16)
    gs = _sb(ctx, f"gs_{tag}", [128, 1, 512], BF16)
    wgb = _sb(ctx, f"wgb_{tag}", [128, 2, NKT, 128], BF16)
    wub = _sb(ctx, f"wub_{tag}", [128, 2, NKT, 128], BF16)
    spbase = ctx.sb_off
    aq = _sb(ctx, f"aq_{tag}", [128, 4, 512], BF16)
    bq = [_sb(ctx, f"b{i}_{tag}", [128, 512], BF16) for i in range(4)]
    gq = _sb(ctx, f"gq_{tag}", [128, 512], BF16)
    sp_end = ctx.sb_off
    wdb = _sb(ctx, f"wdb_{tag}", [128, 2, 32, 128], BF16, at=spbase)
    ctx.sb_off = max(ctx.sb_off, sp_end)
    SPK = tuple((tag, "sp", i) for i in range(6))

    emit_ln(ctx, slice(0, SC), xcn.ap(), (tag, "xcn"), 6, 7, tag, ln_t)

    def sparsify(g0):
            hk = lambda j: ((tag, "h", j * 16 + g0),)
            hk = lambda j: ((tag, "h", j * 16 + g0),)
            for j in range(4):
                P.op("ACT", lambda a, o=aq.ap()[:, j, :],
                     i=h.ap()[:, j * 16 + g0, :]:
                     a.activation(o, i, AF.Abs),
                     reads=hk(j), writes=(SPK[j],), label="spabs")
            aj = [aq.ap()[:, j, :] for j in range(4)]
            bb = [t.ap() for t in bq]
            gg = gq.ap()
            AQ = tuple(SPK[j] for j in range(4))
            BK = lambda i: (tag, "b", i)
            GK = ((tag, "gq"),)

            def tt(dst, dk, x, y, op, reads, label="sp"):
                P.op("DVE", lambda v, o=dst, a_=x, c=y, op_=op: v.tensor_tensor(o, a_, c, op_),
                     reads=reads, writes=(dk,), label=label)

            def cpy(dst, dk, src, reads):
                P.op("DVE", lambda v, o=dst, i=src: v.tensor_copy(o, i),
                     reads=reads, writes=(dk,), label="spc")

            tt(bb[0], BK(0), aj[0], aj[1], ALU.is_ge, AQ)
            cpy(bb[1], BK(1), bb[0], (BK(0),))
            tt(gg, GK[0], aj[0], aj[2], ALU.is_ge, AQ)
            tt(bb[0], BK(0), bb[0], gg, ALU.add, (BK(0), GK[0]))
            cpy(bb[2], BK(2), gg, (GK[0],))
            tt(gg, GK[0], aj[0], aj[3], ALU.is_ge, AQ)
            tt(bb[0], BK(0), bb[0], gg, ALU.add, (BK(0), GK[0]))
            cpy(bb[3], BK(3), gg, (GK[0],))
            tt(gg, GK[0], aj[1], aj[2], ALU.is_ge, AQ)
            tt(bb[1], BK(1), bb[1], gg, ALU.subtract, (BK(1), GK[0]))
            tt(bb[2], BK(2), bb[2], gg, ALU.add, (BK(2), GK[0]))
            tt(gg, GK[0], aj[1], aj[3], ALU.is_ge, AQ)
            tt(bb[1], BK(1), bb[1], gg, ALU.subtract, (BK(1), GK[0]))
            tt(bb[3], BK(3), bb[3], gg, ALU.add, (BK(3), GK[0]))
            tt(gg, GK[0], aj[2], aj[3], ALU.is_ge, AQ)
            tt(bb[2], BK(2), bb[2], gg, ALU.subtract, (BK(2), GK[0]))
            tt(bb[3], BK(3), bb[3], gg, ALU.add, (BK(3), GK[0]))
            thr = [(ALU.is_ge, 1.5), (ALU.is_le, -0.5), (ALU.is_le, 0.5), (ALU.is_le, 1.5)]
            for i in range(4):
                op_, cc_ = thr[i]
                P.op("DVE", lambda v, o=bb[i], i_=bb[i], c=cc_, op2=op_:
                     v.tensor_single_scalar(o, i_, c, op2),
                     reads=(BK(i),), writes=(BK(i),), label="spthr")
            for j in range(4):
                P.op("DVE", lambda v, o=h.ap()[:, j * 16 + g0, :],
                     a_=h.ap()[:, j * 16 + g0, :], c=bb[j]:
                     v.tensor_tensor(o, a_, c, ALU.mult),
                     reads=hk(j) + (BK(j),), writes=hk(j), label="spapply")

    for g in range(ctx.n_tok_sc):
        cols = slice(g * SC, (g + 1) * SC)
        for mi in range(64):
            gt, j = mi // 4, mi % 4
            m = j * 16 + gt
            w = mi % 2
            dma(ctx, "SP", wgb.ap()[:, w, :, :], S["wg"].ap()[m],
                reads=(("dram", "wg"),), writes=((tag, "wgb", w),), label="ldwg")
            dma(ctx, "SP", wub.ap()[:, w, :, :], S["wu"].ap()[m],
                reads=(("dram", "wu"),), writes=((tag, "wub", w),), label="ldwu")
            pg = PS[w].ap()[:, 0:512]
            pu = PS[2 + w].ap()[:, 0:512]
            for k in range(NKT):
                mm_op(ctx, pg, wgb.ap()[:, w, k, :], xcn.ap()[:, k, :],
                      reads=((tag, "wgb", w), ((tag, "xcn"), k)), pwrite=("ps", w),
                      start=(k == 0), stop=(k == NKT - 1), label="mmg")
            for k in range(NKT):
                mm_op(ctx, pu, wub.ap()[:, w, k, :], xcn.ap()[:, k, :],
                      reads=((tag, "wub", w), ((tag, "xcn"), k)), pwrite=("ps", 2 + w),
                      start=(k == 0), stop=(k == NKT - 1), label="mmu")
            P.op("ACT", lambda a, o=gs.ap()[:, 0, :], i=pg: a.activation(o, i, AF.Silu),
                 reads=(("ps", w),), writes=((tag, "gs", 0),), label="silug")
            P.op("DVE", lambda v, o=h.ap()[:, m, :], a=gs.ap()[:, 0, :], b=pu:
                 v.tensor_tensor(o, a, b, ALU.mult),
                 reads=((tag, "gs", 0), ("ps", 2 + w)), writes=((tag, "h", m),), label="hmul")
            if j == 3:
                sparsify(gt)
        if g + 1 < ctx.n_tok_sc:
            emit_ln(ctx, slice((g + 1) * SC, (g + 2) * SC), xcn.ap(), (tag, "xcn"),
                    6, 7, tag, ln_t)
        allsp = tuple(SPK) + tuple((tag, "b", i) for i in range(4)) + ((tag, "gq"),)
        for m in range(16):
            pd = PS[4 + (m % 2)].ap()[:, 0:512]
            for half in range(2):
                w = (2 * m + half) % 2
                dma(ctx, "SP", wdb.ap()[:, w, :, :], S["wd"].ap()[m][:, half * 32:(half + 1) * 32, :],
                    reads=(("dram", "wd"),) + (allsp if m == 0 else ()),
                    writes=((tag, "wdb", w),) + (allsp if m == 0 else ()),
                    label="ldwd")
                for kk in range(32):
                    k = half * 32 + kk
                    mm_op(ctx, pd, wdb.ap()[:, w, kk, :], h.ap()[:, k, :],
                          reads=((tag, "wdb", w), (tag, "h", k)), pwrite=("ps", 4 + (m % 2)),
                          start=(k == 0), stop=(k == 63), label="mmd")
            P.op("DVE", lambda v, o=T["R"].ap()[:, m, cols], a_=pd,
                 b_=T["R"].ap()[:, m, cols]: v.tensor_tensor(o, a_, b_, ALU.add),
                 reads=(("ps", 4 + (m % 2)), ("R", m, g)), writes=(("R", m, g),), label="resid")



def emit_mixer(ctx, cfg):
    P, D, S, T, PS, nc = ctx.P, ctx.D, ctx.S, ctx.T, ctx.PS, ctx.nc
    pre = cfg.pre
    tag = "mx" + pre[0]
    H2, PP, NST, NCT, NMT = cfg.H2, cfg.P, cfg.NST, cfg.NCT, cfg.NMT
    HPD = 2048
    ctx.sb_off = ctx.sb_phase_base
    ln_t = _ln_alloc(ctx, tag)

    pcol = T["pcol"]
    dma(ctx, "SP", pcol.ap()[:, 0:1], D[pre + "dtb"].ap(), (), ((tag, "dtb"),))
    dma(ctx, "SP", pcol.ap()[:, 1:2], D[pre + "alog"].ap(), (), ((tag, "alog"),))
    P.op("ACT", lambda a, o=pcol.ap()[:, 2:3], i=pcol.ap()[:, 1:2]:
         a.activation(o, i, AF.Exp), reads=((tag, "alog"),), writes=((tag, "A"),))
    P.op("DVE", lambda v, o=pcol.ap()[:, 2:3], i=pcol.ap()[:, 2:3]:
         v.tensor_scalar_mul(o, i, -1.0), reads=((tag, "A"),), writes=((tag, "A"),))
    dma(ctx, "SP", T["dex"].ap(), D[pre + "dex"].ap(), (), ((tag, "dex"),))
    dma(ctx, "SP", T["nw"].ap(), D[pre + "nw"].ap(), (), ((tag, "nw"),))
    dma(ctx, "SP", T["cbias"].ap()[:, 0:NCT],
        D[pre + "cb"].ap(), (), ((tag, "cb"),))
    if cfg.has_bias:
        dma(ctx, "SP", T["bias_m"].ap()[:, 0:NMT],
            D["b_ipb"].ap(), (), ((tag, "ipb"),))
        dma(ctx, "SP", T["bias_m"].ap()[:, 36:52],
            D["b_opb"].ap(), (), ((tag, "opb"),))

    xg_base = ctx.sb_off
    xcn = _sb(ctx, f"xcn_{tag}", [128, NKT, 512], BF16)
    for g in range(ctx.n_tok_sc):
        cols = slice(g * SC, (g + 1) * SC)
        emit_ln(ctx, cols, xcn.ap(), (tag, "xcn"), 5, 6, tag, ln_t)
        if ctx.dbg and g == 0:
            tap(ctx, "R0" + tag, T["R"].ap()[:, :, 0:512], tuple(("R", kt, 0) for kt in range(NKT)))
            tap(ctx, "xcn1" + tag, xcn.ap(), tuple(((tag, "xcn"), kt) for kt in range(NKT)))
        dma(ctx, "SP",
            S[f"xcn_in{g}"].ap().rearrange("(kt p) t -> p kt t", p=128),
            xcn.ap(), reads=tuple(((tag, "xcn"), kt) for kt in range(NKT)),
            writes=(("dram", "xcn_in", g),), label="st_xcn")
        P.op("POOL", lambda g_, j=g: g_.collective_compute(
            "AllGather", ALU.bypass, replica_groups=RG,
            ins=[S[f"xcn_in{j}"].ap()], outs=[S[f"xcn_g{j}"].ap()]),
            reads=(("dram", "xcn_in", g),), writes=(("dram", "xcn_g", g),), kind="cc",
            label="ag_xcn")

    ctx.sb_off = xg_base
    xg = _sb(ctx, f"xg_{tag}", [128, NKT, 512], BF16)
    wib = _sb(ctx, f"wib_{tag}", [128, 2, NKT, 128], BF16)
    cvb = _sb(ctx, f"cvb_{tag}", [128, 2, 4, 128], BF16)
    sz = _sb(ctx, f"sz_{tag}", [128, 16, 512], BF16)
    xbcp = _sb(ctx, f"xbcp_{tag}", [128, NCT, 516], BF16)
    xbc = _sb(ctx, f"xbc_{tag}", [128, NCT, 256], BF16)
    xdt = _sb(ctx, f"xdt_{tag}", [128, 2, HPD], BF16)
    ysb = _sb(ctx, f"ysb_{tag}", [128, 16, 256], BF16)
    Sst = _sb(ctx, f"Sst_{tag}", [128, HPD], F32)
    Sbf = _sb(ctx, f"Sbf_{tag}", [128, HPD], BF16)
    dtfm = _sb(ctx, f"dtfm_{tag}", [128, 512], F32)
    abfm = _sb(ctx, f"abfm_{tag}", [128, 512], F32)
    acfm = _sb(ctx, f"acfm_{tag}", [128, 256], F32)
    dsfm = _sb(ctx, f"dsfm_{tag}", [128, 256], F32)
    dttm = _sb(ctx, f"dttm_{tag}", [128, 64], F32)
    natm = _sb(ctx, f"natm_{tag}", [128, 64], F32)
    dstm = _sb(ctx, f"dstm_{tag}", [128, 64], F32)
    erow = _sb(ctx, f"erow_{tag}", [1, 32], F32)
    ebc = _sb(ctx, f"ebc_{tag}", [128, 32], F32)
    gsb = _sb(ctx, f"gsb_{tag}", [128, 384], BF16)
    btok = _sb(ctx, f"btok_{tag}", [128, 2, 128], BF16)
    seg = _sb(ctx, f"seg_{tag}", [128, 2, 384], F32)
    Lb = _sb(ctx, f"Lb_{tag}", [128, 2, 384], BF16)
    glb = _sb(ctx, f"glb_{tag}", [128, 2, 384], BF16)
    cea = _sb(ctx, f"cea_{tag}", [128, 2, 256], BF16)
    expb = _sb(ctx, f"expb_{tag}", [128, 2, 256], BF16)
    cfm0 = _sb(ctx, f"cfm0_{tag}", [128, 256], BF16)
    acr = ctx.nc.alloc_sbuf_tensor_at(f"acr_{tag}", [1, 1024], F32, offset=ctx.lnst_off)
    strow = ctx.nc.alloc_sbuf_tensor_at(f"strow_{tag}", [1, 512], F32,
                                         offset=ctx.lnst_off + 4096)
    mix_end = ctx.sb_off

    P.op("DVE", lambda v, o=Sst.ap(): v.memzero(o), reads=(), writes=((tag, "S"),))
    P.op("DVE", lambda v, o=Sbf.ap(): v.memzero(o), reads=(), writes=((tag, "Sb"),))
    for ct in range(NCT):
        P.op("DVE", lambda v, o=xbcp.ap()[:, ct, 0:4]: v.memzero(o),
             reads=(), writes=((tag, "xbcp", ct),))

    ap0 = dttm.ap().ap[0]

    def drep(t, col, per_head, nh):
        return bass.AP(t.ap().tensor, col, [ap0, [1, nh], [0, per_head]])

    def ld_xg(g):
        rank, j = g // 4, g % 4
        dma(ctx, "SP", xg.ap(),
            S[f"xcn_g{j}"].ap().rearrange("(r kt p) t -> r p kt t", r=2, p=128)[rank],
            reads=(("dram", "xcn_g", j),), writes=((tag, "xg"),), label="ld_xg")
    ld_xg(0)
    for g in range(ctx.n_mix_sc):
        if g > 0:
            for ct in range(NCT):
                P.op("DVE", lambda v, o=xbcp.ap()[:, ct, 0:3], i=xbcp.ap()[:, ct, 512:515]:
                     v.tensor_copy(o, i),
                     reads=((tag, "xbcp", ct),), writes=((tag, "xbcp", ct),), label="carry")
        for mt in range(NMT):
            w = mt % 2
            dma(ctx, "SP", wib.ap()[:, w, :, :], S[pre + "wi"].ap()[mt],
                reads=(("dram", pre + "wi"),), writes=((tag, "wib", w),), label="ld_wi")
            ps = PS[w].ap()[:, 0:512]
            for k in range(NKT):
                mm_op(ctx, ps, wib.ap()[:, w, k, :], xg.ap()[:, k, :],
                      reads=((tag, "wib", w), (tag, "xg")), pwrite=("ps", w),
                      start=(k == 0), stop=(k == NKT - 1), label="mmip")
            brd = ((tag, "ipb"),) if cfg.has_bias else ()
            bap = T["bias_m"].ap()[:, mt:mt + 1] if cfg.has_bias else None
            if mt < 16:
                def em_z(a, o=sz.ap()[:, mt, :], i=ps, ba=bap):
                    return (a.activation(o, i, AF.Silu) if ba is None
                            else a.activation(o, i, AF.Silu, bias=ba))
                P.op("ACT", em_z, reads=(("ps", w),) + brd, writes=((tag, "sz", mt),),
                     label="evz")
            elif mt < 16 + NCT:
                ct = mt - 16
                def em_x(a, o=xbcp.ap()[:, ct, 3:515], i=ps, ba=bap):
                    return (a.copy(o, i) if ba is None
                            else a.activation(o, i, AF.Identity, bias=ba))
                P.op("ACT", em_x, reads=(("ps", w),) + brd, writes=((tag, "xbcp", ct),),
                     label="evx")
            else:
                if cfg.has_bias:
                    P.op("ACT", lambda a, o=dtfm.ap()[0:H2, :], i=ps[0:H2, :], ba=bap:
                         a.activation(o, i, AF.Identity, bias=ba[0:H2, :]),
                         reads=(("ps", w),) + brd, writes=((tag, "dtfm"),))
                    src, rsrc = dtfm.ap()[0:H2, :], ((tag, "dtfm"),)
                else:
                    src, rsrc = ps[0:H2, :], (("ps", w),)
                dtb = pcol.ap()[0:H2, 0:1]
                t1 = abfm.ap()[0:H2, :]
                P.op("ACT", lambda a, o=t1, i=src, bb=dtb: a.activation(o, i, AF.Abs, bias=bb),
                     reads=rsrc + ((tag, "dtb"),), writes=((tag, "ab"),))
                P.op("ACT", lambda a, o=t1, i=t1, n1=T["cconst"].ap()[0:H2, 1:2]:
                     a.activation(o, i, AF.Exp, scale=n1),
                     reads=((tag, "ab"), ("cconst",)), writes=((tag, "ab"),))
                P.op("ACT", lambda a, o=t1, i=t1: a.activation(o, i, AF.Ln, bias=1.0),
                     reads=((tag, "ab"),), writes=((tag, "ab"),))
                P.op("ACT", lambda a, o=dtfm.ap()[0:H2, :], i=src, bb=dtb:
                     a.activation(o, i, AF.Relu, bias=bb),
                     reads=rsrc + ((tag, "dtb"),), writes=((tag, "dtfm"),))
                P.op("DVE", lambda v, o=dtfm.ap()[0:H2, :], a_=dtfm.ap()[0:H2, :], b_=t1:
                     v.tensor_tensor(o, a_, b_, ALU.add),
                     reads=((tag, "dtfm"), (tag, "ab")), writes=((tag, "dtfm"),))
        P.op("DVE", lambda v, o=abfm.ap()[0:H2, :], i=dtfm.ap()[0:H2, :],
             s=pcol.ap()[0:H2, 2:3]: v.tensor_scalar_mul(o, i, s),
             reads=((tag, "dtfm"), (tag, "A")), writes=((tag, "ab"),), label="abar")
        if g + 1 < ctx.n_mix_sc:
            ld_xg(g + 1)
        if ctx.dbg and g == 0:
            tap(ctx, "xg", xg.ap(), ((tag, "xg"),))
            tap(ctx, "sz", sz.ap(), tuple((tag, "sz", k) for k in range(16)))
            tap(ctx, "xbcp", xbcp.ap(), tuple((tag, "xbcp", c) for c in range(NCT)))
            tap(ctx, "dtfm", dtfm.ap(), ((tag, "dtfm"),))
            tap(ctx, "abfm", abfm.ap(), ((tag, "ab"),))

        for cc in range(2):
            ccols = slice(cc * 256, (cc + 1) * 256)
            for ct in range(NCT):
                w = ct % 2
                if cc == 0 and True:
                    pass
                dma(ctx, "ACT", cvb.ap()[:, w, :, :], S[pre + "cv"].ap()[ct],
                    reads=(("dram", pre + "cv"),), writes=((tag, "cvb", w),), label="ld_cv")
                ps = PS[w].ap()[:, 0:256]
                for wp in range(4):
                    mm_op(ctx, ps, cvb.ap()[:, w, wp, :],
                          xbcp.ap()[:, ct, cc * 256 + wp:cc * 256 + wp + 256],
                          reads=((tag, "cvb", w), (tag, "xbcp", ct)), pwrite=("ps", w),
                          start=(wp == 0), stop=(wp == 3), label="mmcv")
                P.op("ACT", lambda a, o=xbc.ap()[:, ct, :], i=ps,
                     bb=T["cbias"].ap()[:, ct:ct + 1]: a.activation(o, i, AF.Silu, bias=bb),
                     reads=(("ps", w), (tag, "cb")), writes=((tag, "xbc", ct),), label="evcv")
            if ctx.dbg and g == 0 and cc == 0:
                tap(ctx, "xbc", xbc.ap(), tuple((tag, "xbc", c) for c in range(NCT)))
            _o1 = T["onescf"].ap()[0:H2, 0:1]
            ones_rep = bass.AP(_o1.tensor, _o1.offset, [_o1.ap[0], [0, 256]])
            P.op("DVE", lambda v, o=acfm.ap()[0:H2, :], d0=ones_rep,
                 d1=abfm.ap()[0:H2, ccols]:
                 v.tensor_tensor_scan(o, d0, d1, 0.0, ALU.mult, ALU.add),
                 reads=((tag, "ab"), ("onescf",)), writes=((tag, "ac"),), label="scan")
            P.op("ACT", lambda a, o=dsfm.ap()[0:H2, :], i=acfm.ap()[0:H2, :],
                 bb=acfm.ap()[0:H2, 255:256], n1=T["cconst"].ap()[0:H2, 1:2]:
                 a.activation(o, i, AF.Exp, scale=n1, bias=bb),
                 reads=((tag, "ac"), ("cconst",)), writes=((tag, "ds"),), label="dsfm")
            for tb in range(2):
                c0 = cc * 256 + tb * 128
                for j, (srct, srck) in enumerate(
                        ((dtfm, (tag, "dtfm")), (acfm, (tag, "ac")), (dsfm, (tag, "ds")))):
                    off = 0 if srct is dtfm else None
                    sap = (srct.ap()[0:H2, c0:c0 + 128] if srct is dtfm
                           else srct.ap()[0:H2, tb * 128:tb * 128 + 128])
                    slot = (tb * 3 + j) * 64
                    P.op("PE", lambda t, o=PS[2].ap()[0:128, slot:slot + H2], i=sap,
                         id_=T["identf"].ap()[0:H2, 0:H2]: t.transpose(o, i, id_),
                         reads=(srck, ("identf",)), writes=(("ps", 2),), label="tpdt")
            for tb in range(2):
                for j, (dstt, dstk, neg) in enumerate(
                        ((dttm, (tag, "dttm"), False), (natm, (tag, "natm"), True),
                         (dstm, (tag, "dstm"), False))):
                    slot = (tb * 3 + j) * 64
                    if neg:
                        P.op("DVE", lambda v, o=dstt.ap()[:, tb * 32:tb * 32 + H2],
                             i=PS[2].ap()[:, slot:slot + H2]: v.tensor_scalar_mul(o, i, -1.0),
                             reads=(("ps", 2),), writes=(dstk,), label="evtp")
                    else:
                        P.op("DVE", lambda v, o=dstt.ap()[:, tb * 32:tb * 32 + H2],
                             i=PS[2].ap()[:, slot:slot + H2]: v.tensor_copy(o, i),
                             reads=(("ps", 2),), writes=(dstk,), label="evtp")
            mm_op(ctx, PS[2].ap()[0:1, 384:384 + H2], T["e127"].ap(),
                  natm.ap()[:, 32:32 + H2], reads=(("e127",), (tag, "natm")),
                  pwrite=("ps", 2), start=True, stop=True, label="e127mm")
            P.op("ACT", lambda a, o=erow.ap()[0:1, 0:H2], i=PS[2].ap()[0:1, 384:384 + H2],
                 n1=T["cconst"].ap()[0:1, 1:2]:
                 a.activation(o, i, AF.Exp, scale=n1),
                 reads=(("ps", 2), ("cconst",)), writes=((tag, "erow"),))
            mm_op(ctx, PS[2].ap()[:, 448:448 + H2], T["ones1f"].ap(), erow.ap()[0:1, 0:H2],
                  reads=(("ones1f",), (tag, "erow")), pwrite=("ps", 2),
                  start=True, stop=True, label="ebcmm")
            P.op("DVE", lambda v, o=ebc.ap()[:, 0:H2], i=PS[2].ap()[:, 448:448 + H2]:
                 v.tensor_copy(o, i), reads=(("ps", 2),), writes=((tag, "ebc"),))
            if ctx.dbg and g == 0 and cc == 0:
                tap(ctx, "acfm", acfm.ap(), ((tag, "ac"),))
                tap(ctx, "dsfm", dsfm.ap(), ((tag, "ds"),))
                tap(ctx, "dttm", dttm.ap(), ((tag, "dttm"),))
                tap(ctx, "natm", natm.ap(), ((tag, "natm"),))
                tap(ctx, "dstm", dstm.ap(), ((tag, "dstm"),))
                tap(ctx, "ebc", ebc.ap(), ((tag, "ebc"),))
            bfm_base = 0 if NST == 128 else 0
            B_ct = 16
            for tb in range(2):
                tp = PS["t3" if tb == 0 else "t4"].ap()
                P.op("PE", lambda t, o=tp[:, 0:NST],
                     i=xbc.ap()[0:NST, B_ct, tb * 128:tb * 128 + 128],
                     id_=T["identb"].ap()[0:NST, 0:NST]: t.transpose(o, i, id_),
                     reads=((tag, "xbc", B_ct), ("identb",)), writes=(("ps", 3 if tb == 0 else 4),),
                     label="tpB")
                P.op("DVE", lambda v, o=btok.ap()[:, tb, 0:NST], i=tp[:, 0:NST]:
                     v.tensor_copy(o, i),
                     reads=(("ps", 3 if tb == 0 else 4),), writes=((tag, "btok"),),
                     label="evB")
            nrep = 2 if PP == 64 else 1
            for ct in range(16):
                for tb in range(2):
                    pk = "t3" if (ct * 2 + tb) % 2 == 0 else "t4"
                    tp = PS[pk].ap()
                    P.op("PE", lambda t, o=tp[:, 0:128],
                         i=xbc.ap()[:, ct, tb * 128:tb * 128 + 128],
                         id_=T["identb"].ap(): t.transpose(o, i, id_),
                         reads=((tag, "xbc", ct), ("identb",)), writes=(("ps", 3 if pk == "t3" else 4),),
                         label="tpx")
                    dcol = tb * 32 + (ct * nrep if PP == 64 else ct)
                    dr = drep(dttm, dcol, PP, nrep) if PP == 64 else \
                        bass.AP(dttm.ap().tensor, tb * 32 + ct, [ap0, [1, 1], [0, 128]])
                    P.op("DVE", lambda v, o=xdt.ap()[:, tb, ct * 128:(ct + 1) * 128],
                         a_=tp[:, 0:128], b_=dr: v.tensor_tensor(o, a_, b_, ALU.mult),
                         reads=(("ps", 3 if pk == "t3" else 4), (tag, "dttm")), writes=((tag, "xdt", tb),),
                         label="evx2")
            cfm_lo = 0
            C_ct = 16 if NST == 64 else 17
            Bs = (xbc.ap()[0:64, B_ct, :] if NST == 64 else xbc.ap()[:, B_ct, :])
            if NST == 64:
                P.op("DVE", lambda v, o=cfm0.ap()[0:64, :], i=xbc.ap()[64:128, C_ct, :]:
                     v.tensor_copy(o, i),
                     reads=((tag, "xbc", C_ct),), writes=((tag, "cfm0"),), label="cp_c")
                Cs = cfm0.ap()[0:64, :]
                CKEY = (tag, "cfm0")
            else:
                Cs = xbc.ap()[:, C_ct, :]
                CKEY = (tag, "xbc", C_ct)
            mm_op(ctx, PS[5].ap()[:, 0:256], Bs[:, 0:128], Cs,
                  reads=((tag, "xbc", B_ct), CKEY), pwrite=("ps", 5),
                  start=True, stop=True, label="mmG0")
            mm_op(ctx, PS[5].ap()[:, 256:384], Bs[:, 128:256], Cs[:, 128:256],
                  reads=((tag, "xbc", B_ct), CKEY), pwrite=("ps", 5),
                  start=True, stop=True, label="mmG1")
            P.op("ACT", lambda a, o=gsb.ap(), i=PS[5].ap()[:, 0:384]: a.copy(o, i),
                 reads=(("ps", 5),), writes=((tag, "gsb"),), label="evG")
            if ctx.dbg and g == 0 and cc == 0:
                tap(ctx, "xdt", xdt.ap(), ((tag, "xdt", 0), (tag, "xdt", 1)))
                tap(ctx, "gsb", gsb.ap(), ((tag, "gsb"),))
                tap(ctx, "btok", btok.ap(), ((tag, "btok"),))
            def prep(h):
                hb = h % 2
                qb = (h // 4) % 2
                pbA, pbB = ((2, 7) if qb == 0 else (3, 4))
                if h % 4 == 0:
                    dma(ctx, "ACT", acr.ap().rearrange("o (hh t) -> o hh t", hh=4),
                        acfm.ap()[h:h + 4, 0:256],
                        reads=((tag, "ac"),), writes=((tag, "acr", qb),), label="acr")
                    mm_op(ctx, PS[pbA].ap()[:, 0:512], T["ones1f"].ap(),
                          acr.ap()[0:1, 0:512],
                          reads=(("ones1f",), (tag, "acr", qb)), pwrite=("ps", pbA),
                          start=True, stop=True, label="bAc")
                    mm_op(ctx, PS[pbB].ap()[:, 0:512], T["ones1f"].ap(),
                          acr.ap()[0:1, 512:1024],
                          reads=(("ones1f",), (tag, "acr", qb)), pwrite=("ps", pbB),
                          start=True, stop=True, label="bAc")
                pb = pbA if h % 4 < 2 else pbB
                c0 = (h % 2) * 256
                bAc = PS[pb].ap()[:, c0:c0 + 256]
                P.op("DVE", lambda v, o=seg.ap()[:, hb, 0:256], i=bAc,
                     s=natm.ap()[:, h:h + 1], m_=T["maskT"].ap()[:, 0:256]:
                     v.scalar_tensor_tensor(o, i, s, m_, ALU.add, ALU.add),
                     reads=(("ps", pb), (tag, "natm"), ("maskT",)),
                     writes=((tag, "seg", hb),), label="seg0")
                P.op("DVE", lambda v, o=seg.ap()[:, hb, 256:384], i=bAc[:, 128:256],
                     s=natm.ap()[:, 32 + h:33 + h], m_=T["maskT"].ap()[:, 0:128]:
                     v.scalar_tensor_tensor(o, i, s, m_, ALU.add, ALU.add),
                     reads=(("ps", pb), (tag, "natm"), ("maskT",)),
                     writes=((tag, "seg", hb),), label="seg1")
                P.op("ACT", lambda a, o=Lb.ap()[:, hb, :], i=seg.ap()[:, hb, :]:
                     a.activation(o, i, AF.Exp),
                     reads=((tag, "seg", hb),), writes=((tag, "L", hb),), label="Lexp")
                P.op("DVE", lambda v, o=glb.ap()[:, hb, :], a_=gsb.ap(), b_=Lb.ap()[:, hb, :]:
                     v.tensor_tensor(o, a_, b_, ALU.mult),
                     reads=((tag, "gsb"), (tag, "L", hb)), writes=((tag, "gl", hb),),
                     label="GL")
                erows = slice(0, NST)
                P.op("ACT", lambda a, o=expb.ap()[erows, hb, :], i=bAc[erows, :]:
                     a.activation(o, i, AF.Exp),
                     reads=(("ps", pb),), writes=((tag, "expb", hb),), label="expbAc")
                P.op("POOL", lambda v, o=cea.ap()[0:NST, hb, :], a_=Cs, b_=expb.ap()[erows, hb, :]:
                     v.tensor_tensor(o, a_, b_, ALU.mult),
                     reads=(CKEY, (tag, "expb", hb)), writes=((tag, "cea", hb),),
                     label="cea")
                if ctx.dbg and g == 0 and cc == 0 and h == 0:
                    tap(ctx, "seg", seg.ap(), ((tag, "seg", 0),))
                    tap(ctx, "Lb", Lb.ap(), ((tag, "L", 0),))
                    tap(ctx, "glb", glb.ap()[:, 0, :], ((tag, "gl", 0),))
                    tap(ctx, "cea", cea.ap()[:, 0, :], ((tag, "cea", 0),))

            def comp(h):
                hb = h % 2
                if PP == 64:
                    pY = 5 + ((h // 2) % 2)
                    r0 = (h % 2) * 64
                else:
                    pY = 5 + (h % 2)
                    r0 = 0
                Yp = PS[pY].ap()
                mm_op(ctx, Yp[r0:r0 + PP, 0:256], xdt.ap()[:, 0, h * PP:(h + 1) * PP],
                      glb.ap()[:, hb, 0:256],
                      reads=(((tag, "xdt", 0)), (tag, "gl", hb)), pwrite=("ps", pY),
                      start=True, stop=False, label="mmY0")
                mm_op(ctx, Yp[r0:r0 + PP, 128:256], xdt.ap()[:, 1, h * PP:(h + 1) * PP],
                      glb.ap()[:, hb, 256:384],
                      reads=(((tag, "xdt", 1)), (tag, "gl", hb)), pwrite=("ps", pY),
                      start=False, stop=False, label="mmY1")
                mm_op(ctx, Yp[r0:r0 + PP, 0:256], Sbf.ap()[0:NST, h * PP:(h + 1) * PP],
                      cea.ap()[0:NST, hb, :],
                      reads=(((tag, "Sb")), (tag, "cea", hb)), pwrite=("ps", pY),
                      start=False, stop=True, label="mmY2")
                if (PP == 64 and h % 2 == 1) or PP == 128:
                    ct = h // 2 if PP == 64 else h
                    P.op("DVE", lambda v, o=ysb.ap()[:, ct, :], i=xbc.ap()[:, ct, :],
                         s=T["dex"].ap()[:, ct:ct + 1], Y=Yp[:, 0:256]:
                         v.scalar_tensor_tensor(o, i, s, Y, ALU.mult, ALU.add),
                         reads=((tag, "xbc", ct), (tag, "dex"), ("ps", pY)),
                         writes=((tag, "y", ct),), label="evY")

            for h in range(H2 + 1):
                if h < H2:
                    prep(h)
                if h > 0:
                    comp(h - 1)
            nh_rep = H2 // 2 if PP == 64 else H2
            for tb in range(2):
                dr = bass.AP(dstm.ap().tensor, tb * 32, [ap0, [1, H2], [0, PP]])
                P.op("DVE", lambda v, o=xdt.ap()[:, tb, :], a_=xdt.ap()[:, tb, :], b_=dr:
                     v.tensor_tensor(o, a_, b_, ALU.mult),
                     reads=((tag, "xdt", tb), (tag, "dstm")), writes=((tag, "xdt", tb),),
                     label="xdd")
            for qq in range(4):
                pM = 5 + (qq % 2)
                qs = slice(qq * 512, (qq + 1) * 512)
                for tb in range(2):
                    mm_op(ctx, PS[pM].ap()[0:NST, 0:512], btok.ap()[:, tb, 0:NST],
                          xdt.ap()[:, tb, qs],
                          reads=((tag, "btok"), (tag, "xdt", tb)), pwrite=("ps", pM),
                          start=(tb == 0), stop=(tb == 1), label="mmM1")
                hlo = qq * (512 // PP)
                _e0 = ebc.ap()[0:NST, 0:1]
                er = bass.AP(_e0.tensor, _e0.offset + hlo, [_e0.ap[0], [1, 512 // PP], [0, PP]])
                P.op("DVE", lambda v, o=Sst.ap()[0:NST, qs], a_=Sst.ap()[0:NST, qs], b_=er:
                     v.tensor_tensor(o, a_, b_, ALU.mult),
                     reads=((tag, "S"), (tag, "ebc")), writes=((tag, "S"),), label="Sdec")
                P.op("DVE", lambda v, o=Sst.ap()[0:NST, qs], a_=Sst.ap()[0:NST, qs],
                     b_=PS[pM].ap()[0:NST, 0:512]: v.tensor_tensor(o, a_, b_, ALU.add),
                     reads=((tag, "S"), ("ps", pM)), writes=((tag, "S"),), label="Sadd")
            P.op("DVE", lambda v, o=Sbf.ap()[0:NST, :], i=Sst.ap()[0:NST, :]:
                 v.tensor_copy(o, i), reads=((tag, "S"),), writes=((tag, "Sb"),),
                 label="Sbf")
            if ctx.dbg and g == 0:
                tap(ctx, "S%d" % cc, Sst.ap(), ((tag, "S"),))
            for kt in range(16):
                P.op("DVE", lambda v, o=ysb.ap()[:, kt, :], a_=ysb.ap()[:, kt, :],
                     b_=sz.ap()[:, kt, ccols]: v.tensor_tensor(o, a_, b_, ALU.mult),
                     reads=((tag, "y", kt), (tag, "sz", kt)), writes=((tag, "y", kt),),
                     label="gate")
            sqb = ln_t[0]
            for kt in range(16):
                b2 = kt % 2
                P.op("ACT", lambda a, o=sqb.ap()[:, 0, b2 * 256:b2 * 256 + 256],
                     i=ysb.ap()[:, kt, :]: a.activation(o, i, AF.Square),
                     reads=((tag, "y", kt),), writes=((tag, "ysq", b2),), label="ysq")
                mm_op(ctx, PS[7].ap()[0:1, 256:512], T["onescb"].ap(),
                      sqb.ap()[:, 0, b2 * 256:b2 * 256 + 256],
                      reads=(("onescb",), (tag, "ysq", b2)), pwrite=("ps", 7),
                      start=(kt == 0), stop=(kt == 15), label="mmst")
            tcol = g * SC + cc * 256
            P.op("ACT", lambda a, o=strow.ap()[0:1, 0:256], i=PS[7].ap()[0:1, 256:512]:
                 a.copy(o, i), reads=(("ps", 7),), writes=((tag, "strow"),), label="evst")
            dma(ctx, "SP", S["st_in"].ap()[0:1, tcol:tcol + 256],
                strow.ap()[0:1, 0:256],
                reads=((tag, "strow"),), writes=(("dram", "st_in"),), label="st_stats")
            qq, lc = tcol // 1024, tcol % 1024
            dma(ctx, "SP",
                S[f"y_in{qq}"].ap().rearrange("(kt p) t -> p kt t", p=128)[:, :, lc:lc + 256],
                ysb.ap(), reads=tuple(((tag, "y", kt)) for kt in range(16)),
                writes=(("dram", "y_in", qq),), label="st_y")
            if tcol + 256 == ctx.n_mix_sc * SC:
                P.op("POOL", lambda g_: g_.collective_compute(
                    "AllGather", ALU.bypass, replica_groups=RG,
                    ins=[S["st_in"].ap()], outs=[S["st_g"].ap()]),
                    reads=(("dram", "st_in"),), writes=(("dram", "st_g"),),
                    kind="cc", label="ag_st")
            if lc + 256 == 1024:
                P.op("POOL", lambda g_, j=qq: g_.collective_compute(
                    "AllGather", ALU.bypass, replica_groups=RG,
                    ins=[S[f"y_in{j}"].ap()], outs=[S["ygt"].ap()[j]]),
                    reads=(("dram", "y_in", qq),), writes=(("dram", "ygt", qq),),
                    kind="cc", label="ag_y")


    ctx.sb_off = xg_base
    yg = _sb(ctx, f"yg_{tag}", [128, 32, 512], BF16, at=xg_base)
    wob = _sb(ctx, f"wob_{tag}", [128, 2, 32, 128], BF16)
    rr = _sb(ctx, f"rr_{tag}", [1, 512], F32)
    rbc2 = _sb(ctx, f"rbc2_{tag}", [128, 512], BF16)
    st2 = _sb(ctx, f"st2_{tag}", [1, 2, 512], F32)
    ctx.sb_off = max(ctx.sb_off, mix_end)


    def em_qsetup(s):
        if not hasattr(ctx, "qreg"):
            pid = s.partition_id()
            reg = s.to_reg(pid)
            s.reg_mod(reg, reg, 2)
            ctx.qreg = reg
            ctx.qyreg = s.alloc_register("qy")
            ctx.qtreg = s.alloc_register("qt")
            s.reg_alu(ctx.qyreg, ctx.qreg, 4194304 * 2, ALU.mult)
            s.reg_alu(ctx.qtreg, ctx.qreg, TOK, ALU.mult)
        return s.nop()
    ctx.mixregs = []
    for m0 in range(2):
        dma(ctx, "SP", wob.ap()[:, m0, :, :], S[pre + "wo"].ap()[m0],
            reads=(("dram", pre + "wo"),), writes=((tag, "wob", m0),), label="ld_wo")
    P.op("SP", em_qsetup, reads=(), writes=((tag, "qreg"),), label="qsetup")

    for g in range(ctx.n_tok_sc):
        cols = slice(g * SC, (g + 1) * SC)

        def em_gofs(s, gg=g):
            r2 = s.alloc_register(f"gofs{gg}{tag}")
            s.reg_alu(r2, ctx.qtreg, gg * SC, ALU.add)
            ctx.gofs = RuntimeValue(r2)
            r5 = s.alloc_register(f"yh0_{gg}{tag}")
            r6 = s.alloc_register(f"yh1_{gg}{tag}")
            s.reg_alu(r5, ctx.qyreg, (gg // 2) * 4194304 + (gg % 2) * 512, ALU.add)
            s.reg_alu(r6, r5, 2097152, ALU.add)
            ctx.yhreg = (r5, r6)
            ctx.mixregs.extend([r2, r5, r6])
            return s.nop()
        P.op("SP", em_gofs, reads=((tag, "qreg"),), writes=((tag, "gofs"),), label="gofs")

        def em_ldst(s):
            src = bass.AP(S["st_g"].ap().tensor, ctx.gofs, [[4096, 2], [1, 512]])
            return s.dma_start(out=st2.ap(), in_=src)
        ctx.P.op("SP", em_ldst, reads=(("dram", "st_g"), (tag, "gofs")),
                 writes=((tag, "st2"),), kind="dma", label="ld_st")
        P.op("DVE", lambda v, o=rr.ap(), a_=st2.ap()[0:1, 0, :], b_=st2.ap()[0:1, 1, :]:
             v.tensor_tensor(o, a_, b_, ALU.add),
             reads=((tag, "st2"),), writes=((tag, "rr"),))
        P.op("DVE", lambda v, o=rr.ap(), i=rr.ap(): v.tensor_scalar_mul(o, i, 1.0 / DIN),
             reads=((tag, "rr"),), writes=((tag, "rr"),))
        P.op("ACT", lambda a, o=rr.ap(), i=rr.ap(), e=T["cconst"].ap()[0:1, 0:1]:
             a.activation(o, i, AF.Sqrt, bias=e),
             reads=((tag, "rr"), ("cconst",)), writes=((tag, "rr"),))
        P.op("DVE", lambda v, o=rr.ap(), i=rr.ap(): v.reciprocal(o, i),
             reads=((tag, "rr"),), writes=((tag, "rr"),))
        mm_op(ctx, PS[2].ap()[:, 0:512], T["ones1f"].ap(), rr.ap(),
              reads=(("ones1f",), (tag, "rr")), pwrite=("ps", 2), start=True, stop=True)
        P.op("ACT", lambda a, o=rbc2.ap(), i=PS[2].ap()[:, 0:512]: a.copy(o, i),
             reads=(("ps", 2),), writes=((tag, "rbc2"),))
        yq_reads = (("dram", "ygt", g // 2), ("dram", "ygt", 2 + g // 2), (tag, "gofs"))
        for half in range(2):
            ks = range(half * 16, (half + 1) * 16)

            def em_ldyg(s, hh=half):
                src = bass.AP(S["ygt"].ap().tensor, RuntimeValue(ctx.yhreg[hh]),
                              [[1024, 128], [131072, 16], [1, 512]])
                return s.dma_start(out=yg.ap()[:, hh * 16:(hh + 1) * 16, :], in_=src)
            ctx.P.op("SP", em_ldyg, reads=yq_reads,
                     writes=tuple((tag, "yg", k) for k in ks), kind="dma", label="ld_yg")
        for kt in range(32):
            P.op("DVE", lambda v, o=yg.ap()[:, kt, :], i=yg.ap()[:, kt, :],
                 s=T["nw"].ap()[:, kt:kt + 1]: v.tensor_scalar_mul(o, i, s),
                 reads=((tag, "yg", kt), (tag, "nw")), writes=((tag, "yg", kt),),
                 label="ynw")
            P.op("DVE", lambda v, o=yg.ap()[:, kt, :], a_=yg.ap()[:, kt, :], b_=rbc2.ap():
                 v.tensor_tensor(o, a_, b_, ALU.mult),
                 reads=((tag, "yg", kt), (tag, "rbc2")), writes=((tag, "yg", kt),),
                 label="yr")
        if ctx.dbg and g == 0:
            tap(ctx, "yg", yg.ap(), tuple((tag, "yg", k) for k in range(32)))
            tap(ctx, "rbc2", rbc2.ap(), ((tag, "rbc2"),))
        for m in range(16):
            w = m % 2
            if g > 0 and m < 2:
                dma(ctx, "SP", wob.ap()[:, w, :, :], S[pre + "wo"].ap()[m],
                    reads=(("dram", pre + "wo"),), writes=((tag, "wob", w),), label="ld_wo")
            ps = PS[w].ap()[:, 0:512]
            for k in range(32):
                mm_op(ctx, ps, wob.ap()[:, w, k, :], yg.ap()[:, k, :],
                      reads=((tag, "wob", w), (tag, "yg", k)), pwrite=("ps", w),
                      start=(k == 0), stop=(k == 31), label="mmop")
            if m + 2 < 16:
                dma(ctx, "SP", wob.ap()[:, (m + 2) % 2, :, :], S[pre + "wo"].ap()[m + 2],
                    reads=(("dram", pre + "wo"),), writes=((tag, "wob", (m + 2) % 2),),
                    label="ld_wo")
            if cfg.has_bias:
                P.op("DVE", lambda v, o=T["R"].ap()[:, m, cols], i=ps,
                     s=T["bias_m"].ap()[:, 36 + m:37 + m], b_=T["R"].ap()[:, m, cols]:
                     v.scalar_tensor_tensor(o, i, s, b_, ALU.add, ALU.add),
                     reads=(("ps", w), (tag, "opb"), ("R", m, g)), writes=(("R", m, g),),
                     label="resop")
            else:
                P.op("DVE", lambda v, o=T["R"].ap()[:, m, cols], a_=ps,
                     b_=T["R"].ap()[:, m, cols]: v.tensor_tensor(o, a_, b_, ALU.add),
                     reads=(("ps", w), ("R", m, g)), writes=(("R", m, g),), label="resop")
    def em_freeregs(s):
        for r in ctx.mixregs:
            s.free_register(r)
        ctx.mixregs = []
        return s.nop()
    P.op("SP", em_freeregs, reads=(), writes=(), label="freeregs")



_CACHED = {}


def kernel(**inputs):
    in_maps = host_prepare(inputs)
    key = "full"
    if key not in _CACHED:
        _CACHED[key] = build_program()
    nc, ctx = _CACHED[key]
    res = run_bass_kernel_spmd(nc, in_maps, core_ids=list(range(8)))
    out = np.zeros((4, SEQ, DM), np.float32)
    for c in range(8):
        b, q = c // 2, c % 2
        out[b, q * TOK:(q + 1) * TOK, :] = res.results[c]["yout"].T
    return out


# version 18
# speedup vs baseline: 1.2182x; 1.0439x over previous
import numpy as np

import concourse.bass as bass
import concourse.mybir as mybir
from concourse.aot_env import allow_ici_all_to_all
from concourse.expressions import RuntimeValue
from concourse.bass_utils import run_bass_kernel_spmd

F32 = mybir.dt.float32
BF16 = mybir.dt.bfloat16
AF = mybir.ActivationFunctionType
ALU = mybir.AluOpType

ENGINES = ("PE", "ACT", "DVE", "POOL", "SP")

DM = 2048
DIN = 4096
TOK = 2048
SEQ = 4096
SC = 512
NKT = DM // 128
NEG = -1.0e9
RG = [[0, 1], [2, 3], [4, 5], [6, 7]]


class Op:
    __slots__ = ("stream", "emit", "deps", "kind", "seq", "tag_idx", "needs_tag",
                 "snapshot", "label", "dma_idx")

    def __init__(self, stream, emit, deps, kind, seq, label=""):
        self.stream, self.emit, self.deps, self.kind, self.seq = stream, emit, deps, kind, seq
        self.tag_idx = None
        self.needs_tag = kind in ("dma", "cc")
        self.snapshot = None
        self.label = label


class _Buf:
    __slots__ = ("writer", "readers")

    def __init__(self):
        self.writer = None
        self.readers = []


class Prog:
    def __init__(self):
        self.ops = []
        self.streams = {e: [] for e in ENGINES}
        self.bufs = {}
        self.last_dma = {}
        self.last_cc = None
        self.dma_tmp_counts = {}

    def _buf(self, key):
        b = self.bufs.get(key)
        if b is None:
            b = _Buf()
            self.bufs[key] = b
        return b

    def op(self, stream, emit, reads=(), writes=(), kind="compute", label=""):
        deps = set()
        for k in reads:
            b = self._buf(k)
            if b.writer is not None:
                deps.add(b.writer)
        for k in writes:
            b = self._buf(k)
            if b.writer is not None:
                deps.add(b.writer)
            deps.update(b.readers)
        o = Op(stream, emit, deps, kind, len(self.ops), label)
        deps.discard(o)
        self.ops.append(o)
        self.streams[stream].append(o)
        if kind == "dma":
            o.dma_idx = self.dma_tmp_counts.get(stream, 0)
            self.dma_tmp_counts[stream] = o.dma_idx + 1
            self.last_dma[(stream, o.dma_idx % Prog.NSUB)] = o
        elif kind == "cc":
            self.last_cc = o
        for k in reads:
            self._buf(k).readers.append(o)
        for k in writes:
            b = self._buf(k)
            b.writer = o
            b.readers = []
        return o

    def barrier(self):
        lasts = [self.streams[e][-1] for e in ENGINES if self.streams[e]]
        lasts += list(self.last_dma.values())
        if self.last_cc is not None:
            lasts.append(self.last_cc)
        for e in ENGINES:
            o = self.op(e, lambda eng: eng.nop(), kind="compute", label="barrier")
            o.deps.update(d for d in lasts
                          if d.stream != e or d.kind != "compute")

    NSUB = 8

    def plan(self):
        for o in self.ops:
            for d in o.deps:
                if d.stream != o.stream or d.kind != "compute":
                    d.needs_tag = True
        counters = {}
        dma_counts = {}

        def sem_of(o):
            if o.kind == "dma":
                return ("dma", o.stream, o.dma_idx % Prog.NSUB)
            if o.kind == "cc":
                return ("cc",)
            return ("eng", o.stream)

        for o in self.ops:
            if not o.needs_tag:
                continue
            if o.kind == "dma":
                o.tag_idx = 16 * (o.dma_idx // Prog.NSUB + 1)
                continue
            s = sem_of(o)
            counters[s] = counters.get(s, 0) + 1
            o.tag_idx = counters[s]

        clocks = {e: {} for e in ENGINES}
        plans = {e: [] for e in ENGINES}
        for o in self.ops:
            ck = clocks[o.stream]
            wmap = {}
            if o.kind == "dma" and o.dma_idx >= Prog.NSUB:
                s = ("dma", o.stream, o.dma_idx % Prog.NSUB)
                v = 16 * (o.dma_idx // Prog.NSUB)
                if ck.get(s, 0) < v:
                    wmap[s] = v
                    ck[s] = v
            for d in sorted(o.deps, key=lambda x: x.seq):
                if d.stream == o.stream and d.kind == "compute":
                    continue
                s = sem_of(d)
                if ck.get(s, 0) < d.tag_idx:
                    wmap[s] = max(wmap.get(s, 0), d.tag_idx)
                    ck[s] = d.tag_idx
                    if d.snapshot:
                        for ks, v in d.snapshot.items():
                            if ck.get(ks, 0) < v:
                                ck[ks] = v
            plans[o.stream].append((sorted(wmap.items(), key=lambda kv: str(kv[0])), o))
            if o.needs_tag:
                snap = dict(ck)
                snap[sem_of(o)] = o.tag_idx
                o.snapshot = snap
        return plans


class Ctx:
    pass


def _sb(ctx, name, shape, dtype, at=None):
    per_part = int(np.prod(shape[1:])) * mybir.dt.size(dtype)
    per_part = (per_part + 31) & ~31
    if at is None:
        off = ctx.sb_off
        ctx.sb_off = off + per_part
    else:
        off = at
        ctx.sb_off = max(ctx.sb_off, off + per_part)
    assert ctx.sb_off <= ctx.sb_limit, (name, ctx.sb_off, ctx.sb_limit)
    return ctx.nc.alloc_sbuf_tensor_at(name, shape, dtype, offset=off)


def tap(ctx, name, ap, reads):
    if not ctx.dbg:
        return
    shp = list(ap.shape)
    t = ctx.nc.dram_tensor("tap_" + name, shp, ap.dtype, kind="ExternalOutput")
    dma(ctx, "ACT", t.ap(), ap, reads=reads, writes=(("tap", name),), label="tap")


def mm_op(ctx, psum_ap, lhsT_ap, rhs_ap, reads, pwrite, start, stop, label=""):
    ctx.P.op("PE",
             lambda t, o=psum_ap, l=lhsT_ap, r=rhs_ap, s1=start, s2=stop:
             t.matmul(o, l, r, start=s1, stop=s2),
             reads=reads, writes=(pwrite,), label=label)


def dma(ctx, stream, out_ap, in_ap, reads, writes, label=""):
    ctx.P.op(stream,
             lambda e, o=out_ap, i=in_ap: e.dma_start(out=o, in_=i),
             reads=reads, writes=writes, kind="dma", label=label)


def rep_ap(t, col_off, step, count, rep):
    base = t.ap()
    return bass.AP(base.tensor, col_off, [base.ap[0], [step, count], [0, rep]])



def _strips(W):
    R, K = W.shape
    m, kt = R // 128, K // 128
    w = W.reshape(m, 128, kt, 128)
    return np.ascontiguousarray(np.transpose(w, (0, 3, 2, 1)))


def _pad_rows(W, mult=128):
    r = W.shape[0]
    pr = (r + mult - 1) // mult * mult
    if pr == r:
        return np.ascontiguousarray(W)
    pad = np.zeros((pr - r,) + W.shape[1:], W.dtype)
    return np.concatenate([W, pad], axis=0)


def _pad_vec(v, n=128):
    out = np.zeros(n, np.float32)
    out[:len(v)] = v
    return out


def host_prepare(inputs):
    u = np.asarray(inputs["L0_attn_a_u"], dtype=np.float32)

    def mixer_pack(W_in, conv_w, conv_b, dt_bias, A_log, Dv, norm_w, W_out,
                   H, P, NST, q, ipb=None, opb=None):
        H2 = H // 2
        hsel = slice(q * H2, (q + 1) * H2)
        csel = slice(q * 2048, (q + 1) * 2048)
        z = W_in[0:DIN][csel]
        x = W_in[DIN:2 * DIN][csel]
        B = W_in[2 * DIN:2 * DIN + NST]
        C = W_in[2 * DIN + NST:2 * DIN + 2 * NST]
        dt = W_in[2 * DIN + 2 * NST:][hsel]
        Wrows = np.concatenate([z, x, B, C, _pad_rows(dt)], axis=0)
        wi = _strips(np.ascontiguousarray(Wrows))
        conv_rows = np.concatenate(
            [conv_w[0:DIN][csel], conv_w[DIN:DIN + NST], conv_w[DIN + NST:DIN + 2 * NST]],
            axis=0)
        cb_rows = np.concatenate(
            [conv_b[0:DIN][csel], conv_b[DIN:DIN + NST], conv_b[DIN + NST:DIN + 2 * NST]],
            axis=0)
        nct = conv_rows.shape[0] // 128
        cv = np.zeros((nct, 128, 4, 128), np.float32)
        for ct in range(nct):
            for w in range(4):
                np.fill_diagonal(cv[ct, :, w, :], conv_rows[ct * 128:(ct + 1) * 128, w])
        d = dict(
            wi=wi.astype(np.float32),
            cv=cv,
            cb=np.ascontiguousarray(cb_rows.reshape(nct, 128).T).astype(np.float32),
            dtb=_pad_vec(dt_bias[hsel]).reshape(128, 1),
            alog=_pad_vec(A_log[hsel]).reshape(128, 1),
            dex=np.ascontiguousarray(
                np.repeat(Dv[hsel], P).reshape(16, 128).T).astype(np.float32),
            nw=np.ascontiguousarray(norm_w.reshape(32, 128).T).astype(np.float32),
            wo=_strips(np.ascontiguousarray(W_out)).astype(np.float32),
        )
        if ipb is not None:
            brows = np.concatenate(
                [ipb[0:DIN][csel], ipb[DIN:2 * DIN][csel],
                 ipb[2 * DIN:2 * DIN + NST], ipb[2 * DIN + NST:2 * DIN + 2 * NST],
                 _pad_vec(ipb[2 * DIN + 2 * NST:][hsel])], axis=0)
            nmt = wi.shape[0]
            d["ipb"] = np.ascontiguousarray(brows.reshape(nmt, 128).T).astype(np.float32)
        if opb is not None:
            d["opb"] = np.ascontiguousarray(opb.reshape(16, 128).T).astype(np.float32)
        return d

    I = 8192
    perm = np.arange(I).reshape(I // 4, 4).T.reshape(-1)
    wg = _strips(np.ascontiguousarray(np.asarray(inputs["L0_ffn_w_gate"], np.float32).T[perm]))
    wu = _strips(np.ascontiguousarray(np.asarray(inputs["L0_ffn_w_up"], np.float32).T[perm]))
    wd = _strips(np.ascontiguousarray(np.asarray(inputs["L0_ffn_w_down"], np.float32)[perm].T))

    ident = np.eye(128, dtype=np.float32)
    ones1f = np.ones((1, 128), np.float32)
    onescol = np.ones((128, 1), np.float32)
    e127 = np.zeros((128, 1), np.float32)
    e127[127, 0] = 1.0
    cconst = np.zeros((128, 4), np.float32)
    cconst[:, 0] = 1e-5
    cconst[:, 1] = -1.0
    maskT = np.zeros((128, 256), np.float32)
    maskT[:, 0:128] = np.where(
        np.arange(128)[:, None] <= np.arange(128)[None, :], 0.0, NEG)

    in_maps = []
    for c in range(8):
        b, q = c // 2, c % 2
        m = {}
        m["u"] = np.ascontiguousarray(u[b, q * TOK:(q + 1) * TOK, :].T)
        da = mixer_pack(np.asarray(inputs["L0_attn_a_W_in"], np.float32),
                        np.asarray(inputs["L0_attn_a_conv_w"], np.float32),
                        np.asarray(inputs["L0_attn_a_conv_b"], np.float32),
                        np.asarray(inputs["L0_attn_a_dt_bias"], np.float32),
                        np.asarray(inputs["L0_attn_a_A_log"], np.float32),
                        np.asarray(inputs["L0_attn_a_D"], np.float32),
                        np.asarray(inputs["L0_attn_a_norm_w"], np.float32),
                        np.asarray(inputs["L0_attn_a_W_out"], np.float32),
                        64, 64, 64, q)
        for k, v in da.items():
            m["a_" + k] = v
        db = mixer_pack(np.asarray(inputs["L0_attn_b_in_proj_w"], np.float32),
                        np.asarray(inputs["L0_attn_b_conv_w"], np.float32)[:, 0, :],
                        np.asarray(inputs["L0_attn_b_conv_b"], np.float32),
                        np.asarray(inputs["L0_attn_b_dt_bias"], np.float32),
                        np.asarray(inputs["L0_attn_b_A_log"], np.float32),
                        np.asarray(inputs["L0_attn_b_D"], np.float32),
                        np.asarray(inputs["L0_attn_b_norm_w"], np.float32),
                        np.asarray(inputs["L0_attn_b_out_proj_w"], np.float32),
                        32, 128, 128, q,
                        ipb=np.asarray(inputs["L0_attn_b_in_proj_b"], np.float32),
                        opb=np.asarray(inputs["L0_attn_b_out_proj_b"], np.float32))
        for k, v in db.items():
            m["b_" + k] = v
        m["wg"], m["wu"], m["wd"] = wg, wu, wd
        m["ident"], m["ones1f"], m["onescol"] = ident, ones1f, onescol
        m["e127"], m["maskT"] = e127, maskT
        m["cconst"] = cconst
        in_maps.append(m)
    return in_maps



class MixCfg:
    def __init__(self, pre, H, P, NST, has_bias):
        self.pre, self.H, self.P, self.NST = pre, H, P, NST
        self.H2 = H // 2
        self.NCT = (2048 + 2 * NST) // 128
        self.NMT = 16 + self.NCT + 1
        self.has_bias = has_bias


CFG_A = MixCfg("a_", 64, 64, 64, False)
CFG_B = MixCfg("b_", 32, 128, 128, True)


def build_program(phases=("mixa", "ffn1", "mixb", "ffn2"), n_mix_sc=8, n_tok_sc=4,
                  dbg=False):
    nc = bass.Bass("TRN2", target_bir_lowering=False, debug=False)
    ctx = Ctx()
    ctx.nc, ctx.P = nc, Prog()
    ctx.phases, ctx.n_mix_sc, ctx.n_tok_sc = phases, n_mix_sc, n_tok_sc
    ctx.dbg = dbg
    ctx.tapn = 0

    D = {}
    ctx.D = D
    D["u"] = nc.dram_tensor("u", [DM, TOK], F32, kind="ExternalInput")
    for pre, cfg in (("a_", CFG_A), ("b_", CFG_B)):
        if ("mixa" if pre == "a_" else "mixb") not in phases:
            continue
        D[pre + "wi"] = nc.dram_tensor(pre + "wi", [cfg.NMT, 128, NKT, 128], F32, kind="ExternalInput")
        D[pre + "cv"] = nc.dram_tensor(pre + "cv", [cfg.NCT, 128, 4, 128], F32, kind="ExternalInput")
        D[pre + "cb"] = nc.dram_tensor(pre + "cb", [128, cfg.NCT], F32, kind="ExternalInput")
        D[pre + "dtb"] = nc.dram_tensor(pre + "dtb", [128, 1], F32, kind="ExternalInput")
        D[pre + "alog"] = nc.dram_tensor(pre + "alog", [128, 1], F32, kind="ExternalInput")
        D[pre + "dex"] = nc.dram_tensor(pre + "dex", [128, 16], F32, kind="ExternalInput")
        D[pre + "nw"] = nc.dram_tensor(pre + "nw", [128, 32], F32, kind="ExternalInput")
        D[pre + "wo"] = nc.dram_tensor(pre + "wo", [16, 128, 32, 128], F32, kind="ExternalInput")
    if "mixb" in phases:
        D["b_ipb"] = nc.dram_tensor("b_ipb", [128, CFG_B.NMT], F32, kind="ExternalInput")
        D["b_opb"] = nc.dram_tensor("b_opb", [128, 16], F32, kind="ExternalInput")
    if "ffn1" in phases or "ffn2" in phases:
        D["wg"] = nc.dram_tensor("wg", [64, 128, NKT, 128], F32, kind="ExternalInput")
        D["wu"] = nc.dram_tensor("wu", [64, 128, NKT, 128], F32, kind="ExternalInput")
        D["wd"] = nc.dram_tensor("wd", [16, 128, 64, 128], F32, kind="ExternalInput")
    for nm, shp in (("ident", [128, 128]), ("ones1f", [1, 128]), ("onescol", [128, 1]),
                    ("e127", [128, 1]), ("maskT", [128, 256]), ("cconst", [128, 4])):
        D[nm] = nc.dram_tensor(nm, shp, F32, kind="ExternalInput")
    D["yout"] = nc.dram_tensor("yout", [DM, TOK], F32, kind="ExternalOutput")

    S = {}
    ctx.S = S
    S["u"] = nc.dram_tensor("s_u", [DM, TOK], BF16)
    for pre, cfg in (("a_", CFG_A), ("b_", CFG_B)):
        if ("mixa" if pre == "a_" else "mixb") not in phases:
            continue
        S[pre + "wi"] = nc.dram_tensor("s_" + pre + "wi", [cfg.NMT, 128, NKT, 128], BF16)
        S[pre + "cv"] = nc.dram_tensor("s_" + pre + "cv", [cfg.NCT, 128, 4, 128], BF16)
        S[pre + "wo"] = nc.dram_tensor("s_" + pre + "wo", [16, 128, 32, 128], BF16)
    if "ffn1" in phases or "ffn2" in phases:
        S["wg"] = nc.dram_tensor("s_wg", [64, 128, NKT, 128], BF16)
        S["wu"] = nc.dram_tensor("s_wu", [64, 128, NKT, 128], BF16)
        S["wd"] = nc.dram_tensor("s_wd", [16, 128, 64, 128], BF16)
    if "mixa" in phases or "mixb" in phases:
        for j in range(4):
            S[f"xcn_in{j}"] = nc.dram_tensor(f"s_xcn_in{j}", [DM, SC], BF16)
            S[f"xcn_g{j}"] = nc.dram_tensor(f"s_xcn_g{j}", [2 * DM, SC], BF16)
            S[f"y_in{j}"] = nc.dram_tensor(f"s_y_in{j}", [2048, 1024], BF16)
        S["ygt"] = nc.dram_tensor("s_ygt", [4, 2 * 2048, 1024], BF16)
        S["st_in"] = nc.dram_tensor("s_st_in", [1, SEQ], F32)
        S["st_g"] = nc.dram_tensor("s_st_g", [2, SEQ], F32)

    ctx.sb_limit = nc.SBUF_PARTITION_SIZE_BYTES - 16 * 1024 - 640
    ctx.sb_off = 16 * 1024 + 640
    T = {}
    ctx.T = T
    T["R"] = _sb(ctx, "sb_R", [128, NKT, TOK], BF16)
    T["identf"] = _sb(ctx, "sb_identf", [128, 128], F32)
    T["identb"] = _sb(ctx, "sb_identb", [128, 128], BF16)
    T["ones1f"] = _sb(ctx, "sb_ones1f", [1, 128], F32)
    T["ones1b"] = _sb(ctx, "sb_ones1b", [1, 128], BF16)
    T["onescf"] = _sb(ctx, "sb_onescf", [128, 1], F32)
    T["onescb"] = _sb(ctx, "sb_onescb", [128, 1], BF16)
    T["cconst"] = _sb(ctx, "sb_cconst", [128, 4], F32)
    T["e127"] = _sb(ctx, "sb_e127", [128, 1], F32)
    T["maskT"] = _sb(ctx, "sb_maskT", [128, 256], F32)
    T["pcol"] = _sb(ctx, "sb_pcol", [128, 4], F32)
    T["dex"] = _sb(ctx, "sb_dex", [128, 16], F32)
    T["nw"] = _sb(ctx, "sb_nw", [128, 32], F32)
    T["cbias"] = _sb(ctx, "sb_cbias", [128, 20], F32)
    T["bias_m"] = _sb(ctx, "sb_bias_m", [128, 56], F32)
    ctx.sb_phase_base = ctx.sb_off

    PS = {}
    ctx.PS = PS
    for i in range(8):
        nc.psum_base = i
        PS[i] = nc.alloc_psum_tensor(f"pb{i}", [128, 512], F32)
    nc.psum_base = 3
    PS["t3"] = nc.alloc_psum_tensor("pbt3", [128, 512], BF16)
    nc.psum_base = 4
    PS["t4"] = nc.alloc_psum_tensor("pbt4", [128, 512], BF16)
    nc.psum_base = 8

    emit_init(ctx)
    for ph in phases:
        if ph == "mixa":
            emit_mixer(ctx, CFG_A)
        elif ph == "mixb":
            emit_mixer(ctx, CFG_B)
        elif ph in ("ffn1", "ffn2"):
            emit_ffn(ctx, ph)
        ctx.P.barrier()
    emit_out(ctx)
    ctx.P.barrier()

    plans = ctx.P.plan()

    import contextlib
    with contextlib.ExitStack() as _stk:
        s_pe = _stk.enter_context(nc.semaphore("sem_PE"))
        s_act = _stk.enter_context(nc.semaphore("sem_ACT"))
        s_dve = _stk.enter_context(nc.semaphore("sem_DVE"))
        s_pool = _stk.enter_context(nc.semaphore("sem_POOL"))
        s_sp = _stk.enter_context(nc.semaphore("sem_SP"))
        s_cc = _stk.enter_context(nc.semaphore("cc_sem"))
        dsems = {}
        for q in ("SP", "ACT", "POOL"):
            for i in range(Prog.NSUB):
                dsems[(q, i)] = _stk.enter_context(nc.semaphore(f"dma_{q}{i}"))
        block = _stk.enter_context(nc.Block())
        sems = {"PE": s_pe, "ACT": s_act, "DVE": s_dve, "POOL": s_pool, "SP": s_sp}

        def resolve(s):
            if s[0] == "dma":
                return dsems[(s[1], s[2])]
            if s[0] == "cc":
                return s_cc
            return sems[s[1]]

        def run(stream, eng):
            for waits, o in plans[stream]:
                for s, v in waits:
                    eng.wait_ge(resolve(s), v)
                inst = o.emit(eng)
                if o.needs_tag:
                    if o.kind == "dma":
                        inst.then_inc(dsems[(o.stream, o.dma_idx % Prog.NSUB)], 16)
                    elif o.kind == "cc":
                        inst.then_inc(s_cc, 1)
                    else:
                        inst.then_inc(sems[o.stream], 1)

        @block.tensor
        def _(t):
            run("PE", t)

        @block.scalar
        def _(a):
            run("ACT", a)

        @block.vector
        def _(v):
            run("DVE", v)

        @block.gpsimd
        def _(g):
            run("POOL", g)

        @block.sync
        def _(s):
            run("SP", s)

    return nc, ctx



def emit_init(ctx):
    P, D, S, T = ctx.P, ctx.D, ctx.S, ctx.T
    order = ["u"]
    for ph in ctx.phases:
        if ph == "mixa":
            order += ["a_wi", "a_cv", "a_wo"]
        elif ph == "mixb":
            order += ["b_wi", "b_cv", "b_wo"]
        elif ph == "ffn1" or (ph == "ffn2" and "ffn1" not in ctx.phases):
            order += ["wg", "wu", "wd"]
    for nm in order:
        dma(ctx, "POOL", S[nm].ap(), D[nm].ap(), reads=(),
            writes=(("dram", nm),), label=f"cast_{nm}")
    for nm, dn in (("identf", "ident"), ("ones1f", "ones1f"), ("onescf", "onescol"),
                   ("e127", "e127"), ("maskT", "maskT"), ("cconst", "cconst")):
        dma(ctx, "SP", T[nm].ap(), D[dn].ap(), (), ((nm,),), label="ldc")
    P.op("DVE", lambda v, o=T["identb"].ap(), i=T["identf"].ap(): v.tensor_copy(o, i),
         reads=(("identf",),), writes=(("identb",),))
    P.op("DVE", lambda v, o=T["ones1b"].ap(), i=T["ones1f"].ap(): v.tensor_copy(o, i),
         reads=(("ones1f",),), writes=(("ones1b",),))
    P.op("DVE", lambda v, o=T["onescb"].ap(), i=T["onescf"].ap(): v.tensor_copy(o, i),
         reads=(("onescf",),), writes=(("onescb",),))
    dma(ctx, "SP", T["R"].ap(), S["u"].ap().rearrange("(kt p) t -> p kt t", p=128),
        reads=(("dram", "u"),),
        writes=tuple(("R", kt, gg) for kt in range(NKT) for gg in range(4)), label="loadR")
    P.barrier()


def emit_out(ctx):
    P, D, T = ctx.P, ctx.D, ctx.T
    dma(ctx, "POOL", D["yout"].ap().rearrange("(kt p) t -> p kt t", p=128), T["R"].ap(),
        reads=tuple(("R", kt, gg) for kt in range(NKT) for gg in range(4)),
        writes=(("dram", "yout"),), label="store_out")



def emit_ln(ctx, cols, xcn_ap, xcn_key, psA, psB, tag, ln_t):
    P, T, PS = ctx.P, ctx.T, ctx.PS
    R = ctx.T["R"]
    gR = cols.start // SC
    sq, st, mb, rb = ln_t
    s1p = PS[psA].ap()[0:1, 0:512]
    s2p = PS[psB].ap()[0:1, 0:512]
    for kt in range(NKT):
        mm_op(ctx, s1p, T["onescb"].ap(), R.ap()[:, kt, cols],
              reads=(("onescb",), ("R", kt, gR)), pwrite=("ps", psA),
              start=(kt == 0), stop=(kt == NKT - 1), label="lns1")
    for kt in range(NKT):
        P.op("ACT", lambda a, o=sq.ap()[:, 0, :], i=R.ap()[:, kt, cols]:
             a.activation(o, i, AF.Square),
             reads=(("R", kt, gR),), writes=((tag, "lnsq", 0),), label="lnsq")
        mm_op(ctx, s2p, T["onescb"].ap(), sq.ap()[:, 0, :],
              reads=(("onescb",), (tag, "lnsq", 0)), pwrite=("ps", psB),
              start=(kt == 0), stop=(kt == NKT - 1), label="lns2")
    m_ = st.ap()[0:1, 0:512]
    t_ = st.ap()[0:1, 512:1024]
    v_ = st.ap()[0:1, 1024:1536]
    P.op("DVE", lambda v, o=m_, i=s1p: v.tensor_scalar_mul(o, i, 1.0 / DM),
         reads=(("ps", psA),), writes=((tag, "m"),))
    P.op("DVE", lambda v, o=t_, i=s2p: v.tensor_scalar_mul(o, i, 1.0 / DM),
         reads=(("ps", psB),), writes=((tag, "t"),))
    P.op("DVE", lambda v, o=v_, a=m_: v.scalar_tensor_tensor(o, a, -1.0, a, ALU.mult, ALU.mult),
         reads=((tag, "m"),), writes=((tag, "v"),))
    P.op("DVE", lambda v, o=v_, a=v_, b=t_: v.tensor_tensor(o, a, b, ALU.add),
         reads=((tag, "v"), (tag, "t")), writes=((tag, "v"),))
    P.op("ACT", lambda a, o=v_, i=v_, e=T["cconst"].ap()[0:1, 0:1]:
         a.activation(o, i, AF.Sqrt, bias=e),
         reads=((tag, "v"), ("cconst",)), writes=((tag, "v"),))
    P.op("DVE", lambda v, o=v_, i=v_: v.reciprocal(o, i),
         reads=((tag, "v"),), writes=((tag, "v"),))
    mm_op(ctx, PS[psA].ap()[:, 0:512], T["ones1f"].ap(), m_,
          reads=(("ones1f",), (tag, "m")), pwrite=("ps", psA), start=True, stop=True)
    mm_op(ctx, PS[psB].ap()[:, 0:512], T["ones1f"].ap(), v_,
          reads=(("ones1f",), (tag, "v")), pwrite=("ps", psB), start=True, stop=True)
    P.op("ACT", lambda a, o=mb.ap(), i=PS[psA].ap()[:, 0:512]: a.copy(o, i),
         reads=(("ps", psA),), writes=((tag, "mb"),))
    P.op("ACT", lambda a, o=rb.ap(), i=PS[psB].ap()[:, 0:512]: a.copy(o, i),
         reads=(("ps", psB),), writes=((tag, "rb"),))
    for kt in range(NKT):
        P.op("POOL", lambda v, o=xcn_ap[:, kt, :], a=R.ap()[:, kt, cols], b=mb.ap():
             v.tensor_tensor(o, a, b, ALU.subtract),
             reads=(("R", kt, gR), (tag, "mb")), writes=((xcn_key, kt),))
        P.op("POOL", lambda v, o=xcn_ap[:, kt, :], a=xcn_ap[:, kt, :], b=rb.ap():
             v.tensor_tensor(o, a, b, ALU.mult),
             reads=((xcn_key, kt), (tag, "rb")), writes=((xcn_key, kt),))


def _ln_alloc(ctx, tag):
    sq = _sb(ctx, f"lnsq_{tag}", [128, 1, 512], BF16)
    ctx.lnst_off = ctx.sb_off
    st = _sb(ctx, f"lnst_{tag}", [1, 3 * 512], F32)
    mb = _sb(ctx, f"lnmb_{tag}", [128, 512], BF16)
    rb = _sb(ctx, f"lnrb_{tag}", [128, 512], BF16)
    return sq, st, mb, rb



def emit_ffn(ctx, tag):
    P, S, T, PS = ctx.P, ctx.S, ctx.T, ctx.PS
    ctx.sb_off = ctx.sb_phase_base
    ln_t = _ln_alloc(ctx, tag)
    xcn = _sb(ctx, f"xcn_{tag}", [128, NKT, 512], BF16)
    h = _sb(ctx, f"h_{tag}", [128, 64, 512], BF16)
    gs = _sb(ctx, f"gs_{tag}", [128, 1, 512], BF16)
    wgb = _sb(ctx, f"wgb_{tag}", [128, 2, NKT, 128], BF16)
    wub = _sb(ctx, f"wub_{tag}", [128, 2, NKT, 128], BF16)
    spbase = ctx.sb_off
    aq = _sb(ctx, f"aq_{tag}", [128, 4, 512], BF16)
    bq = [_sb(ctx, f"b{i}_{tag}", [128, 512], BF16) for i in range(4)]
    gq = _sb(ctx, f"gq_{tag}", [128, 512], BF16)
    sp_end = ctx.sb_off
    wdb = _sb(ctx, f"wdb_{tag}", [128, 2, 32, 128], BF16, at=spbase)
    ctx.sb_off = max(ctx.sb_off, sp_end)
    SPK = tuple((tag, "sp", i) for i in range(6))

    emit_ln(ctx, slice(0, SC), xcn.ap(), (tag, "xcn"), 6, 7, tag, ln_t)

    def sparsify(g0):
            hk = lambda j: ((tag, "h", j * 16 + g0),)
            hk = lambda j: ((tag, "h", j * 16 + g0),)
            for j in range(4):
                P.op("ACT", lambda a, o=aq.ap()[:, j, :],
                     i=h.ap()[:, j * 16 + g0, :]:
                     a.activation(o, i, AF.Abs),
                     reads=hk(j), writes=(SPK[j],), label="spabs")
            aj = [aq.ap()[:, j, :] for j in range(4)]
            bb = [t.ap() for t in bq]
            gg = gq.ap()
            AQ = tuple(SPK[j] for j in range(4))
            BK = lambda i: (tag, "b", i)
            GK = ((tag, "gq"),)

            def tt(dst, dk, x, y, op, reads, label="sp"):
                P.op("DVE", lambda v, o=dst, a_=x, c=y, op_=op: v.tensor_tensor(o, a_, c, op_),
                     reads=reads, writes=(dk,), label=label)

            def cpy(dst, dk, src, reads):
                P.op("DVE", lambda v, o=dst, i=src: v.tensor_copy(o, i),
                     reads=reads, writes=(dk,), label="spc")

            tt(bb[0], BK(0), aj[0], aj[1], ALU.is_ge, AQ)
            cpy(bb[1], BK(1), bb[0], (BK(0),))
            tt(gg, GK[0], aj[0], aj[2], ALU.is_ge, AQ)
            tt(bb[0], BK(0), bb[0], gg, ALU.add, (BK(0), GK[0]))
            cpy(bb[2], BK(2), gg, (GK[0],))
            tt(gg, GK[0], aj[0], aj[3], ALU.is_ge, AQ)
            tt(bb[0], BK(0), bb[0], gg, ALU.add, (BK(0), GK[0]))
            cpy(bb[3], BK(3), gg, (GK[0],))
            tt(gg, GK[0], aj[1], aj[2], ALU.is_ge, AQ)
            tt(bb[1], BK(1), bb[1], gg, ALU.subtract, (BK(1), GK[0]))
            tt(bb[2], BK(2), bb[2], gg, ALU.add, (BK(2), GK[0]))
            tt(gg, GK[0], aj[1], aj[3], ALU.is_ge, AQ)
            tt(bb[1], BK(1), bb[1], gg, ALU.subtract, (BK(1), GK[0]))
            tt(bb[3], BK(3), bb[3], gg, ALU.add, (BK(3), GK[0]))
            tt(gg, GK[0], aj[2], aj[3], ALU.is_ge, AQ)
            tt(bb[2], BK(2), bb[2], gg, ALU.subtract, (BK(2), GK[0]))
            tt(bb[3], BK(3), bb[3], gg, ALU.add, (BK(3), GK[0]))
            thr = [(ALU.is_ge, 1.5), (ALU.is_le, -0.5), (ALU.is_le, 0.5), (ALU.is_le, 1.5)]
            for i in range(4):
                op_, cc_ = thr[i]
                P.op("DVE", lambda v, o=bb[i], i_=bb[i], c=cc_, op2=op_:
                     v.tensor_single_scalar(o, i_, c, op2),
                     reads=(BK(i),), writes=(BK(i),), label="spthr")
            for j in range(4):
                P.op("DVE", lambda v, o=h.ap()[:, j * 16 + g0, :],
                     a_=h.ap()[:, j * 16 + g0, :], c=bb[j]:
                     v.tensor_tensor(o, a_, c, ALU.mult),
                     reads=hk(j) + (BK(j),), writes=hk(j), label="spapply")

    for g in range(ctx.n_tok_sc):
        cols = slice(g * SC, (g + 1) * SC)
        for mi in range(64):
            gt, j = mi // 4, mi % 4
            m = j * 16 + gt
            w = mi % 2
            dma(ctx, "SP", wgb.ap()[:, w, :, :], S["wg"].ap()[m],
                reads=(("dram", "wg"),), writes=((tag, "wgb", w),), label="ldwg")
            dma(ctx, "SP", wub.ap()[:, w, :, :], S["wu"].ap()[m],
                reads=(("dram", "wu"),), writes=((tag, "wub", w),), label="ldwu")
            pg = PS[w].ap()[:, 0:512]
            pu = PS[2 + w].ap()[:, 0:512]
            for k in range(NKT):
                mm_op(ctx, pg, wgb.ap()[:, w, k, :], xcn.ap()[:, k, :],
                      reads=((tag, "wgb", w), ((tag, "xcn"), k)), pwrite=("ps", w),
                      start=(k == 0), stop=(k == NKT - 1), label="mmg")
            for k in range(NKT):
                mm_op(ctx, pu, wub.ap()[:, w, k, :], xcn.ap()[:, k, :],
                      reads=((tag, "wub", w), ((tag, "xcn"), k)), pwrite=("ps", 2 + w),
                      start=(k == 0), stop=(k == NKT - 1), label="mmu")
            P.op("ACT", lambda a, o=gs.ap()[:, 0, :], i=pg: a.activation(o, i, AF.Silu),
                 reads=(("ps", w),), writes=((tag, "gs", 0),), label="silug")
            P.op("DVE", lambda v, o=h.ap()[:, m, :], a=gs.ap()[:, 0, :], b=pu:
                 v.tensor_tensor(o, a, b, ALU.mult),
                 reads=((tag, "gs", 0), ("ps", 2 + w)), writes=((tag, "h", m),), label="hmul")
            if j == 3:
                sparsify(gt)
        if g + 1 < ctx.n_tok_sc:
            emit_ln(ctx, slice((g + 1) * SC, (g + 2) * SC), xcn.ap(), (tag, "xcn"),
                    6, 7, tag, ln_t)
        allsp = tuple(SPK) + tuple((tag, "b", i) for i in range(4)) + ((tag, "gq"),)
        for m in range(16):
            pd = PS[4 + (m % 2)].ap()[:, 0:512]
            for half in range(2):
                w = (2 * m + half) % 2
                dma(ctx, "SP", wdb.ap()[:, w, :, :], S["wd"].ap()[m][:, half * 32:(half + 1) * 32, :],
                    reads=(("dram", "wd"),) + (allsp if m == 0 else ()),
                    writes=((tag, "wdb", w),) + (allsp if m == 0 else ()),
                    label="ldwd")
                for kk in range(32):
                    k = half * 32 + kk
                    mm_op(ctx, pd, wdb.ap()[:, w, kk, :], h.ap()[:, k, :],
                          reads=((tag, "wdb", w), (tag, "h", k)), pwrite=("ps", 4 + (m % 2)),
                          start=(k == 0), stop=(k == 63), label="mmd")
            P.op("DVE", lambda v, o=T["R"].ap()[:, m, cols], a_=pd,
                 b_=T["R"].ap()[:, m, cols]: v.tensor_tensor(o, a_, b_, ALU.add),
                 reads=(("ps", 4 + (m % 2)), ("R", m, g)), writes=(("R", m, g),), label="resid")



def emit_mixer(ctx, cfg):
    P, D, S, T, PS, nc = ctx.P, ctx.D, ctx.S, ctx.T, ctx.PS, ctx.nc
    pre = cfg.pre
    tag = "mx" + pre[0]
    H2, PP, NST, NCT, NMT = cfg.H2, cfg.P, cfg.NST, cfg.NCT, cfg.NMT
    HPD = 2048
    ctx.sb_off = ctx.sb_phase_base
    ln_t = _ln_alloc(ctx, tag)

    pcol = T["pcol"]
    dma(ctx, "SP", pcol.ap()[:, 0:1], D[pre + "dtb"].ap(), (), ((tag, "dtb"),))
    dma(ctx, "SP", pcol.ap()[:, 1:2], D[pre + "alog"].ap(), (), ((tag, "alog"),))
    P.op("ACT", lambda a, o=pcol.ap()[:, 2:3], i=pcol.ap()[:, 1:2]:
         a.activation(o, i, AF.Exp), reads=((tag, "alog"),), writes=((tag, "A"),))
    P.op("DVE", lambda v, o=pcol.ap()[:, 2:3], i=pcol.ap()[:, 2:3]:
         v.tensor_scalar_mul(o, i, -1.0), reads=((tag, "A"),), writes=((tag, "A"),))
    dma(ctx, "SP", T["dex"].ap(), D[pre + "dex"].ap(), (), ((tag, "dex"),))
    dma(ctx, "SP", T["nw"].ap(), D[pre + "nw"].ap(), (), ((tag, "nw"),))
    dma(ctx, "SP", T["cbias"].ap()[:, 0:NCT],
        D[pre + "cb"].ap(), (), ((tag, "cb"),))
    if cfg.has_bias:
        dma(ctx, "SP", T["bias_m"].ap()[:, 0:NMT],
            D["b_ipb"].ap(), (), ((tag, "ipb"),))
        dma(ctx, "SP", T["bias_m"].ap()[:, 36:52],
            D["b_opb"].ap(), (), ((tag, "opb"),))

    xg_base = ctx.sb_off
    xcn = _sb(ctx, f"xcn_{tag}", [128, NKT, 512], BF16)
    for g in range(ctx.n_tok_sc):
        cols = slice(g * SC, (g + 1) * SC)
        emit_ln(ctx, cols, xcn.ap(), (tag, "xcn"), 5, 6, tag, ln_t)
        if ctx.dbg and g == 0:
            tap(ctx, "R0" + tag, T["R"].ap()[:, :, 0:512], tuple(("R", kt, 0) for kt in range(NKT)))
            tap(ctx, "xcn1" + tag, xcn.ap(), tuple(((tag, "xcn"), kt) for kt in range(NKT)))
        dma(ctx, "SP",
            S[f"xcn_in{g}"].ap().rearrange("(kt p) t -> p kt t", p=128),
            xcn.ap(), reads=tuple(((tag, "xcn"), kt) for kt in range(NKT)),
            writes=(("dram", "xcn_in", g),), label="st_xcn")
        P.op("POOL", lambda g_, j=g: g_.collective_compute(
            "AllGather", ALU.bypass, replica_groups=RG,
            ins=[S[f"xcn_in{j}"].ap()], outs=[S[f"xcn_g{j}"].ap()]),
            reads=(("dram", "xcn_in", g),), writes=(("dram", "xcn_g", g),), kind="cc",
            label="ag_xcn")

    ctx.sb_off = xg_base
    xg = _sb(ctx, f"xg_{tag}", [128, NKT, 512], BF16)
    wib = _sb(ctx, f"wib_{tag}", [128, 2, NKT, 128], BF16)
    cvb = _sb(ctx, f"cvb_{tag}", [128, 2, 4, 128], BF16)
    sz = _sb(ctx, f"sz_{tag}", [128, 16, 512], BF16)
    xbcp = _sb(ctx, f"xbcp_{tag}", [128, NCT, 516], BF16)
    xbc = _sb(ctx, f"xbc_{tag}", [128, NCT, 256], BF16)
    xdt = _sb(ctx, f"xdt_{tag}", [128, 2, HPD], BF16)
    ysb = _sb(ctx, f"ysb_{tag}", [128, 16, 256], BF16)
    Sst = _sb(ctx, f"Sst_{tag}", [128, HPD], F32)
    Sbf = _sb(ctx, f"Sbf_{tag}", [128, HPD], BF16)
    dtfm = _sb(ctx, f"dtfm_{tag}", [128, 512], F32)
    abfm = _sb(ctx, f"abfm_{tag}", [128, 512], F32)
    acfm = _sb(ctx, f"acfm_{tag}", [128, 256], F32)
    dsfm = _sb(ctx, f"dsfm_{tag}", [128, 256], F32)
    dttm = _sb(ctx, f"dttm_{tag}", [128, 64], F32)
    natm = _sb(ctx, f"natm_{tag}", [128, 64], F32)
    dstm = _sb(ctx, f"dstm_{tag}", [128, 64], F32)
    erow = _sb(ctx, f"erow_{tag}", [1, H2], F32)
    ebc = _sb(ctx, f"ebc_{tag}", [128, H2], F32)
    gsb = _sb(ctx, f"gsb_{tag}", [128, 384], BF16)
    btok = _sb(ctx, f"btok_{tag}", [128, 2, 128], BF16)
    seg = _sb(ctx, f"seg_{tag}", [128, 2, 384], F32)
    Lb = _sb(ctx, f"Lb_{tag}", [128, 2, 384], BF16)
    glb = _sb(ctx, f"glb_{tag}", [128, 3, 384], BF16)
    cea = _sb(ctx, f"cea_{tag}", [128, 3, 256], BF16)
    expb = _sb(ctx, f"expb_{tag}", [128, 2, 256], BF16)
    cfm0 = _sb(ctx, f"cfm0_{tag}", [128, 256], BF16) if NST == 64 else None
    acr = ctx.nc.alloc_sbuf_tensor_at(f"acr_{tag}", [1, 1024], F32, offset=ctx.lnst_off)
    strow = ctx.nc.alloc_sbuf_tensor_at(f"strow_{tag}", [1, 512], F32,
                                         offset=ctx.lnst_off + 4096)
    mix_end = ctx.sb_off

    P.op("DVE", lambda v, o=Sst.ap(): v.memzero(o), reads=(), writes=((tag, "S"),))
    P.op("DVE", lambda v, o=Sbf.ap(): v.memzero(o), reads=(), writes=((tag, "Sb"),))
    for ct in range(NCT):
        P.op("DVE", lambda v, o=xbcp.ap()[:, ct, 0:4]: v.memzero(o),
             reads=(), writes=((tag, "xbcp", ct),))

    ap0 = dttm.ap().ap[0]

    def drep(t, col, per_head, nh):
        return bass.AP(t.ap().tensor, col, [ap0, [1, nh], [0, per_head]])

    def ld_xg(g):
        rank, j = g // 4, g % 4
        dma(ctx, "SP", xg.ap(),
            S[f"xcn_g{j}"].ap().rearrange("(r kt p) t -> r p kt t", r=2, p=128)[rank],
            reads=(("dram", "xcn_g", j),), writes=((tag, "xg"),), label="ld_xg")
    ld_xg(0)
    for g in range(ctx.n_mix_sc):
        if g > 0:
            for ct in range(NCT):
                P.op("DVE", lambda v, o=xbcp.ap()[:, ct, 0:3], i=xbcp.ap()[:, ct, 512:515]:
                     v.tensor_copy(o, i),
                     reads=((tag, "xbcp", ct),), writes=((tag, "xbcp", ct),), label="carry")
        for mt in range(NMT):
            w = mt % 2
            dma(ctx, "SP", wib.ap()[:, w, :, :], S[pre + "wi"].ap()[mt],
                reads=(("dram", pre + "wi"),), writes=((tag, "wib", w),), label="ld_wi")
            ps = PS[w].ap()[:, 0:512]
            for k in range(NKT):
                mm_op(ctx, ps, wib.ap()[:, w, k, :], xg.ap()[:, k, :],
                      reads=((tag, "wib", w), (tag, "xg")), pwrite=("ps", w),
                      start=(k == 0), stop=(k == NKT - 1), label="mmip")
            brd = ((tag, "ipb"),) if cfg.has_bias else ()
            bap = T["bias_m"].ap()[:, mt:mt + 1] if cfg.has_bias else None
            if mt < 16:
                def em_z(a, o=sz.ap()[:, mt, :], i=ps, ba=bap):
                    return (a.activation(o, i, AF.Silu) if ba is None
                            else a.activation(o, i, AF.Silu, bias=ba))
                P.op("ACT", em_z, reads=(("ps", w),) + brd, writes=((tag, "sz", mt),),
                     label="evz")
            elif mt < 16 + NCT:
                ct = mt - 16
                def em_x(a, o=xbcp.ap()[:, ct, 3:515], i=ps, ba=bap):
                    return (a.copy(o, i) if ba is None
                            else a.activation(o, i, AF.Identity, bias=ba))
                P.op("ACT", em_x, reads=(("ps", w),) + brd, writes=((tag, "xbcp", ct),),
                     label="evx")
            else:
                if cfg.has_bias:
                    P.op("ACT", lambda a, o=dtfm.ap()[0:H2, :], i=ps[0:H2, :], ba=bap:
                         a.activation(o, i, AF.Identity, bias=ba[0:H2, :]),
                         reads=(("ps", w),) + brd, writes=((tag, "dtfm"),))
                    src, rsrc = dtfm.ap()[0:H2, :], ((tag, "dtfm"),)
                else:
                    src, rsrc = ps[0:H2, :], (("ps", w),)
                dtb = pcol.ap()[0:H2, 0:1]
                t1 = abfm.ap()[0:H2, :]
                P.op("ACT", lambda a, o=t1, i=src, bb=dtb: a.activation(o, i, AF.Abs, bias=bb),
                     reads=rsrc + ((tag, "dtb"),), writes=((tag, "ab"),))
                P.op("ACT", lambda a, o=t1, i=t1, n1=T["cconst"].ap()[0:H2, 1:2]:
                     a.activation(o, i, AF.Exp, scale=n1),
                     reads=((tag, "ab"), ("cconst",)), writes=((tag, "ab"),))
                P.op("ACT", lambda a, o=t1, i=t1: a.activation(o, i, AF.Ln, bias=1.0),
                     reads=((tag, "ab"),), writes=((tag, "ab"),))
                P.op("ACT", lambda a, o=dtfm.ap()[0:H2, :], i=src, bb=dtb:
                     a.activation(o, i, AF.Relu, bias=bb),
                     reads=rsrc + ((tag, "dtb"),), writes=((tag, "dtfm"),))
                P.op("DVE", lambda v, o=dtfm.ap()[0:H2, :], a_=dtfm.ap()[0:H2, :], b_=t1:
                     v.tensor_tensor(o, a_, b_, ALU.add),
                     reads=((tag, "dtfm"), (tag, "ab")), writes=((tag, "dtfm"),))
        P.op("DVE", lambda v, o=abfm.ap()[0:H2, :], i=dtfm.ap()[0:H2, :],
             s=pcol.ap()[0:H2, 2:3]: v.tensor_scalar_mul(o, i, s),
             reads=((tag, "dtfm"), (tag, "A")), writes=((tag, "ab"),), label="abar")
        if g + 1 < ctx.n_mix_sc:
            ld_xg(g + 1)
        if ctx.dbg and g == 0:
            tap(ctx, "xg", xg.ap(), ((tag, "xg"),))
            tap(ctx, "sz", sz.ap(), tuple((tag, "sz", k) for k in range(16)))
            tap(ctx, "xbcp", xbcp.ap(), tuple((tag, "xbcp", c) for c in range(NCT)))
            tap(ctx, "dtfm", dtfm.ap(), ((tag, "dtfm"),))
            tap(ctx, "abfm", abfm.ap(), ((tag, "ab"),))

        for cc in range(2):
            ccols = slice(cc * 256, (cc + 1) * 256)
            for ct in range(NCT):
                w = ct % 2
                if cc == 0 and True:
                    pass
                dma(ctx, "ACT", cvb.ap()[:, w, :, :], S[pre + "cv"].ap()[ct],
                    reads=(("dram", pre + "cv"),), writes=((tag, "cvb", w),), label="ld_cv")
                ps = PS[w].ap()[:, 0:256]
                for wp in range(4):
                    mm_op(ctx, ps, cvb.ap()[:, w, wp, :],
                          xbcp.ap()[:, ct, cc * 256 + wp:cc * 256 + wp + 256],
                          reads=((tag, "cvb", w), (tag, "xbcp", ct)), pwrite=("ps", w),
                          start=(wp == 0), stop=(wp == 3), label="mmcv")
                P.op("ACT", lambda a, o=xbc.ap()[:, ct, :], i=ps,
                     bb=T["cbias"].ap()[:, ct:ct + 1]: a.activation(o, i, AF.Silu, bias=bb),
                     reads=(("ps", w), (tag, "cb")), writes=((tag, "xbc", ct),), label="evcv")
            if ctx.dbg and g == 0 and cc == 0:
                tap(ctx, "xbc", xbc.ap(), tuple((tag, "xbc", c) for c in range(NCT)))
            _o1 = T["onescf"].ap()[0:H2, 0:1]
            ones_rep = bass.AP(_o1.tensor, _o1.offset, [_o1.ap[0], [0, 256]])
            P.op("DVE", lambda v, o=acfm.ap()[0:H2, :], d0=ones_rep,
                 d1=abfm.ap()[0:H2, ccols]:
                 v.tensor_tensor_scan(o, d0, d1, 0.0, ALU.mult, ALU.add),
                 reads=((tag, "ab"), ("onescf",)), writes=((tag, "ac"),), label="scan")
            P.op("ACT", lambda a, o=dsfm.ap()[0:H2, :], i=acfm.ap()[0:H2, :],
                 bb=acfm.ap()[0:H2, 255:256], n1=T["cconst"].ap()[0:H2, 1:2]:
                 a.activation(o, i, AF.Exp, scale=n1, bias=bb),
                 reads=((tag, "ac"), ("cconst",)), writes=((tag, "ds"),), label="dsfm")
            for tb in range(2):
                c0 = cc * 256 + tb * 128
                for j, (srct, srck) in enumerate(
                        ((dtfm, (tag, "dtfm")), (acfm, (tag, "ac")), (dsfm, (tag, "ds")))):
                    off = 0 if srct is dtfm else None
                    sap = (srct.ap()[0:H2, c0:c0 + 128] if srct is dtfm
                           else srct.ap()[0:H2, tb * 128:tb * 128 + 128])
                    slot = (tb * 3 + j) * 64
                    P.op("PE", lambda t, o=PS[2].ap()[0:128, slot:slot + H2], i=sap,
                         id_=T["identf"].ap()[0:H2, 0:H2]: t.transpose(o, i, id_),
                         reads=(srck, ("identf",)), writes=(("ps", 2),), label="tpdt")
            for tb in range(2):
                for j, (dstt, dstk, neg) in enumerate(
                        ((dttm, (tag, "dttm"), False), (natm, (tag, "natm"), True),
                         (dstm, (tag, "dstm"), False))):
                    slot = (tb * 3 + j) * 64
                    if neg:
                        P.op("DVE", lambda v, o=dstt.ap()[:, tb * 32:tb * 32 + H2],
                             i=PS[2].ap()[:, slot:slot + H2]: v.tensor_scalar_mul(o, i, -1.0),
                             reads=(("ps", 2),), writes=(dstk,), label="evtp")
                    else:
                        P.op("DVE", lambda v, o=dstt.ap()[:, tb * 32:tb * 32 + H2],
                             i=PS[2].ap()[:, slot:slot + H2]: v.tensor_copy(o, i),
                             reads=(("ps", 2),), writes=(dstk,), label="evtp")
            mm_op(ctx, PS[2].ap()[0:1, 384:384 + H2], T["e127"].ap(),
                  natm.ap()[:, 32:32 + H2], reads=(("e127",), (tag, "natm")),
                  pwrite=("ps", 2), start=True, stop=True, label="e127mm")
            P.op("ACT", lambda a, o=erow.ap()[0:1, 0:H2], i=PS[2].ap()[0:1, 384:384 + H2],
                 n1=T["cconst"].ap()[0:1, 1:2]:
                 a.activation(o, i, AF.Exp, scale=n1),
                 reads=(("ps", 2), ("cconst",)), writes=((tag, "erow"),))
            mm_op(ctx, PS[2].ap()[:, 448:448 + H2], T["ones1f"].ap(), erow.ap()[0:1, 0:H2],
                  reads=(("ones1f",), (tag, "erow")), pwrite=("ps", 2),
                  start=True, stop=True, label="ebcmm")
            P.op("DVE", lambda v, o=ebc.ap()[:, 0:H2], i=PS[2].ap()[:, 448:448 + H2]:
                 v.tensor_copy(o, i), reads=(("ps", 2),), writes=((tag, "ebc"),))
            if ctx.dbg and g == 0 and cc == 0:
                tap(ctx, "acfm", acfm.ap(), ((tag, "ac"),))
                tap(ctx, "dsfm", dsfm.ap(), ((tag, "ds"),))
                tap(ctx, "dttm", dttm.ap(), ((tag, "dttm"),))
                tap(ctx, "natm", natm.ap(), ((tag, "natm"),))
                tap(ctx, "dstm", dstm.ap(), ((tag, "dstm"),))
                tap(ctx, "ebc", ebc.ap(), ((tag, "ebc"),))
            bfm_base = 0 if NST == 128 else 0
            B_ct = 16
            for tb in range(2):
                tp = PS["t3" if tb == 0 else "t4"].ap()
                P.op("PE", lambda t, o=tp[:, 0:NST],
                     i=xbc.ap()[0:NST, B_ct, tb * 128:tb * 128 + 128],
                     id_=T["identb"].ap()[0:NST, 0:NST]: t.transpose(o, i, id_),
                     reads=((tag, "xbc", B_ct), ("identb",)), writes=(("ps", 3 if tb == 0 else 4),),
                     label="tpB")
                P.op("DVE", lambda v, o=btok.ap()[:, tb, 0:NST], i=tp[:, 0:NST]:
                     v.tensor_copy(o, i),
                     reads=(("ps", 3 if tb == 0 else 4),), writes=((tag, "btok"),),
                     label="evB")
            nrep = 2 if PP == 64 else 1
            for ct in range(16):
                for tb in range(2):
                    pk = "t3" if (ct * 2 + tb) % 2 == 0 else "t4"
                    tp = PS[pk].ap()
                    P.op("PE", lambda t, o=tp[:, 0:128],
                         i=xbc.ap()[:, ct, tb * 128:tb * 128 + 128],
                         id_=T["identb"].ap(): t.transpose(o, i, id_),
                         reads=((tag, "xbc", ct), ("identb",)), writes=(("ps", 3 if pk == "t3" else 4),),
                         label="tpx")
                    dcol = tb * 32 + (ct * nrep if PP == 64 else ct)
                    dr = drep(dttm, dcol, PP, nrep) if PP == 64 else \
                        bass.AP(dttm.ap().tensor, tb * 32 + ct, [ap0, [1, 1], [0, 128]])
                    P.op("DVE", lambda v, o=xdt.ap()[:, tb, ct * 128:(ct + 1) * 128],
                         a_=tp[:, 0:128], b_=dr: v.tensor_tensor(o, a_, b_, ALU.mult),
                         reads=(("ps", 3 if pk == "t3" else 4), (tag, "dttm")), writes=((tag, "xdt", tb),),
                         label="evx2")
            cfm_lo = 0
            C_ct = 16 if NST == 64 else 17
            Bs = (xbc.ap()[0:64, B_ct, :] if NST == 64 else xbc.ap()[:, B_ct, :])
            if NST == 64:
                P.op("DVE", lambda v, o=cfm0.ap()[0:64, :], i=xbc.ap()[64:128, C_ct, :]:
                     v.tensor_copy(o, i),
                     reads=((tag, "xbc", C_ct),), writes=((tag, "cfm0"),), label="cp_c")
                Cs = cfm0.ap()[0:64, :]
                CKEY = (tag, "cfm0")
            else:
                Cs = xbc.ap()[:, C_ct, :]
                CKEY = (tag, "xbc", C_ct)
            mm_op(ctx, PS[5].ap()[:, 0:256], Bs[:, 0:128], Cs,
                  reads=((tag, "xbc", B_ct), CKEY), pwrite=("ps", 5),
                  start=True, stop=True, label="mmG0")
            mm_op(ctx, PS[5].ap()[:, 256:384], Bs[:, 128:256], Cs[:, 128:256],
                  reads=((tag, "xbc", B_ct), CKEY), pwrite=("ps", 5),
                  start=True, stop=True, label="mmG1")
            P.op("ACT", lambda a, o=gsb.ap(), i=PS[5].ap()[:, 0:384]: a.copy(o, i),
                 reads=(("ps", 5),), writes=((tag, "gsb"),), label="evG")
            if ctx.dbg and g == 0 and cc == 0:
                tap(ctx, "xdt", xdt.ap(), ((tag, "xdt", 0), (tag, "xdt", 1)))
                tap(ctx, "gsb", gsb.ap(), ((tag, "gsb"),))
                tap(ctx, "btok", btok.ap(), ((tag, "btok"),))
            def prep(h):
                hb = h % 2
                hg = h % 3
                qb = (h // 4) % 2
                pbA, pbB = ((2, 7) if qb == 0 else (3, 4))
                if h % 4 == 0:
                    dma(ctx, "ACT", acr.ap().rearrange("o (hh t) -> o hh t", hh=4),
                        acfm.ap()[h:h + 4, 0:256],
                        reads=((tag, "ac"),), writes=((tag, "acr", qb),), label="acr")
                    mm_op(ctx, PS[pbA].ap()[:, 0:512], T["ones1f"].ap(),
                          acr.ap()[0:1, 0:512],
                          reads=(("ones1f",), (tag, "acr", qb)), pwrite=("ps", pbA),
                          start=True, stop=True, label="bAc")
                    mm_op(ctx, PS[pbB].ap()[:, 0:512], T["ones1f"].ap(),
                          acr.ap()[0:1, 512:1024],
                          reads=(("ones1f",), (tag, "acr", qb)), pwrite=("ps", pbB),
                          start=True, stop=True, label="bAc")
                pb = pbA if h % 4 < 2 else pbB
                c0 = (h % 2) * 256
                bAc = PS[pb].ap()[:, c0:c0 + 256]
                P.op("DVE", lambda v, o=seg.ap()[:, hb, 0:256], i=bAc,
                     s=natm.ap()[:, h:h + 1], m_=T["maskT"].ap()[:, 0:256]:
                     v.scalar_tensor_tensor(o, i, s, m_, ALU.add, ALU.add),
                     reads=(("ps", pb), (tag, "natm"), ("maskT",)),
                     writes=((tag, "seg", hb),), label="seg0")
                P.op("DVE", lambda v, o=seg.ap()[:, hb, 256:384], i=bAc[:, 128:256],
                     s=natm.ap()[:, 32 + h:33 + h], m_=T["maskT"].ap()[:, 0:128]:
                     v.scalar_tensor_tensor(o, i, s, m_, ALU.add, ALU.add),
                     reads=(("ps", pb), (tag, "natm"), ("maskT",)),
                     writes=((tag, "seg", hb),), label="seg1")
                P.op("ACT", lambda a, o=Lb.ap()[:, hb, :], i=seg.ap()[:, hb, :]:
                     a.activation(o, i, AF.Exp),
                     reads=((tag, "seg", hb),), writes=((tag, "L", hb),), label="Lexp")
                P.op("DVE", lambda v, o=glb.ap()[:, hg, :], a_=gsb.ap(), b_=Lb.ap()[:, hb, :]:
                     v.tensor_tensor(o, a_, b_, ALU.mult),
                     reads=((tag, "gsb"), (tag, "L", hb)), writes=((tag, "gl", hg),),
                     label="GL")
                erows = slice(0, NST)
                P.op("ACT", lambda a, o=expb.ap()[erows, hb, :], i=bAc[erows, :]:
                     a.activation(o, i, AF.Exp),
                     reads=(("ps", pb),), writes=((tag, "expb", hb),), label="expbAc")
                P.op("POOL", lambda v, o=cea.ap()[0:NST, hg, :], a_=Cs, b_=expb.ap()[erows, hb, :]:
                     v.tensor_tensor(o, a_, b_, ALU.mult),
                     reads=(CKEY, (tag, "expb", hb)), writes=((tag, "cea", hg),),
                     label="cea")
                if ctx.dbg and g == 0 and cc == 0 and h == 0:
                    tap(ctx, "seg", seg.ap(), ((tag, "seg", 0),))
                    tap(ctx, "Lb", Lb.ap(), ((tag, "L", 0),))
                    tap(ctx, "glb", glb.ap()[:, 0, :], ((tag, "gl", 0),))
                    tap(ctx, "cea", cea.ap()[:, 0, :], ((tag, "cea", 0),))

            def comp(h):
                hg = h % 3
                if PP == 64:
                    pY = 5 + ((h // 2) % 2)
                    r0 = (h % 2) * 64
                else:
                    pY = 5 + (h % 2)
                    r0 = 0
                Yp = PS[pY].ap()
                mm_op(ctx, Yp[r0:r0 + PP, 0:256], xdt.ap()[:, 0, h * PP:(h + 1) * PP],
                      glb.ap()[:, hg, 0:256],
                      reads=(((tag, "xdt", 0)), (tag, "gl", hg)), pwrite=("ps", pY),
                      start=True, stop=False, label="mmY0")
                mm_op(ctx, Yp[r0:r0 + PP, 128:256], xdt.ap()[:, 1, h * PP:(h + 1) * PP],
                      glb.ap()[:, hg, 256:384],
                      reads=(((tag, "xdt", 1)), (tag, "gl", hg)), pwrite=("ps", pY),
                      start=False, stop=False, label="mmY1")
                mm_op(ctx, Yp[r0:r0 + PP, 0:256], Sbf.ap()[0:NST, h * PP:(h + 1) * PP],
                      cea.ap()[0:NST, hg, :],
                      reads=(((tag, "Sb")), (tag, "cea", hg)), pwrite=("ps", pY),
                      start=False, stop=True, label="mmY2")
                if (PP == 64 and h % 2 == 1) or PP == 128:
                    ct = h // 2 if PP == 64 else h
                    P.op("DVE", lambda v, o=ysb.ap()[:, ct, :], i=xbc.ap()[:, ct, :],
                         s=T["dex"].ap()[:, ct:ct + 1], Y=Yp[:, 0:256]:
                         v.scalar_tensor_tensor(o, i, s, Y, ALU.mult, ALU.add),
                         reads=((tag, "xbc", ct), (tag, "dex"), ("ps", pY)),
                         writes=((tag, "y", ct),), label="evY")

            for h in range(H2 + 2):
                if h < H2:
                    prep(h)
                if h >= 2:
                    comp(h - 2)
            nh_rep = H2 // 2 if PP == 64 else H2
            for tb in range(2):
                dr = bass.AP(dstm.ap().tensor, tb * 32, [ap0, [1, H2], [0, PP]])
                P.op("DVE", lambda v, o=xdt.ap()[:, tb, :], a_=xdt.ap()[:, tb, :], b_=dr:
                     v.tensor_tensor(o, a_, b_, ALU.mult),
                     reads=((tag, "xdt", tb), (tag, "dstm")), writes=((tag, "xdt", tb),),
                     label="xdd")
            for qq in range(4):
                pM = 5 + (qq % 2)
                qs = slice(qq * 512, (qq + 1) * 512)
                for tb in range(2):
                    mm_op(ctx, PS[pM].ap()[0:NST, 0:512], btok.ap()[:, tb, 0:NST],
                          xdt.ap()[:, tb, qs],
                          reads=((tag, "btok"), (tag, "xdt", tb)), pwrite=("ps", pM),
                          start=(tb == 0), stop=(tb == 1), label="mmM1")
                hlo = qq * (512 // PP)
                _e0 = ebc.ap()[0:NST, 0:1]
                er = bass.AP(_e0.tensor, _e0.offset + hlo, [_e0.ap[0], [1, 512 // PP], [0, PP]])
                P.op("DVE", lambda v, o=Sst.ap()[0:NST, qs], a_=Sst.ap()[0:NST, qs], b_=er:
                     v.tensor_tensor(o, a_, b_, ALU.mult),
                     reads=((tag, "S"), (tag, "ebc")), writes=((tag, "S"),), label="Sdec")
                P.op("DVE", lambda v, o=Sst.ap()[0:NST, qs], a_=Sst.ap()[0:NST, qs],
                     b_=PS[pM].ap()[0:NST, 0:512]: v.tensor_tensor(o, a_, b_, ALU.add),
                     reads=((tag, "S"), ("ps", pM)), writes=((tag, "S"),), label="Sadd")
            P.op("DVE", lambda v, o=Sbf.ap()[0:NST, :], i=Sst.ap()[0:NST, :]:
                 v.tensor_copy(o, i), reads=((tag, "S"),), writes=((tag, "Sb"),),
                 label="Sbf")
            if ctx.dbg and g == 0:
                tap(ctx, "S%d" % cc, Sst.ap(), ((tag, "S"),))
            for kt in range(16):
                P.op("DVE", lambda v, o=ysb.ap()[:, kt, :], a_=ysb.ap()[:, kt, :],
                     b_=sz.ap()[:, kt, ccols]: v.tensor_tensor(o, a_, b_, ALU.mult),
                     reads=((tag, "y", kt), (tag, "sz", kt)), writes=((tag, "y", kt),),
                     label="gate")
            sqb = ln_t[0]
            for kt in range(16):
                b2 = kt % 2
                P.op("ACT", lambda a, o=sqb.ap()[:, 0, b2 * 256:b2 * 256 + 256],
                     i=ysb.ap()[:, kt, :]: a.activation(o, i, AF.Square),
                     reads=((tag, "y", kt),), writes=((tag, "ysq", b2),), label="ysq")
                mm_op(ctx, PS[7].ap()[0:1, 256:512], T["onescb"].ap(),
                      sqb.ap()[:, 0, b2 * 256:b2 * 256 + 256],
                      reads=(("onescb",), (tag, "ysq", b2)), pwrite=("ps", 7),
                      start=(kt == 0), stop=(kt == 15), label="mmst")
            tcol = g * SC + cc * 256
            P.op("ACT", lambda a, o=strow.ap()[0:1, 0:256], i=PS[7].ap()[0:1, 256:512]:
                 a.copy(o, i), reads=(("ps", 7),), writes=((tag, "strow"),), label="evst")
            dma(ctx, "SP", S["st_in"].ap()[0:1, tcol:tcol + 256],
                strow.ap()[0:1, 0:256],
                reads=((tag, "strow"),), writes=(("dram", "st_in"),), label="st_stats")
            qq, lc = tcol // 1024, tcol % 1024
            dma(ctx, "SP",
                S[f"y_in{qq}"].ap().rearrange("(kt p) t -> p kt t", p=128)[:, :, lc:lc + 256],
                ysb.ap(), reads=tuple(((tag, "y", kt)) for kt in range(16)),
                writes=(("dram", "y_in", qq),), label="st_y")
            if tcol + 256 == ctx.n_mix_sc * SC:
                P.op("POOL", lambda g_: g_.collective_compute(
                    "AllGather", ALU.bypass, replica_groups=RG,
                    ins=[S["st_in"].ap()], outs=[S["st_g"].ap()]),
                    reads=(("dram", "st_in"),), writes=(("dram", "st_g"),),
                    kind="cc", label="ag_st")
            if lc + 256 == 1024:
                P.op("POOL", lambda g_, j=qq: g_.collective_compute(
                    "AllGather", ALU.bypass, replica_groups=RG,
                    ins=[S[f"y_in{j}"].ap()], outs=[S["ygt"].ap()[j]]),
                    reads=(("dram", "y_in", qq),), writes=(("dram", "ygt", qq),),
                    kind="cc", label="ag_y")


    ctx.sb_off = xg_base
    yg = _sb(ctx, f"yg_{tag}", [128, 32, 512], BF16, at=xg_base)
    wob = _sb(ctx, f"wob_{tag}", [128, 2, 32, 128], BF16)
    rr = _sb(ctx, f"rr_{tag}", [1, 512], F32)
    rbc2 = _sb(ctx, f"rbc2_{tag}", [128, 512], BF16)
    st2 = _sb(ctx, f"st2_{tag}", [1, 2, 512], F32)
    ctx.sb_off = max(ctx.sb_off, mix_end)


    def em_qsetup(s):
        if not hasattr(ctx, "qreg"):
            pid = s.partition_id()
            reg = s.to_reg(pid)
            s.reg_mod(reg, reg, 2)
            ctx.qreg = reg
            ctx.qyreg = s.alloc_register("qy")
            ctx.qtreg = s.alloc_register("qt")
            s.reg_alu(ctx.qyreg, ctx.qreg, 4194304 * 2, ALU.mult)
            s.reg_alu(ctx.qtreg, ctx.qreg, TOK, ALU.mult)
        return s.nop()
    ctx.mixregs = []
    for m0 in range(2):
        dma(ctx, "SP", wob.ap()[:, m0, :, :], S[pre + "wo"].ap()[m0],
            reads=(("dram", pre + "wo"),), writes=((tag, "wob", m0),), label="ld_wo")
    P.op("SP", em_qsetup, reads=(), writes=((tag, "qreg"),), label="qsetup")

    for g in range(ctx.n_tok_sc):
        cols = slice(g * SC, (g + 1) * SC)

        def em_gofs(s, gg=g):
            r2 = s.alloc_register(f"gofs{gg}{tag}")
            s.reg_alu(r2, ctx.qtreg, gg * SC, ALU.add)
            ctx.gofs = RuntimeValue(r2)
            r5 = s.alloc_register(f"yh0_{gg}{tag}")
            r6 = s.alloc_register(f"yh1_{gg}{tag}")
            s.reg_alu(r5, ctx.qyreg, (gg // 2) * 4194304 + (gg % 2) * 512, ALU.add)
            s.reg_alu(r6, r5, 2097152, ALU.add)
            ctx.yhreg = (r5, r6)
            ctx.mixregs.extend([r2, r5, r6])
            return s.nop()
        P.op("SP", em_gofs, reads=((tag, "qreg"),), writes=((tag, "gofs"),), label="gofs")

        def em_ldst(s):
            src = bass.AP(S["st_g"].ap().tensor, ctx.gofs, [[4096, 2], [1, 512]])
            return s.dma_start(out=st2.ap(), in_=src)
        ctx.P.op("SP", em_ldst, reads=(("dram", "st_g"), (tag, "gofs")),
                 writes=((tag, "st2"),), kind="dma", label="ld_st")
        P.op("DVE", lambda v, o=rr.ap(), a_=st2.ap()[0:1, 0, :], b_=st2.ap()[0:1, 1, :]:
             v.tensor_tensor(o, a_, b_, ALU.add),
             reads=((tag, "st2"),), writes=((tag, "rr"),))
        P.op("DVE", lambda v, o=rr.ap(), i=rr.ap(): v.tensor_scalar_mul(o, i, 1.0 / DIN),
             reads=((tag, "rr"),), writes=((tag, "rr"),))
        P.op("ACT", lambda a, o=rr.ap(), i=rr.ap(), e=T["cconst"].ap()[0:1, 0:1]:
             a.activation(o, i, AF.Sqrt, bias=e),
             reads=((tag, "rr"), ("cconst",)), writes=((tag, "rr"),))
        P.op("DVE", lambda v, o=rr.ap(), i=rr.ap(): v.reciprocal(o, i),
             reads=((tag, "rr"),), writes=((tag, "rr"),))
        mm_op(ctx, PS[2].ap()[:, 0:512], T["ones1f"].ap(), rr.ap(),
              reads=(("ones1f",), (tag, "rr")), pwrite=("ps", 2), start=True, stop=True)
        P.op("ACT", lambda a, o=rbc2.ap(), i=PS[2].ap()[:, 0:512]: a.copy(o, i),
             reads=(("ps", 2),), writes=((tag, "rbc2"),))
        yq_reads = (("dram", "ygt", g // 2), ("dram", "ygt", 2 + g // 2), (tag, "gofs"))
        for half in range(2):
            ks = range(half * 16, (half + 1) * 16)

            def em_ldyg(s, hh=half):
                src = bass.AP(S["ygt"].ap().tensor, RuntimeValue(ctx.yhreg[hh]),
                              [[1024, 128], [131072, 16], [1, 512]])
                return s.dma_start(out=yg.ap()[:, hh * 16:(hh + 1) * 16, :], in_=src)
            ctx.P.op("SP", em_ldyg, reads=yq_reads,
                     writes=tuple((tag, "yg", k) for k in ks), kind="dma", label="ld_yg")
        for kt in range(32):
            P.op("DVE", lambda v, o=yg.ap()[:, kt, :], i=yg.ap()[:, kt, :],
                 s=T["nw"].ap()[:, kt:kt + 1]: v.tensor_scalar_mul(o, i, s),
                 reads=((tag, "yg", kt), (tag, "nw")), writes=((tag, "yg", kt),),
                 label="ynw")
            P.op("DVE", lambda v, o=yg.ap()[:, kt, :], a_=yg.ap()[:, kt, :], b_=rbc2.ap():
                 v.tensor_tensor(o, a_, b_, ALU.mult),
                 reads=((tag, "yg", kt), (tag, "rbc2")), writes=((tag, "yg", kt),),
                 label="yr")
        if ctx.dbg and g == 0:
            tap(ctx, "yg", yg.ap(), tuple((tag, "yg", k) for k in range(32)))
            tap(ctx, "rbc2", rbc2.ap(), ((tag, "rbc2"),))
        for m in range(16):
            w = m % 2
            if g > 0 and m < 2:
                dma(ctx, "SP", wob.ap()[:, w, :, :], S[pre + "wo"].ap()[m],
                    reads=(("dram", pre + "wo"),), writes=((tag, "wob", w),), label="ld_wo")
            ps = PS[w].ap()[:, 0:512]
            for k in range(32):
                mm_op(ctx, ps, wob.ap()[:, w, k, :], yg.ap()[:, k, :],
                      reads=((tag, "wob", w), (tag, "yg", k)), pwrite=("ps", w),
                      start=(k == 0), stop=(k == 31), label="mmop")
            if m + 2 < 16:
                dma(ctx, "SP", wob.ap()[:, (m + 2) % 2, :, :], S[pre + "wo"].ap()[m + 2],
                    reads=(("dram", pre + "wo"),), writes=((tag, "wob", (m + 2) % 2),),
                    label="ld_wo")
            if cfg.has_bias:
                P.op("DVE", lambda v, o=T["R"].ap()[:, m, cols], i=ps,
                     s=T["bias_m"].ap()[:, 36 + m:37 + m], b_=T["R"].ap()[:, m, cols]:
                     v.scalar_tensor_tensor(o, i, s, b_, ALU.add, ALU.add),
                     reads=(("ps", w), (tag, "opb"), ("R", m, g)), writes=(("R", m, g),),
                     label="resop")
            else:
                P.op("DVE", lambda v, o=T["R"].ap()[:, m, cols], a_=ps,
                     b_=T["R"].ap()[:, m, cols]: v.tensor_tensor(o, a_, b_, ALU.add),
                     reads=(("ps", w), ("R", m, g)), writes=(("R", m, g),), label="resop")
    def em_freeregs(s):
        for r in ctx.mixregs:
            s.free_register(r)
        ctx.mixregs = []
        return s.nop()
    P.op("SP", em_freeregs, reads=(), writes=(), label="freeregs")



_CACHED = {}


def kernel(**inputs):
    in_maps = host_prepare(inputs)
    key = "full"
    if key not in _CACHED:
        _CACHED[key] = build_program()
    nc, ctx = _CACHED[key]
    res = run_bass_kernel_spmd(nc, in_maps, core_ids=list(range(8)))
    out = np.zeros((4, SEQ, DM), np.float32)
    for c in range(8):
        b, q = c // 2, c % 2
        out[b, q * TOK:(q + 1) * TOK, :] = res.results[c]["yout"].T
    return out
